# version 04
# baseline (speedup 1.0000x reference)
import contextlib
import numpy as np
import concourse.bass as bass
import concourse.mybir as mybir
from concourse.bass_utils import run_bass_kernel_spmd

try:
    import ml_dtypes
    _BF16_NP = ml_dtypes.bfloat16
except ImportError:
    _BF16_NP = None

F32 = mybir.dt.float32
BF16 = mybir.dt.bfloat16
I32 = mybir.dt.int32
I8 = mybir.dt.int8
ALU = mybir.AluOpType
AX = mybir.AxisListType
ACTF = mybir.ActivationFunctionType

S = 2048
DM = 2048
HID = 8192
NTC = S // 128
KC = DM // 128
OC = HID // 128
TBLK = 1024
NSW = S // TBLK
SWTC = TBLK // 128
OSL = 256
NOSL = HID // OSL
HD = 64
EPS = 1e-5
NEG = -1.0e30
MAGIC = 12582912.0
CLIPC = float(np.nextafter(np.float32(127.5), np.float32(0)))

ENGS = ("sync", "gpsimd", "vector", "scalar", "tensor")
PHASES = ("wprep", "fa", "attn", "fb")
DBG_SCRATCH = False


class Sem:
    def __init__(self, h):
        self.h = h
        self.n = 0

    def inc(self, k=1):
        self.n += k
        return self.n


class Prog:
    def __init__(self, nc):
        self.nc = nc
        self.q = {e: [] for e in ENGS}
        self.dirty = {e: set() for e in ENGS}

    def wait(self, eng, sem, val):
        if val > 0:
            self.q[eng].append(lambda E, s=sem, v=val: E.wait_ge(s.h, v))

    def op(self, eng, build, reads=(), writes=(), inc=None, inc_n=1):
        names = set(reads) | set(writes)
        if eng in ("vector", "scalar", "gpsimd") and (names & self.dirty[eng]):
            self.q[eng].append(lambda E: E.drain())
            self.dirty[eng] = set()
        self.dirty[eng] |= set(writes)
        if inc is not None:
            def fn(E, b=build, s=inc, n=inc_n):
                b(E).then_inc(s.h, n)
            self.q[eng].append(fn)
            return inc.inc(inc_n)
        self.q[eng].append(build)
        return None

    def dma(self, eng, out_ap, in_ap, sem, **kw):
        def fn(E, o=out_ap, i=in_ap, s=sem, kw=kw):
            E.dma_start(o, i, **kw).then_inc(s.h, 16)
        self.q[eng].append(fn)
        return sem.inc(16)

    def flush_into_block(self, block):
        q = self.q

        @block.sync
        def _(e):
            for fn in q["sync"]:
                fn(e)

        @block.gpsimd
        def _(e):
            for fn in q["gpsimd"]:
                fn(e)

        @block.vector
        def _(e):
            for fn in q["vector"]:
                fn(e)

        @block.scalar
        def _(e):
            for fn in q["scalar"]:
                fn(e)

        @block.tensor
        def _(e):
            for fn in q["tensor"]:
                fn(e)


def nm(t):
    return t.ap().name


def _host_consts():
    half = HD // 2
    inv = (1.0 / np.power(np.float32(10000.0),
                          np.arange(half, dtype=np.float32) / np.float32(half))
           ).astype(np.float32)
    pos = np.arange(S, dtype=np.float32)
    fr = np.outer(pos, inv).astype(np.float32)
    cos = np.cos(fr).astype(np.float32)
    sin = np.sin(fr).astype(np.float32)
    j_of_r = (np.arange(128) % HD) // 2
    C = cos[:, j_of_r].T.astype(np.float32)
    Sg = sin[:, j_of_r].T.astype(np.float32)
    sign = np.where(np.arange(128) % 2 == 0, -1.0, 1.0).astype(np.float32)
    Stab = (Sg * sign[:, None]).astype(np.float32)
    swap = np.arange(128)
    swap = swap + np.where(swap % 2 == 0, 1, -1)
    perm = np.zeros((128, 128), dtype=np.float32)
    perm[swap, np.arange(128)] = 1.0
    ident = np.eye(128, dtype=np.float32)
    ii = np.arange(128)
    mask = np.where(ii[None, :] <= ii[:, None], 0.0, NEG).astype(np.float32)
    if _BF16_NP is not None:
        perm = perm.astype(_BF16_NP)
        ident = ident.astype(_BF16_NP)
    return dict(cq=(C * np.float32(0.125)).astype(np.float32),
                sq=(Stab * np.float32(0.125)).astype(np.float32),
                ck=C, sk=Stab, perm=perm, ident=ident, mask=mask)


def _head_perm():
    order = []
    for gp in range(4):
        for i in range(4):
            for h in (8 * gp + i, 8 * gp + 4 + i):
                order.extend(range(h * HD, (h + 1) * HD))
    return np.asarray(order, dtype=np.int64)



def emit_mx_quant_core(p, sem, src_t, src_sl, nblk, bufs):
    am, rsc, ssc, y, r = (bufs["am"], bufs["rsc"], bufs["ssc"], bufs["y"],
                          bufs["r"])
    n = nblk * 32
    src3 = src_t.ap()[:, src_sl].rearrange("p (nb k) -> p nb k", k=32)
    p.op("vector", lambda E: E.tensor_reduce(
        am[:, 0:nblk], src3, AX.X, ALU.max, apply_absolute_value=True),
        reads=(nm(src_t),), writes=(nm(am),))
    p.op("vector", lambda E: E.tensor_single_scalar(
        rsc[:, 0:nblk], am.ap()[:, 0:nblk].bitcast(I32), 23,
        ALU.logical_shift_right), reads=(nm(am),), writes=(nm(rsc),))
    p.op("vector", lambda E: E.tensor_scalar_max(
        rsc[:, 0:nblk], rsc[:, 0:nblk], 64),
        reads=(nm(rsc),), writes=(nm(rsc),))
    p.op("vector", lambda E: E.tensor_scalar(
        ssc[:, 0:nblk], rsc[:, 0:nblk], -6, 8388608, ALU.add, ALU.mult),
        reads=(nm(rsc),), writes=(nm(ssc),))
    p.op("vector", lambda E: E.tensor_scalar(
        rsc[:, 0:nblk], rsc[:, 0:nblk], -260, -8388608, ALU.add, ALU.mult),
        reads=(nm(rsc),), writes=(nm(rsc),))
    rb = rsc.ap()[:, 0:nblk].bitcast(F32).unsqueeze(2).broadcast_to(
        [128, nblk, 32])
    y3 = y.ap()[:, 0:n].rearrange("p (nb k) -> p nb k", k=32)
    p.op("vector", lambda E: E.tensor_tensor(y3, src3, rb, ALU.mult),
         reads=(nm(src_t), nm(rsc)), writes=(nm(y),))
    return p.op("vector", lambda E: E.tensor_scalar(
        r[:, 0:n], y[:, 0:n], MAGIC, MAGIC, ALU.add, ALU.subtract),
        reads=(nm(y),), writes=(nm(r),), inc=sem["dv"])


def emit_mx_q8(p, sem, bufs, r_sl, q8_t, q8_sl, inc=None):
    r = bufs["r"]
    return p.op("vector", lambda E: E.tensor_scalar(
        q8_t.ap()[:, q8_sl], r[:, r_sl], 127.0, None, ALU.min),
        reads=(nm(r),), writes=(nm(q8_t),), inc=inc)


def emit_mx_expand(p, sem, q8_t, q8_sl, ssc_t, ssc_sl, nblk, out_t, out_sl,
                   inc=None):
    q83 = q8_t.ap()[:, q8_sl].rearrange("p (nb k) -> p nb k", k=32)
    sc3 = ssc_t.ap()[:, ssc_sl].bitcast(F32).unsqueeze(2).broadcast_to(
        [128, nblk, 32])
    o3 = out_t.ap()[:, out_sl].rearrange("p (nb k) -> p nb k", k=32)
    return p.op("vector", lambda E: E.scalar_tensor_tensor(
        o3, q83, 0.0, sc3, ALU.add, ALU.mult),
        reads=(nm(q8_t), nm(ssc_t)), writes=(nm(out_t),), inc=inc)


def emit_ln_stats(p, sem, src_t, stat6, stat2):
    for j in range(4):
        p.op("vector", lambda E, j=j, src_t=src_t: E.bn_stats(
            stat6[:, 6 * j:6 * (j + 1)], src_t[:, 512 * j:512 * (j + 1)]),
            reads=(nm(src_t),), writes=("stat6",))
    return p.op("vector", lambda E: E.bn_aggr(stat2[:, :], stat6[:, :]),
                reads=("stat6",), writes=("stat2",), inc=sem["dv"])


def emit_rsig(p, sem, stat2, rsg, epsc, nhalf, gate_dv=0):
    if gate_dv:
        p.wait("scalar", sem["dv"], gate_dv)
    p.op("scalar", lambda E: E.activation(
        rsg[:, :], stat2[:, 1:2], ACTF.Ln, bias=epsc[:, 0:1], scale=1.0),
        reads=("stat2",), writes=("rsg",))
    return p.op("scalar", lambda E: E.activation(
        rsg[:, :], rsg[:, :], ACTF.Exp, bias=0.0, scale=nhalf[:, 0:1]),
        reads=("rsg",), writes=("rsg",), inc=sem["ac"])



def build_nc(phases=None):
    phases = PHASES if phases is None else phases
    scratch_kind = "ExternalOutput" if DBG_SCRATCH else "Internal"
    nc = bass.Bass("TRN2", target_bir_lowering=False)

    xin = nc.dram_tensor("xin", [S, DM], F32, kind="ExternalInput")
    wq = nc.dram_tensor("wq", [DM, DM], F32, kind="ExternalInput")
    wk = nc.dram_tensor("wk", [DM, 512], F32, kind="ExternalInput")
    wv = nc.dram_tensor("wv", [DM, 512], F32, kind="ExternalInput")
    wo = nc.dram_tensor("wo", [DM, DM], F32, kind="ExternalInput")
    w1 = nc.dram_tensor("w1", [HID, DM], F32, kind="ExternalInput")
    w2 = nc.dram_tensor("w2", [DM, HID], F32, kind="ExternalInput")
    lnww = nc.dram_tensor("lnww", [128, DM], F32, kind="ExternalInput")
    lnbb = nc.dram_tensor("lnbb", [128, DM], F32, kind="ExternalInput")
    b1r = nc.dram_tensor("b1r", [1, HID], F32, kind="ExternalInput")
    b2r = nc.dram_tensor("b2r", [1, DM], F32, kind="ExternalInput")
    cqd = nc.dram_tensor("cqd", [128, S], F32, kind="ExternalInput")
    sqd = nc.dram_tensor("sqd", [128, S], F32, kind="ExternalInput")
    ckd = nc.dram_tensor("ckd", [128, S], F32, kind="ExternalInput")
    skd = nc.dram_tensor("skd", [128, S], F32, kind="ExternalInput")
    permd = nc.dram_tensor("permd", [128, 128], BF16, kind="ExternalInput")
    identd = nc.dram_tensor("identd", [128, 128], BF16, kind="ExternalInput")
    maskd = nc.dram_tensor("maskd", [128, 128], F32, kind="ExternalInput")
    yout = nc.dram_tensor("yout", [S, DM], F32, kind="ExternalOutput")

    xscr = nc.dram_tensor("xscr", [S, DM], F32, kind=scratch_kind)
    w1t_s = nc.dram_tensor("w1t_s", [KC, 128, HID], BF16, kind=scratch_kind)
    w2t_s = nc.dram_tensor("w2t_s", [OC, 128, DM], BF16, kind=scratch_kind)
    q_s = nc.dram_tensor("q_s", [16, 128, S], BF16, kind=scratch_kind)
    k_s = nc.dram_tensor("k_s", [4, 128, S], BF16, kind=scratch_kind)
    v_s = nc.dram_tensor("v_s", [4, NTC, 128, 128], BF16, kind=scratch_kind)
    ao_s = nc.dram_tensor("ao_s", [16, 128, S], BF16, kind=scratch_kind)

    st = contextlib.ExitStack()
    sem = {k: Sem(st.enter_context(nc.semaphore("sem_" + k)))
           for k in ("dv", "ac", "pe")}
    dnames = ("blks", "blkp", "xt0", "xt1", "w10", "w11", "b10", "b11",
              "w20", "w21", "wc0", "wc1", "vwl", "qs0", "qs1", "kst", "vst",
              "spl", "vsp", "aosp", "ao0", "ao1", "xb0", "xb1", "wld0",
              "wld1", "wst0", "wst1")
    dsem = {k: Sem(st.enter_context(nc.semaphore("dma_" + k)))
            for k in dnames}
    dq = {"blkp": "gpsimd", "spl": "gpsimd", "vsp": "gpsimd",
          "aosp": "gpsimd", "wst0": "gpsimd", "wst1": "gpsimd",
          "vwl": "gpsimd", "wc0": "gpsimd", "wc1": "gpsimd",
          "b10": "gpsimd", "b11": "gpsimd"}

    def sbt(stack, name, shape, dtype):
        return stack.enter_context(nc.sbuf_tensor(name, shape, dtype))

    def pst(stack, name, shape, dtype):
        return stack.enter_context(nc.psum_tensor(name, shape, dtype))

    ident_b = sbt(st, "ident_b", [128, 128], BF16)
    perm_b = sbt(st, "perm_b", [128, 128], BF16)
    mask_b = sbt(st, "mask_b", [128, 128], F32)
    ones1 = sbt(st, "ones1", [1, 128], BF16)
    stat6 = sbt(st, "stat6", [128, 24], F32)
    stat2 = sbt(st, "stat2", [128, 2], F32)
    rsg = sbt(st, "rsg", [128, 1], F32)
    musig = sbt(st, "musig", [128, 2 * NTC], F32)
    zsum = sbt(st, "zsum", [128, 1], F32)
    rz = sbt(st, "rz", [128, 1], F32)
    negmax = sbt(st, "negmax", [128, 1], F32)
    epsc = sbt(st, "epsc", [128, 1], F32)
    nhalf = sbt(st, "nhalf", [128, 1], F32)

    io = dict(xin=xin, wq=wq, wk=wk, wv=wv, wo=wo, w1=w1, w2=w2,
              lnww=lnww, lnbb=lnbb, b1r=b1r, b2r=b2r, cqd=cqd, sqd=sqd,
              ckd=ckd, skd=skd, permd=permd, identd=identd, maskd=maskd,
              yout=yout, xscr=xscr, w1t_s=w1t_s, w2t_s=w2t_s, q_s=q_s,
              k_s=k_s, v_s=v_s, ao_s=ao_s)
    pers = dict(ident_b=ident_b, perm_b=perm_b, mask_b=mask_b, ones1=ones1,
                stat6=stat6, stat2=stat2, rsg=rsg, musig=musig, zsum=zsum,
                rz=rz, negmax=negmax, epsc=epsc, nhalf=nhalf)

    with st:
        p = Prog(nc)
        for dst, src in ((ident_b, identd), (perm_b, permd), (mask_b, maskd)):
            p.dma("sync", dst[:, :], src.ap(), dsem["blks"])
        p.wait("vector", dsem["blks"], dsem["blks"].n)
        p.op("vector", lambda E: E.memset(ones1[:, :], 1.0), writes=("ones1",))
        p.op("vector", lambda E: E.memset(epsc[:, :], EPS), writes=("epsc",))
        p.op("vector", lambda E: E.memset(nhalf[:, :], -0.5),
             writes=("nhalf",), inc=sem["dv"])
        p.wait("tensor", sem["dv"], sem["dv"].n)
        p.wait("scalar", sem["dv"], sem["dv"].n)
        with nc.Block() as blk:
            p.flush_into_block(blk)

        if "wprep" in phases:
            _phase_wprep(nc, sem, dsem, sbt, pst, io, pers)
        if "fa" in phases:
            _phase_ffn(nc, sem, dsem, sbt, pst, io, pers, xin, xscr, "fa")
        else:
            p = Prog(nc)
            for tc in range(NTC):
                p.dma("sync", xscr.ap()[tc * 128:(tc + 1) * 128, :],
                      xin.ap()[tc * 128:(tc + 1) * 128, :], dsem["blks"])
            p.wait("sync", dsem["blks"], dsem["blks"].n)
            with nc.Block() as blk:
                p.flush_into_block(blk)
        if "attn" in phases:
            _phase_attn(nc, sem, dsem, sbt, pst, io, pers)
        if "fb" in phases:
            _phase_ffn(nc, sem, dsem, sbt, pst, io, pers, xscr, yout, "fb")
        else:
            p = Prog(nc)
            for tc in range(NTC):
                p.dma("sync", yout.ap()[tc * 128:(tc + 1) * 128, :],
                      xscr.ap()[tc * 128:(tc + 1) * 128, :], dsem["blks"])
            p.wait("sync", dsem["blks"], dsem["blks"].n)
            with nc.Block() as blk:
                p.flush_into_block(blk)

        p = Prog(nc)
        for k in dnames:
            p.wait(dq.get(k, "sync"), dsem[k], dsem[k].n)
        with nc.Block() as blk:
            p.flush_into_block(blk)

    nc.finalize()
    return nc



def _phase_wprep(nc, sem, dsem, sbt, pst, io, pers):
    st = contextlib.ExitStack()
    wld = [sbt(st, "wp_ld%d" % i, [128, HID], F32) for i in range(2)]
    qbufs = dict(am=sbt(st, "wp_am", [128, 256], F32),
                 rsc=sbt(st, "wp_rsc", [128, 256], I32),
                 ssc=sbt(st, "wp_ssc", [128, 256], I32),
                 y=sbt(st, "wp_y", [128, HID], F32),
                 r=sbt(st, "wp_r", [128, HID], F32))
    q8 = sbt(st, "wp_q8", [128, HID], I8)
    qnat = sbt(st, "wp_qn", [128, HID], BF16)
    wfm = [sbt(st, "wp_fm%d" % i, [128, HID], BF16) for i in range(2)]
    tp = [pst(st, "wp_tp%d" % i, [128, 1024], BF16) for i in range(2)]
    ident_b = pers["ident_b"]

    p = Prog(nc)
    q_done = {}
    stg = {}
    specs = []
    for rc in range(HID // 128):
        specs.append((io["w1"], io["w1t_s"], rc, DM, 4, HID))
    for rc in range(DM // 128):
        specs.append((io["w2"], io["w2t_s"], rc, HID, 8, DM))

    with st:
        for g, (wt, scr, rc, ncols, nsplit, scr_cols) in enumerate(specs):
            buf = wld[g % 2]
            fmb = wfm[g % 2]
            sld = dsem["wld%d" % (g % 2)]
            sst = dsem["wst%d" % (g % 2)]
            if g >= 2:
                p.wait("sync", sem["dv"], q_done[g - 2])
            c_ld = p.dma("sync", buf[:, 0:ncols],
                         wt.ap()[rc * 128:(rc + 1) * 128, :], sld)
            p.wait("vector", sld, c_ld)
            nblk = ncols // 32
            q_done[g] = emit_mx_quant_core(p, sem, buf, slice(0, ncols), nblk,
                                           qbufs)
            emit_mx_q8(p, sem, qbufs, slice(0, ncols), q8, slice(0, ncols))
            c_qn = emit_mx_expand(p, sem, q8, slice(0, ncols), qbufs["ssc"],
                                  slice(0, nblk), nblk, qnat, slice(0, ncols),
                                  inc=sem["dv"])
            p.wait("tensor", sem["dv"], c_qn)
            if g >= 2:
                p.wait("scalar", sst, stg[g - 2])
            ntp = ncols // 128
            for j in range(ntp):
                t = tp[j % 2]
                c_pe = p.op("tensor", lambda E, j=j, t=t: E.transpose(
                    t[:, 0:128], qnat[:, j * 128:(j + 1) * 128],
                    ident_b[:, :]), inc=sem["pe"])
                p.wait("scalar", sem["pe"], c_pe)
                c_ev = p.op("scalar", lambda E, j=j, t=t, fmb=fmb: E.copy(
                    fmb[:, j * 128:(j + 1) * 128], t[:, 0:128]),
                    reads=(nm(t),), writes=(nm(fmb),), inc=sem["ac"])
                p.wait("tensor", sem["ac"], c_ev)
            p.wait("vector", sem["pe"], sem["pe"].n)
            p.wait("gpsimd", sem["ac"], sem["ac"].n)
            ngrp = ntp // nsplit
            c_st = 0
            for sp_i in range(nsplit):
                out_ap = bass.AP(
                    scr, sp_i * ngrp * 128 * scr_cols + rc * 128,
                    [[scr_cols, 128], [128 * scr_cols, ngrp], [1, 128]])
                in_ap = fmb.ap().rearrange("p (c o) -> p c o", o=128)[
                    :, sp_i * ngrp:(sp_i + 1) * ngrp, :]
                c_st = p.dma("gpsimd", out_ap, in_ap, sst)
            stg[g] = c_st
        p.wait("sync", dsem["wst0"], dsem["wst0"].n)
        p.wait("sync", dsem["wst1"], dsem["wst1"].n)
        with nc.Block() as blk:
            p.flush_into_block(blk)



def _phase_ffn(nc, sem, dsem, sbt, pst, io, pers, src, dst, tag):
    st = contextlib.ExitStack()
    xt = [sbt(st, tag + "_x%d" % i, [128, DM], F32) for i in range(2)]
    lnw_t = sbt(st, tag + "_lnw", [128, DM], F32)
    lnb_t = sbt(st, tag + "_lnb", [128, DM], F32)
    qa_fm = [sbt(st, tag + "_qa%d" % k, [128, TBLK], BF16) for k in range(KC)]
    q8g = [sbt(st, tag + "_q8_%d" % i, [128, HID], I8) for i in range(SWTC)]
    sscg = [sbt(st, tag + "_sg%d" % i, [128, HID // 32], I32)
            for i in range(SWTC)]
    xbb = sbt(st, tag + "_xbb", [128, DM], BF16)
    ntile = sbt(st, tag + "_n", [128, DM], BF16)
    qbufs = dict(am=sbt(st, tag + "_am", [128, 64], F32),
                 rsc=sbt(st, tag + "_rsc", [128, 64], I32),
                 ssc=sbt(st, tag + "_ssc", [128, 64], I32),
                 y=sbt(st, tag + "_y", [128, DM], BF16),
                 r=sbt(st, tag + "_r", [128, DM], F32))
    t1 = qbufs["r"]
    q8a = sbt(st, tag + "_q8a", [128, DM], I8)
    w1sl = [sbt(st, tag + "_w1s%d" % i, [128, KC * OSL], BF16)
            for i in range(2)]
    b1sl = [sbt(st, tag + "_b1s%d" % i, [1, OSL], BF16) for i in range(2)]
    b2sl = sbt(st, tag + "_b2s", [1, DM], BF16)
    gcat = [sbt(st, tag + "_g%d" % i, [128, SWTC * OSL], BF16)
            for i in range(2)]
    h1b = [sbt(st, tag + "_h1b%d" % i, [128, OSL], BF16) for i in range(4)]
    gq = qbufs
    w2sl = [sbt(st, tag + "_w2s%d" % i, [128, DM], BF16) for i in range(2)]
    qgtm = [sbt(st, tag + "_qgtm%d" % i, [128, 128], BF16) for i in range(2)]
    qgfm = [sbt(st, tag + "_qgfm%d" % i, [128, 128], BF16) for i in range(3)]
    h2b = sbt(st, tag + "_h2b", [128, DM], BF16)
    fb = xbb

    ps_h1 = [pst(st, tag + "_ph1%d" % i, [128, 512], F32) for i in range(2)]
    ps_tp = [pst(st, tag + "_ptp%d" % i, [128, 1024], BF16) for i in range(2)]
    ps_h2 = pst(st, tag + "_ph2", [128, DM], F32)

    ident_b = pers["ident_b"]
    ones1 = pers["ones1"]
    stat6, stat2, rsg, musig = (pers["stat6"], pers["stat2"], pers["rsg"],
                                pers["musig"])
    epsc, nhalf = pers["epsc"], pers["nhalf"]
    p = Prog(nc)

    with st:
        c_lnw = p.dma("sync", lnw_t[:, :], io["lnww"].ap(), dsem["blks"])
        c_lnb = p.dma("sync", lnb_t[:, :], io["lnbb"].ap(), dsem["blks"])
        c_b2 = p.dma("gpsimd", b2sl[:, :], io["b2r"].ap(), dsem["blkp"])
        p.wait("vector", dsem["blks"], c_lnb)
        p.wait("tensor", dsem["blkp"], c_b2)

        xt_gate = [[], []]
        rsg_gate = 0
        f1_mm_last = 0
        f2_done_prev = 0
        h2mm_last = {}
        tp_used = {}
        w2_gate = {}

        for sw in range(NSW):
            tcs = [sw * SWTC + i for i in range(SWTC)]
            for i, tc in enumerate(tcs):
                xb = xt[i % 2]
                sx = dsem["xt%d" % (i % 2)]
                for s_, v_ in xt_gate[i % 2]:
                    p.wait("sync", s_, v_)
                c_x = p.dma("sync", xb[:, :],
                            src.ap()[tc * 128:(tc + 1) * 128, :], sx)
                xnm = nm(xb)
                p.wait("vector", sx, c_x)
                c_st = emit_ln_stats(p, sem, xb, stat6, stat2)
                c_rs = emit_rsig(p, sem, stat2, rsg, epsc, nhalf,
                                 gate_dv=max(c_st, rsg_gate))
                p.wait("vector", sem["ac"], c_rs)
                p.op("vector", lambda E, tc=tc: E.tensor_copy(
                    musig[:, 2 * tc:2 * tc + 1], stat2[:, 0:1]),
                    reads=("stat2",), writes=("musig",))
                p.op("vector", lambda E, tc=tc: E.tensor_copy(
                    musig[:, 2 * tc + 1:2 * tc + 2], rsg[:, :]),
                    reads=("rsg",), writes=("musig",))
                c_xbb = p.op("vector", lambda E, xb=xb, tc=tc: E.tensor_scalar(
                    xbb[:, :], xb[:, :], musig[:, 2 * tc:2 * tc + 1],
                    musig[:, 2 * tc + 1:2 * tc + 2], ALU.subtract, ALU.mult),
                    reads=(xnm, "musig"), writes=(nm(xbb),), inc=sem["dv"])
                xt_gate[i % 2] = [(sem["dv"], c_xbb)]
                c_st2 = emit_ln_stats(p, sem, xbb, stat6, stat2)
                c_rs2 = emit_rsig(p, sem, stat2, rsg, epsc, nhalf,
                                  gate_dv=c_st2)
                p.wait("vector", sem["ac"], c_rs2)
                rsg_gate = p.op("vector", lambda E: E.tensor_scalar(
                    t1[:, :], xbb[:, :], stat2[:, 0:1], rsg[:, 0:1],
                    ALU.subtract, ALU.mult),
                    reads=(nm(xbb), "stat2", "rsg"), writes=(nm(t1),),
                    inc=sem["dv"])
                p.op("vector", lambda E: E.tensor_tensor(
                    t1[:, :], t1[:, :], lnw_t[:, :], ALU.mult),
                    reads=(nm(t1), nm(lnw_t)), writes=(nm(t1),))
                p.op("vector", lambda E: E.tensor_tensor(
                    ntile[:, :], t1[:, :], lnb_t[:, :], ALU.add),
                    reads=(nm(t1), nm(lnb_t)), writes=(nm(ntile),))
                emit_mx_quant_core(p, sem, ntile, slice(0, DM), 64, qbufs)
                emit_mx_q8(p, sem, qbufs, slice(0, DM), q8a, slice(0, DM))
                c_qa = emit_mx_expand(p, sem, q8a, slice(0, DM),
                                      qbufs["ssc"], slice(0, 64), 64,
                                      ntile, slice(0, DM), inc=sem["dv"])
                p.wait("tensor", sem["dv"], c_qa)
                if i == 0 and f1_mm_last:
                    p.wait("scalar", sem["pe"], f1_mm_last)
                for kc in range(KC):
                    t = ps_tp[kc % 2]
                    c_pe = p.op("tensor", lambda E, kc=kc, t=t: E.transpose(
                        t[:, 0:128], ntile[:, kc * 128:(kc + 1) * 128],
                        ident_b[:, :]), inc=sem["pe"])
                    p.wait("scalar", sem["pe"], c_pe)
                    c_ev = p.op("scalar", lambda E, kc=kc, t=t, i=i: E.copy(
                        qa_fm[kc][:, i * 128:(i + 1) * 128], t[:, 0:128]),
                        reads=(nm(t),), writes=(nm(qa_fm[kc]),),
                        inc=sem["ac"])
                    p.wait("tensor", sem["ac"], c_ev)
                p.wait("vector", sem["pe"], sem["pe"].n)
            c_qa_all = sem["ac"].n

            p.wait("tensor", sem["ac"], c_qa_all)
            mm_last = {}
            gq_done = {}
            h1_free = {}
            for osl in range(NOSL):
                wb = w1sl[osl % 2]
                bb1 = b1sl[osl % 2]
                sw1 = dsem["w1%d" % (osl % 2)]
                sb1 = dsem["b1%d" % (osl % 2)]
                if osl >= 2:
                    p.wait("sync", sem["pe"], mm_last[osl - 2])
                    p.wait("gpsimd", sem["pe"], mm_last[osl - 2])
                in_ap = bass.AP(io["w1t_s"], osl * OSL,
                                [[HID, 128], [128 * HID, KC], [1, OSL]])
                out_ap = wb.ap().rearrange("p (kc c) -> p kc c", c=OSL)
                c_w = 0
                for hh in range(2):
                    c_w = p.dma("sync", out_ap[:, hh * 8:(hh + 1) * 8, :],
                                in_ap[:, hh * 8:(hh + 1) * 8, :], sw1)
                c_b1 = p.dma("gpsimd", bb1[:, :],
                             io["b1r"].ap()[:, osl * OSL:(osl + 1) * OSL],
                             sb1)
                p.wait("tensor", sw1, c_w)
                p.wait("tensor", sb1, c_b1)
                gc = gcat[osl % 2]
                if osl >= 2:
                    p.wait("scalar", sem["dv"], gq_done[osl - 2])
                for i in range(SWTC):
                    idx = osl * SWTC + i
                    ph = ps_h1[idx % 2]
                    gate = h1_free.get(idx - 2)
                    if gate:
                        p.wait("tensor", sem["ac"], gate)
                    for kc in range(KC):
                        p.op("tensor", lambda E, kc=kc, i=i, ph=ph, wb=wb:
                             E.matmul(ph[:, 0:OSL],
                                      qa_fm[kc][:, i * 128:(i + 1) * 128],
                                      wb[:, kc * OSL:(kc + 1) * OSL],
                                      start=(kc == 0), stop=False))
                    c_mm = p.op("tensor", lambda E, ph=ph, bb1=bb1: E.matmul(
                        ph[:, 0:OSL], ones1[:, :], bb1[:, :],
                        start=False, stop=True), inc=sem["pe"])
                    mm_last[osl] = c_mm
                    hb = h1b[idx % 4]
                    p.wait("scalar", sem["pe"], c_mm)
                    p.op("scalar", lambda E, hb=hb, ph=ph: E.copy(
                        hb[:, :], ph[:, 0:OSL]), reads=(nm(ph),),
                        writes=(nm(hb),))
                    c_gl = p.op("scalar", lambda E, hb=hb, gc=gc, i=i:
                                E.activation(gc[:, i * OSL:(i + 1) * OSL],
                                             hb[:, :], ACTF.Gelu_apprx_tanh,
                                             bias=0.0, scale=1.0),
                                reads=(nm(hb),), writes=(nm(gc),),
                                inc=sem["ac"])
                    h1_free[idx] = c_gl
                p.wait("vector", sem["ac"], sem["ac"].n)
                emit_mx_quant_core(p, sem, gc, slice(0, SWTC * OSL), 64, gq)
                nbo = OSL // 32
                for i in range(SWTC):
                    emit_mx_q8(p, sem, gq, slice(i * OSL, (i + 1) * OSL),
                               q8g[i], slice(osl * OSL, (osl + 1) * OSL))
                    p.op("vector", lambda E, i=i, osl=osl: E.tensor_copy(
                        sscg[i][:, osl * nbo:(osl + 1) * nbo],
                        gq["ssc"][:, i * nbo:(i + 1) * nbo]),
                        reads=(nm(gq["ssc"]),), writes=(nm(sscg[i]),))
                gq_done[osl] = p.op("vector", lambda E: E.drain(),
                                    inc=sem["dv"])
            f1_mm_last = mm_last[NOSL - 1]

            h2_free = f2_done_prev
            for i, tc in enumerate(tcs):
                if h2_free:
                    p.wait("tensor", sem["ac"], h2_free)
                for oc in range(OC):
                    wb2 = w2sl[oc % 2]
                    sw2 = dsem["w2%d" % (oc % 2)]
                    gate = w2_gate.get(oc % 2)
                    if gate is not None:
                        p.wait("sync", sem["pe"], gate)
                    c_w2 = p.dma("sync", wb2[:, :],
                                 bass.AP(io["w2t_s"], oc * 128 * DM,
                                         [[DM, 128], [1, DM]]), sw2)
                    qt = qgtm[oc % 2]
                    gate = tp_used.get(oc % 2)
                    if gate is not None:
                        p.wait("vector", sem["pe"], gate)
                    c_ex = emit_mx_expand(p, sem, q8g[i],
                                          slice(oc * 128, (oc + 1) * 128),
                                          sscg[i], slice(oc * 4, (oc + 1) * 4),
                                          4, qt, slice(0, 128), inc=sem["dv"])
                    p.wait("tensor", sem["dv"], c_ex)
                    t = ps_tp[oc % 2]
                    c_tp = p.op("tensor", lambda E, t=t, qt=qt: E.transpose(
                        t[:, 0:128], qt[:, 0:128], ident_b[:, :]),
                        inc=sem["pe"])
                    tp_used[oc % 2] = c_tp
                    p.wait("scalar", sem["pe"], c_tp)
                    qf = qgfm[oc % 3]
                    gate = h2mm_last.get(oc % 3)
                    if gate is not None:
                        p.wait("scalar", sem["pe"], gate)
                    c_ev = p.op("scalar", lambda E, qf=qf, t=t: E.copy(
                        qf[:, :], t[:, 0:128]), reads=(nm(t),),
                        writes=(nm(qf),), inc=sem["ac"])
                    p.wait("tensor", sem["ac"], c_ev)
                    p.wait("tensor", sw2, c_w2)
                    for ds in range(4):
                        p.op("tensor", lambda E, ds=ds, qf=qf, wb2=wb2, oc=oc:
                             E.matmul(ps_h2[:, ds * 512:(ds + 1) * 512],
                                      qf[:, :],
                                      wb2[:, ds * 512:(ds + 1) * 512],
                                      start=(oc == 0), stop=False))
                    c_h2m = p.op("tensor", lambda E: E.drain(), inc=sem["pe"])
                    h2mm_last[oc % 3] = c_h2m
                    w2_gate[oc % 2] = c_h2m
                for hh in range(4):
                    c_b = p.op("tensor", lambda E, hh=hh: E.matmul(
                        ps_h2[:, hh * 512:(hh + 1) * 512], ones1[:, :],
                        b2sl[:, hh * 512:(hh + 1) * 512],
                        start=False, stop=True), inc=sem["pe"])
                p.wait("scalar", sem["pe"], c_b)
                c_h2b = p.op("scalar", lambda E: E.copy(
                    h2b[:, :], ps_h2[:, :]), reads=(nm(ps_h2),),
                    writes=(nm(h2b),), inc=sem["ac"])
                f2_done_prev = c_h2b
                h2_free = c_h2b
                xb = xt[i % 2]
                sx = dsem["xt%d" % (i % 2)]
                for s_, v_ in xt_gate[i % 2]:
                    p.wait("sync", s_, v_)
                c_x2 = p.dma("sync", xb[:, :],
                             src.ap()[tc * 128:(tc + 1) * 128, :], sx)
                p.wait("vector", sx, c_x2)
                p.wait("vector", sem["ac"], c_h2b)
                xnm = nm(xb)
                p.op("vector", lambda E, tc=tc, xb=xb: E.tensor_scalar(
                    fb[:, :], xb[:, :], musig[:, 2 * tc:2 * tc + 1],
                    musig[:, 2 * tc + 1:2 * tc + 2], ALU.subtract, ALU.mult),
                    reads=(xnm, "musig"), writes=(nm(fb),))
                p.op("vector", lambda E: E.tensor_tensor(
                    fb[:, :], fb[:, :], h2b[:, :], ALU.add),
                    reads=(nm(fb), nm(h2b)), writes=(nm(fb),))
                c_xu = p.op("vector", lambda E, xb=xb: E.scalar_tensor_tensor(
                    xb[:, :], fb[:, :], 0.5, xb[:, :], ALU.mult, ALU.add),
                    reads=(nm(fb), xnm), writes=(xnm,), inc=sem["dv"])
                p.wait("sync", sem["dv"], c_xu)
                c_ws = p.dma("sync", dst.ap()[tc * 128:(tc + 1) * 128, :],
                             xb[:, :], sx)
                xt_gate[i % 2] = [(sx, c_ws)]

        for k in ("xt0", "xt1"):
            p.wait("sync", dsem[k], dsem[k].n)
        with nc.Block() as blk:
            p.flush_into_block(blk)



def _phase_attn(nc, sem, dsem, sbt, pst, io, pers):
    ident_b = pers["ident_b"]
    perm_b = pers["perm_b"]
    mask_b = pers["mask_b"]
    stat6, stat2, rsg = pers["stat6"], pers["stat2"], pers["rsg"]
    zsum, rz, negmax = pers["zsum"], pers["rz"], pers["negmax"]
    epsc, nhalf = pers["epsc"], pers["nhalf"]
    xscr = io["xscr"]
    p = Prog(nc)

    stA = contextlib.ExitStack()
    a_fm = [sbt(stA, "a_fm%d" % k, [128, S], BF16) for k in range(KC)]
    xbuf = [sbt(stA, "at_x%d" % i, [128, DM], F32) for i in range(2)]
    atile = sbt(stA, "at_t", [128, DM], BF16)
    wcol = [sbt(stA, "pj_w%d" % i, [128, KC * 128], BF16) for i in range(2)]
    vwb = sbt(stA, "pj_vw", [128, KC * 512], BF16)
    pout = sbt(stA, "pj_o", [128, S], BF16)
    vout = sbt(stA, "pj_v", [128, 512], BF16)
    ps_tp = [pst(stA, "at_tp%d" % i, [128, 1024], BF16) for i in range(2)]
    ps_p = pst(stA, "pj_ps", [128, 512], F32)
    ps_v = pst(stA, "pj_pv", [128, 512], F32)

    with stA:
        c_at_prev = 0
        xb_gate = [0, 0]
        for tc in range(NTC):
            xb = xbuf[tc % 2]
            sx = dsem["xt%d" % (tc % 2)]
            if xb_gate[tc % 2]:
                p.wait("sync", sem["dv"], xb_gate[tc % 2])
            c_x = p.dma("sync", xb[:, :],
                        xscr.ap()[tc * 128:(tc + 1) * 128, :], sx)
            xnm = nm(xb)
            p.wait("vector", sx, c_x)
            c_st = emit_ln_stats(p, sem, xb, stat6, stat2)
            c_rs = emit_rsig(p, sem, stat2, rsg, epsc, nhalf,
                             gate_dv=max(c_st, c_at_prev))
            p.wait("vector", sem["ac"], c_rs)
            c_at = p.op("vector", lambda E, xb=xb: E.tensor_scalar(
                atile[:, :], xb[:, :], stat2[:, 0:1], rsg[:, 0:1],
                ALU.subtract, ALU.mult), reads=(xnm, "stat2", "rsg"),
                writes=(nm(atile),), inc=sem["dv"])
            c_at_prev = c_at
            xb_gate[tc % 2] = c_at
            p.wait("tensor", sem["dv"], c_at)
            for kc in range(KC):
                t = ps_tp[kc % 2]
                c_pe = p.op("tensor", lambda E, kc=kc, t=t: E.transpose(
                    t[:, 0:128], atile[:, kc * 128:(kc + 1) * 128],
                    ident_b[:, :]), inc=sem["pe"])
                p.wait("scalar", sem["pe"], c_pe)
                c_ev = p.op("scalar", lambda E, kc=kc, t=t, tc=tc: E.copy(
                    a_fm[kc][:, tc * 128:(tc + 1) * 128], t[:, 0:128]),
                    reads=(nm(t),), writes=(nm(a_fm[kc]),), inc=sem["ac"])
                p.wait("tensor", sem["ac"], c_ev)
            p.wait("vector", sem["pe"], sem["pe"].n)
        c_afm = sem["ac"].n
        p.wait("tensor", sem["ac"], c_afm)

        proj_fin = {}
        spl_prev = 0
        for sidx in range(20):
            wb = wcol[sidx % 2]
            swc = dsem["wc%d" % (sidx % 2)]
            gate = proj_fin.get(sidx - 2)
            if gate:
                p.wait("gpsimd", sem["pe"], gate)
            if sidx < 16:
                wt, col0, spill, sp_i = io["wq"], sidx * 128, io["q_s"], sidx
                wcols = 2048
            else:
                wt, col0, spill, sp_i = (io["wk"], (sidx - 16) * 128,
                                         io["k_s"], sidx - 16)
                wcols = 512
            in_ap = bass.AP(wt, col0,
                            [[wcols, 128], [128 * wcols, KC], [1, 128]])
            out_ap = wb.ap().rearrange("p (kc c) -> p kc c", c=128)
            c_wl = p.dma("gpsimd", out_ap, in_ap, swc)
            p.wait("tensor", swc, c_wl)
            for ts in range(4):
                if spl_prev:
                    p.wait("scalar", dsem["spl"], spl_prev)
                for kc in range(KC):
                    p.op("tensor", lambda E, kc=kc, ts=ts, wb=wb: E.matmul(
                        ps_p[:, :], wb[:, kc * 128:(kc + 1) * 128],
                        a_fm[kc][:, ts * 512:(ts + 1) * 512],
                        start=(kc == 0), stop=(kc == KC - 1)))
                c_mm = p.op("tensor", lambda E: E.drain(), inc=sem["pe"])
                p.wait("scalar", sem["pe"], c_mm)
                c_ev = p.op("scalar", lambda E, ts=ts: E.copy(
                    pout[:, ts * 512:(ts + 1) * 512], ps_p[:, :]),
                    reads=(nm(ps_p),), writes=(nm(pout),), inc=sem["ac"])
                p.wait("tensor", sem["ac"], c_ev)
            proj_fin[sidx] = sem["pe"].n
            p.wait("gpsimd", sem["ac"], sem["ac"].n)
            spl_prev = p.dma("gpsimd",
                             bass.AP(spill, sp_i * 128 * S,
                                     [[S, 128], [1, S]]),
                             pout[:, :], dsem["spl"])
        in_ap = bass.AP(io["wv"], 0, [[512, 128], [128 * 512, KC], [1, 512]])
        c_vw = p.dma("gpsimd",
                     vwb.ap().rearrange("p (kc c) -> p kc c", c=512),
                     in_ap, dsem["vwl"])
        p.wait("tensor", dsem["vwl"], c_vw)
        vsp_prev = 0
        for tc in range(NTC):
            for kc in range(KC):
                p.op("tensor", lambda E, kc=kc, tc=tc: E.matmul(
                    ps_v[:, :], a_fm[kc][:, tc * 128:(tc + 1) * 128],
                    vwb[:, kc * 512:(kc + 1) * 512],
                    start=(kc == 0), stop=(kc == KC - 1)))
            c_mm = p.op("tensor", lambda E: E.drain(), inc=sem["pe"])
            p.wait("scalar", sem["pe"], c_mm)
            if vsp_prev:
                p.wait("scalar", dsem["vsp"], vsp_prev)
            c_ev = p.op("scalar", lambda E: E.copy(vout[:, :], ps_v[:, :]),
                        reads=(nm(ps_v),), writes=(nm(vout),), inc=sem["ac"])
            p.wait("tensor", sem["ac"], c_ev)
            p.wait("gpsimd", sem["ac"], c_ev)
            out_ap = bass.AP(io["v_s"], tc * 128 * 128,
                             [[128, 128], [NTC * 128 * 128, 4], [1, 128]])
            vsp_prev = p.dma("gpsimd", out_ap,
                             vout.ap().rearrange("p (g c) -> p g c", c=128),
                             dsem["vsp"])
        p.wait("sync", dsem["vsp"], dsem["vsp"].n)
        p.wait("sync", dsem["spl"], dsem["spl"].n)

    stC = contextlib.ExitStack()
    tabs = {name: sbt(stC, "tab_" + name, [128, S], F32)
            for name in ("cq", "sq", "ck", "sk")}
    kst = sbt(stC, "sc_k", [128, S], BF16)
    krot = sbt(stC, "sc_kr", [128, S], BF16)
    vst = sbt(stC, "sc_v", [128, NTC * 128], BF16)
    qst = [sbt(stC, "sc_q%d" % i, [128, S], BF16) for i in range(2)]
    qrot = sbt(stC, "sc_qr", [128, S], BF16)
    t1r = sbt(stC, "sc_t1", [128, 512], F32)
    t2r = sbt(stC, "sc_t2", [128, 512], F32)
    esb = sbt(stC, "sc_e", [128, S], BF16)
    et = [sbt(stC, "sc_et%d" % i, [128, 128], BF16) for i in range(3)]
    ao = sbt(stC, "sc_ao", [128, S], BF16)
    ps_sc = pst(stC, "sc_ps", [128, S], F32)
    ps_rp = pst(stC, "sc_rp", [128, 512], F32)
    ps_tp2 = [pst(stC, "sc_tp%d" % i, [128, 1024], BF16) for i in range(2)]
    ps_pv = pst(stC, "sc_pv", [128, 512], F32)

    def rope(src_t, dst_t, ctab, stab):
        for q4 in range(4):
            sl = slice(q4 * 512, (q4 + 1) * 512)
            c_pm = p.op("tensor", lambda E, sl=sl, src_t=src_t: E.matmul(
                ps_rp[:, :], perm_b[:, :], src_t[:, sl],
                start=True, stop=True), inc=sem["pe"])
            p.op("vector", lambda E, sl=sl, src_t=src_t, ctab=ctab:
                 E.tensor_tensor(t1r[:, :], src_t[:, sl], ctab[:, sl],
                                 ALU.mult),
                 reads=(nm(src_t), nm(ctab)), writes=(nm(t1r),))
            p.wait("vector", sem["pe"], c_pm)
            p.op("vector", lambda E, sl=sl, stab=stab:
                 E.scalar_tensor_tensor(
                     t2r[:, :], ps_rp[:, :], 0.0, stab[:, sl],
                     ALU.add, ALU.mult),
                 reads=(nm(ps_rp), nm(stab)), writes=(nm(t2r),))
            c_rt = p.op("vector", lambda E, sl=sl, dst_t=dst_t:
                        E.tensor_tensor(dst_t[:, sl], t1r[:, :], t2r[:, :],
                                        ALU.add),
                        reads=(nm(t1r), nm(t2r)), writes=(nm(dst_t),),
                        inc=sem["dv"])
            p.wait("tensor", sem["dv"], c_rt)
        return sem["dv"].n

    with stC:
        for name, srcd in (("cq", io["cqd"]), ("sq", io["sqd"]),
                           ("ck", io["ckd"]), ("sk", io["skd"])):
            p.dma("sync", tabs[name][:, :], srcd.ap(), dsem["blks"])
        p.wait("vector", dsem["blks"], dsem["blks"].n)
        c_exp_prev = 0
        c_nr_prev = 0
        c_pv_prev = 0
        gp_end = (0, 0)
        strip_end = {}
        strip_scores_end = 0
        aosp_prev = 0
        for gp in range(4):
            p.wait("sync", sem["pe"], gp_end[0])
            p.wait("sync", sem["dv"], gp_end[1])
            c_k = p.dma("sync", kst[:, :],
                        bass.AP(io["k_s"], gp * 128 * S, [[S, 128], [1, S]]),
                        dsem["kst"])
            in_ap = bass.AP(io["v_s"], gp * NTC * 128 * 128,
                            [[128, 128], [128 * 128, NTC], [1, 128]])
            c_v = p.dma("sync",
                        vst.ap().rearrange("p (t c) -> p t c", c=128),
                        in_ap, dsem["vst"])
            p.wait("vector", dsem["kst"], c_k)
            p.wait("tensor", dsem["kst"], c_k)
            p.wait("vector", sem["pe"], gp_end[0])
            rope(kst, krot, tabs["ck"], tabs["sk"])
            p.wait("tensor", dsem["vst"], c_v)
            for i in range(4):
                sidx = 4 * gp + i
                qb = qst[i % 2]
                sq = dsem["qs%d" % (i % 2)]
                g2 = strip_end.get(sidx - 2)
                if g2:
                    p.wait("sync", sem["pe"], g2[0])
                    p.wait("sync", sem["dv"], g2[1])
                c_q = p.dma("sync", qb[:, :],
                            bass.AP(io["q_s"], sidx * 128 * S,
                                    [[S, 128], [1, S]]), sq)
                p.wait("vector", sq, c_q)
                p.wait("tensor", sq, c_q)
                if sidx > 0:
                    p.wait("vector", sem["pe"], strip_scores_end)
                c_qrot = rope(qb, qrot, tabs["cq"], tabs["sq"])
                strip_end[sidx] = (sem["pe"].n, c_qrot)
                p.wait("tensor", sem["dv"], c_qrot)
                for hh in range(2):
                    rows = slice(hh * 64, (hh + 1) * 64)
                    for qtc in range(NTC):
                        L = (qtc + 1) * 128
                        if c_exp_prev:
                            p.wait("tensor", sem["ac"], c_exp_prev)
                        off = 0
                        while off < L:
                            n = min(512, L - off)
                            p.op("tensor", lambda E, off=off, n=n, rows=rows,
                                 qtc=qtc: E.matmul(
                                     ps_sc[:, off:off + n],
                                     qrot[rows, qtc * 128:(qtc + 1) * 128],
                                     krot[rows, off:off + n],
                                     start=True, stop=True))
                            off += n
                        c_sc = p.op("tensor", lambda E: E.drain(),
                                    inc=sem["pe"])
                        p.wait("vector", sem["pe"], c_sc)
                        p.op("vector", lambda E, qtc=qtc: E.tensor_tensor(
                            ps_sc[:, qtc * 128:(qtc + 1) * 128],
                            ps_sc[:, qtc * 128:(qtc + 1) * 128],
                            mask_b[:, :], ALU.add),
                            reads=(nm(ps_sc), "mask_b"), writes=(nm(ps_sc),))
                        c_mx = p.op("vector", lambda E, L=L: E.tensor_reduce(
                            negmax[:, :], ps_sc[:, 0:L], AX.X, ALU.max,
                            negate=True), reads=(nm(ps_sc),),
                            writes=("negmax",), inc=sem["dv"])
                        p.wait("scalar", sem["dv"], c_mx)
                        if c_nr_prev:
                            p.wait("scalar", sem["dv"], c_nr_prev)
                        if c_pv_prev:
                            p.wait("scalar", sem["pe"], c_pv_prev)
                        c_ex = p.op("scalar", lambda E, L=L: E.activation(
                            esb[:, 0:L], ps_sc[:, 0:L], ACTF.Exp,
                            bias=negmax[:, 0:1], scale=1.0,
                            accum_out=zsum[:, :]),
                            reads=(nm(ps_sc), "negmax"),
                            writes=(nm(esb), "zsum"), inc=sem["ac"])
                        c_exp_prev = c_ex
                        p.wait("vector", sem["ac"], c_ex)
                        p.op("vector", lambda E: E.reciprocal(
                            rz[:, :], zsum[:, :]), reads=("zsum",),
                            writes=("rz",))
                        c_nr = p.op("vector", lambda E, L=L: E.tensor_scalar(
                            esb[:, 0:L], esb[:, 0:L], rz[:, 0:1], None,
                            ALU.mult), reads=(nm(esb), "rz"),
                            writes=(nm(esb),), inc=sem["dv"])
                        c_nr_prev = c_nr
                        p.wait("tensor", sem["dv"], c_nr)
                        for j in range(qtc + 1):
                            t = ps_tp2[j % 2]
                            c_tp = p.op("tensor", lambda E, j=j, t=t:
                                        E.transpose(
                                            t[:, 0:128],
                                            esb[:, j * 128:(j + 1) * 128],
                                            ident_b[:, :]), inc=sem["pe"])
                            p.wait("scalar", sem["pe"], c_tp)
                            eb = et[j % 3]
                            c_ev = p.op("scalar", lambda E, eb=eb, t=t:
                                        E.copy(eb[:, :], t[:, 0:128]),
                                        reads=(nm(t),), writes=(nm(eb),),
                                        inc=sem["ac"])
                            p.wait("tensor", sem["ac"], c_ev)
                            p.op("tensor", lambda E, j=j, eb=eb,
                                 qtc=qtc: E.matmul(
                                     ps_pv[:, 0:128],
                                     vst[:, j * 128:(j + 1) * 128],
                                     eb[:, :],
                                     start=(j == 0), stop=(j == qtc)))
                        c_pv = p.op("tensor", lambda E: E.drain(),
                                    inc=sem["pe"])
                        c_pv_prev = c_pv
                        p.wait("scalar", sem["pe"], c_pv)
                        c_ao = p.op("scalar", lambda E, hh=hh, qtc=qtc:
                                    E.copy(ao[hh * 64:(hh + 1) * 64,
                                              qtc * 128:(qtc + 1) * 128],
                                           ps_pv[hh * 64:(hh + 1) * 64,
                                                 0:128]),
                                    reads=(nm(ps_pv),), writes=(nm(ao),),
                                    inc=sem["ac"])
                        p.wait("tensor", sem["ac"], c_ao)
                strip_scores_end = sem["pe"].n
                p.wait("gpsimd", sem["ac"], sem["ac"].n)
                if aosp_prev:
                    p.wait("scalar", dsem["aosp"], aosp_prev)
                aosp_prev = p.dma("gpsimd",
                                  bass.AP(io["ao_s"], sidx * 128 * S,
                                          [[S, 128], [1, S]]), ao[:, :],
                                  dsem["aosp"])
            gp_end = (sem["pe"].n, sem["dv"].n)
        p.wait("sync", dsem["aosp"], dsem["aosp"].n)

    stD = contextlib.ExitStack()
    wob = [sbt(stD, "op_w%d" % s, [128, DM], BF16) for s in range(16)]
    aoc = [sbt(stD, "op_a%d" % i, [128, 16 * 128], BF16) for i in range(2)]
    xb2 = [sbt(stD, "op_x%d" % i, [128, DM], F32) for i in range(2)]
    ps_o = pst(stD, "op_ps", [128, DM], F32)
    with stD:
        for s in range(16):
            p.dma("gpsimd", wob[s][:, :],
                  bass.AP(io["wo"], s * 128 * DM, [[DM, 128], [1, DM]]),
                  dsem["blkp"])
        c_wo = dsem["blkp"].n
        p.wait("tensor", dsem["blkp"], c_wo)
        res_done = {}
        for tc in range(NTC):
            ab = aoc[tc % 2]
            xb = xb2[tc % 2]
            sao = dsem["ao%d" % (tc % 2)]
            sxb = dsem["xb%d" % (tc % 2)]
            gate = res_done.get(tc - 2)
            if gate:
                p.wait("sync", sxb, gate)
            in_ap = bass.AP(io["ao_s"], tc * 128,
                            [[S, 128], [128 * S, 16], [1, 128]])
            c_ao = p.dma("sync",
                         ab.ap().rearrange("p (s c) -> p s c", c=128),
                         in_ap, sao)
            c_x = p.dma("sync", xb[:, :],
                        xscr.ap()[tc * 128:(tc + 1) * 128, :], sxb)
            p.wait("tensor", sao, c_ao)
            for s in range(16):
                for ds in range(4):
                    p.op("tensor", lambda E, s=s, ds=ds, ab=ab: E.matmul(
                        ps_o[:, ds * 512:(ds + 1) * 512],
                        ab[:, s * 128:(s + 1) * 128],
                        wob[s][:, ds * 512:(ds + 1) * 512],
                        start=(s == 0), stop=(s == 15)))
            c_om = p.op("tensor", lambda E: E.drain(), inc=sem["pe"])
            p.wait("vector", sem["pe"], c_om)
            p.wait("vector", sxb, c_x)
            xnm = nm(xb)
            c_xr = p.op("vector", lambda E, xb=xb: E.tensor_tensor(
                xb[:, :], xb[:, :], ps_o[:, :], ALU.add),
                reads=(xnm, nm(ps_o)), writes=(xnm,), inc=sem["dv"])
            p.wait("tensor", sem["dv"], c_xr)
            p.wait("sync", sem["dv"], c_xr)
            c_st = p.dma("sync", xscr.ap()[tc * 128:(tc + 1) * 128, :],
                         xb[:, :], sxb)
            res_done[tc] = c_st
        for k in ("xb0", "xb1"):
            p.wait("sync", dsem[k], dsem[k].n)
        with nc.Block() as blk:
            p.flush_into_block(blk)



def kernel(L0_attn_x, L0_attn_Wq, L0_attn_Wk, L0_attn_Wv, L0_attn_Wo,
           L0_ffn_ln_w, L0_ffn_ln_b, L0_ffn_W1, L0_ffn_b1, L0_ffn_W2,
           L0_ffn_b2):
    x = np.ascontiguousarray(np.asarray(L0_attn_x, dtype=np.float32))
    consts = _host_consts()
    perm = _head_perm()
    wq_p = np.ascontiguousarray(np.asarray(L0_attn_Wq, np.float32)[:, perm])
    wo_p = np.ascontiguousarray(np.asarray(L0_attn_Wo, np.float32)[perm, :])
    wk_ = np.ascontiguousarray(np.asarray(L0_attn_Wk, np.float32))
    wv_ = np.ascontiguousarray(np.asarray(L0_attn_Wv, np.float32))
    w1_ = np.ascontiguousarray(np.asarray(L0_ffn_W1, np.float32))
    w2_ = np.ascontiguousarray(np.asarray(L0_ffn_W2, np.float32))
    lnw = np.ascontiguousarray(np.broadcast_to(
        np.asarray(L0_ffn_ln_w, np.float32)[None, :], (128, DM)))
    lnb = np.ascontiguousarray(np.broadcast_to(
        np.asarray(L0_ffn_ln_b, np.float32)[None, :], (128, DM)))
    b1v = np.ascontiguousarray(np.asarray(L0_ffn_b1, np.float32)[None, :])
    b2v = np.ascontiguousarray(np.asarray(L0_ffn_b2, np.float32)[None, :])

    nc = build_nc()
    base = {"wq": wq_p, "wk": wk_, "wv": wv_, "wo": wo_p, "w1": w1_,
            "w2": w2_, "lnww": lnw, "lnbb": lnb, "b1r": b1v, "b2r": b2v,
            "cqd": consts["cq"], "sqd": consts["sq"], "ckd": consts["ck"],
            "skd": consts["sk"], "permd": consts["perm"],
            "identd": consts["ident"], "maskd": consts["mask"]}
    in_maps = [dict(base, xin=x[c]) for c in range(8)]
    res = run_bass_kernel_spmd(nc, in_maps, core_ids=list(range(8)))
    return np.stack([np.asarray(r["yout"], dtype=np.float32)
                     for r in res.results], axis=0)


# version 06
# speedup vs baseline: 1.1978x; 1.1978x over previous
import contextlib
import os
import numpy as np
import concourse.bass as bass
import concourse.mybir as mybir
from concourse.bass_utils import run_bass_kernel_spmd

try:
    import ml_dtypes
    _BF16_NP = ml_dtypes.bfloat16
except ImportError:
    _BF16_NP = None

F32 = mybir.dt.float32
BF16 = mybir.dt.bfloat16
I32 = mybir.dt.int32
I8 = mybir.dt.int8
ALU = mybir.AluOpType
AX = mybir.AxisListType
ACTF = mybir.ActivationFunctionType

S = 2048
DM = 2048
HID = 8192
NTC = S // 128
KC = DM // 128
OC = HID // 128
TBLK = 1024
NSW = S // TBLK
SWTC = TBLK // 128
OSL = 256
NOSL = HID // OSL
HD = 64
EPS = 1e-5
NEG = -1.0e30
MAGIC = 12582912.0
CLIPC = float(np.nextafter(np.float32(127.5), np.float32(0)))

ENGS = ("sync", "gpsimd", "vector", "scalar", "tensor")
SIM_DRAINS = True
PHASES = ("wprep", "fa", "attn", "fb")
DBG_SCRATCH = False


class Sem:
    def __init__(self, h):
        self.h = h
        self.n = 0

    def inc(self, k=1):
        self.n += k
        return self.n


class Prog:
    def __init__(self, nc):
        self.nc = nc
        self.q = {e: [] for e in ENGS}
        self.dirty = {e: set() for e in ENGS}

    def wait(self, eng, sem, val):
        if val > 0:
            self.q[eng].append(lambda E, s=sem, v=val: E.wait_ge(s.h, v))

    def op(self, eng, build, reads=(), writes=(), inc=None, inc_n=1):
        names = set(reads) | set(writes)
        if (SIM_DRAINS and eng in ("vector", "scalar", "gpsimd")
                and (names & self.dirty[eng])):
            self.q[eng].append(lambda E: E.drain())
            self.dirty[eng] = set()
        self.dirty[eng] |= set(writes)
        if inc is not None:
            def fn(E, b=build, s=inc, n=inc_n):
                b(E).then_inc(s.h, n)
            self.q[eng].append(fn)
            return inc.inc(inc_n)
        self.q[eng].append(build)
        return None

    def dma(self, eng, out_ap, in_ap, sem, **kw):
        def fn(E, o=out_ap, i=in_ap, s=sem, kw=kw):
            E.dma_start(o, i, **kw).then_inc(s.h, 16)
        self.q[eng].append(fn)
        return sem.inc(16)

    def flush_into_block(self, block):
        q = self.q

        @block.sync
        def _(e):
            for fn in q["sync"]:
                fn(e)

        @block.gpsimd
        def _(e):
            for fn in q["gpsimd"]:
                fn(e)

        @block.vector
        def _(e):
            for fn in q["vector"]:
                fn(e)

        @block.scalar
        def _(e):
            for fn in q["scalar"]:
                fn(e)

        @block.tensor
        def _(e):
            for fn in q["tensor"]:
                fn(e)


def nm(t):
    return t.ap().name


def _host_consts():
    half = HD // 2
    inv = (1.0 / np.power(np.float32(10000.0),
                          np.arange(half, dtype=np.float32) / np.float32(half))
           ).astype(np.float32)
    pos = np.arange(S, dtype=np.float32)
    fr = np.outer(pos, inv).astype(np.float32)
    cos = np.cos(fr).astype(np.float32)
    sin = np.sin(fr).astype(np.float32)
    j_of_r = (np.arange(128) % HD) // 2
    C = cos[:, j_of_r].T.astype(np.float32)
    Sg = sin[:, j_of_r].T.astype(np.float32)
    sign = np.where(np.arange(128) % 2 == 0, -1.0, 1.0).astype(np.float32)
    Stab = (Sg * sign[:, None]).astype(np.float32)
    swap = np.arange(128)
    swap = swap + np.where(swap % 2 == 0, 1, -1)
    perm = np.zeros((128, 128), dtype=np.float32)
    perm[swap, np.arange(128)] = 1.0
    ident = np.eye(128, dtype=np.float32)
    ii = np.arange(128)
    mask = np.where(ii[None, :] <= ii[:, None], 0.0, NEG).astype(np.float32)
    if _BF16_NP is not None:
        perm = perm.astype(_BF16_NP)
        ident = ident.astype(_BF16_NP)
    return dict(cq=(C * np.float32(0.125)).astype(np.float32),
                sq=(Stab * np.float32(0.125)).astype(np.float32),
                ck=C, sk=Stab, perm=perm, ident=ident, mask=mask)


def _head_perm():
    order = []
    for gp in range(4):
        for i in range(4):
            for h in (8 * gp + i, 8 * gp + 4 + i):
                order.extend(range(h * HD, (h + 1) * HD))
    return np.asarray(order, dtype=np.int64)



def emit_mx_quant_core(p, sem, src_t, src_sl, nblk, bufs):
    am, rsc, ssc, y, r = (bufs["am"], bufs["rsc"], bufs["ssc"], bufs["y"],
                          bufs["r"])
    n = nblk * 32
    src3 = src_t.ap()[:, src_sl].rearrange("p (nb k) -> p nb k", k=32)
    p.op("vector", lambda E: E.tensor_reduce(
        am[:, 0:nblk], src3, AX.X, ALU.max, apply_absolute_value=True),
        reads=(nm(src_t),), writes=(nm(am),))
    p.op("vector", lambda E: E.tensor_single_scalar(
        rsc[:, 0:nblk], am.ap()[:, 0:nblk].bitcast(I32), 23,
        ALU.logical_shift_right), reads=(nm(am),), writes=(nm(rsc),))
    p.op("vector", lambda E: E.tensor_scalar_max(
        rsc[:, 0:nblk], rsc[:, 0:nblk], 64),
        reads=(nm(rsc),), writes=(nm(rsc),))
    p.op("vector", lambda E: E.tensor_scalar(
        ssc[:, 0:nblk], rsc[:, 0:nblk], -6, 8388608, ALU.add, ALU.mult),
        reads=(nm(rsc),), writes=(nm(ssc),))
    p.op("vector", lambda E: E.tensor_scalar(
        rsc[:, 0:nblk], rsc[:, 0:nblk], -260, -8388608, ALU.add, ALU.mult),
        reads=(nm(rsc),), writes=(nm(rsc),))
    rb = rsc.ap()[:, 0:nblk].bitcast(F32).unsqueeze(2).broadcast_to(
        [128, nblk, 32])
    y3 = y.ap()[:, 0:n].rearrange("p (nb k) -> p nb k", k=32)
    p.op("vector", lambda E: E.tensor_tensor(y3, src3, rb, ALU.mult),
         reads=(nm(src_t), nm(rsc)), writes=(nm(y),))
    return p.op("vector", lambda E: E.tensor_scalar(
        r[:, 0:n], y[:, 0:n], MAGIC, MAGIC, ALU.add, ALU.subtract),
        reads=(nm(y),), writes=(nm(r),), inc=sem["dv"])


def emit_mx_q8(p, sem, bufs, r_sl, q8_t, q8_sl, inc=None):
    r = bufs["r"]
    return p.op("vector", lambda E: E.tensor_scalar(
        q8_t.ap()[:, q8_sl], r[:, r_sl], 127.0, None, ALU.min),
        reads=(nm(r),), writes=(nm(q8_t),), inc=inc)


def emit_mx_expand(p, sem, q8_t, q8_sl, ssc_t, ssc_sl, nblk, out_t, out_sl,
                   inc=None):
    q83 = q8_t.ap()[:, q8_sl].rearrange("p (nb k) -> p nb k", k=32)
    sc3 = ssc_t.ap()[:, ssc_sl].bitcast(F32).unsqueeze(2).broadcast_to(
        [128, nblk, 32])
    o3 = out_t.ap()[:, out_sl].rearrange("p (nb k) -> p nb k", k=32)
    return p.op("vector", lambda E: E.scalar_tensor_tensor(
        o3, q83, 0.0, sc3, ALU.add, ALU.mult),
        reads=(nm(q8_t), nm(ssc_t)), writes=(nm(out_t),), inc=inc)


def emit_ln_stats(p, sem, src_t, stat6, stat2):
    for j in range(4):
        p.op("vector", lambda E, j=j, src_t=src_t: E.bn_stats(
            stat6[:, 6 * j:6 * (j + 1)], src_t[:, 512 * j:512 * (j + 1)]),
            reads=(nm(src_t),), writes=("stat6",))
    return p.op("vector", lambda E: E.bn_aggr(stat2[:, :], stat6[:, :]),
                reads=("stat6",), writes=("stat2",), inc=sem["dv"])


def emit_rsig(p, sem, stat2, rsg, epsc, nhalf, gate_dv=0):
    if gate_dv:
        p.wait("scalar", sem["dv"], gate_dv)
    p.op("scalar", lambda E: E.activation(
        rsg[:, :], stat2[:, 1:2], ACTF.Ln, bias=epsc[:, 0:1], scale=1.0),
        reads=("stat2",), writes=("rsg",))
    return p.op("scalar", lambda E: E.activation(
        rsg[:, :], rsg[:, :], ACTF.Exp, bias=0.0, scale=nhalf[:, 0:1]),
        reads=("rsg",), writes=("rsg",), inc=sem["ac"])



def build_nc(phases=None):
    phases = PHASES if phases is None else phases
    scratch_kind = "ExternalOutput" if DBG_SCRATCH else "Internal"
    nc = bass.Bass("TRN2", target_bir_lowering=False)

    xin = nc.dram_tensor("xin", [S, DM], F32, kind="ExternalInput")
    wq = nc.dram_tensor("wq", [DM, DM], F32, kind="ExternalInput")
    wk = nc.dram_tensor("wk", [DM, 512], F32, kind="ExternalInput")
    wv = nc.dram_tensor("wv", [DM, 512], F32, kind="ExternalInput")
    wo = nc.dram_tensor("wo", [DM, DM], F32, kind="ExternalInput")
    w1 = nc.dram_tensor("w1", [HID, DM], F32, kind="ExternalInput")
    w2 = nc.dram_tensor("w2", [DM, HID], F32, kind="ExternalInput")
    lnww = nc.dram_tensor("lnww", [128, DM], F32, kind="ExternalInput")
    lnbb = nc.dram_tensor("lnbb", [128, DM], F32, kind="ExternalInput")
    b1r = nc.dram_tensor("b1r", [1, HID], F32, kind="ExternalInput")
    b2r = nc.dram_tensor("b2r", [1, DM], F32, kind="ExternalInput")
    cqd = nc.dram_tensor("cqd", [128, S], F32, kind="ExternalInput")
    sqd = nc.dram_tensor("sqd", [128, S], F32, kind="ExternalInput")
    ckd = nc.dram_tensor("ckd", [128, S], F32, kind="ExternalInput")
    skd = nc.dram_tensor("skd", [128, S], F32, kind="ExternalInput")
    permd = nc.dram_tensor("permd", [128, 128], BF16, kind="ExternalInput")
    identd = nc.dram_tensor("identd", [128, 128], BF16, kind="ExternalInput")
    maskd = nc.dram_tensor("maskd", [128, 128], F32, kind="ExternalInput")
    yout = nc.dram_tensor("yout", [S, DM], F32, kind="ExternalOutput")

    xscr = nc.dram_tensor("xscr", [S, DM], F32, kind=scratch_kind)
    w1t_s = nc.dram_tensor("w1t_s", [KC, 128, HID], BF16, kind=scratch_kind)
    w2t_s = nc.dram_tensor("w2t_s", [OC, 128, DM], BF16, kind=scratch_kind)
    q_s = nc.dram_tensor("q_s", [16, 128, S], BF16, kind=scratch_kind)
    k_s = nc.dram_tensor("k_s", [4, 128, S], BF16, kind=scratch_kind)
    v_s = nc.dram_tensor("v_s", [4, NTC, 128, 128], BF16, kind=scratch_kind)
    ao_s = nc.dram_tensor("ao_s", [16, 128, S], BF16, kind=scratch_kind)

    st = contextlib.ExitStack()
    sem = {k: Sem(st.enter_context(nc.semaphore("sem_" + k)))
           for k in ("dv", "ac", "pe")}
    dnames = ("blks", "blkp", "xt0", "xt1", "w10", "w11", "b10", "b11",
              "w20", "w21", "wc0", "wc1", "vwl", "qs0", "qs1", "kst", "vst",
              "spl", "vsp", "aosp", "ao0", "ao1", "xb0", "xb1", "wld0",
              "wld1", "wst0", "wst1")
    dsem = {k: Sem(st.enter_context(nc.semaphore("dma_" + k)))
            for k in dnames}
    dq = {"blkp": "gpsimd", "spl": "gpsimd", "vsp": "gpsimd",
          "aosp": "gpsimd", "wst0": "gpsimd", "wst1": "gpsimd",
          "vwl": "gpsimd", "wc0": "gpsimd", "wc1": "gpsimd",
          "b10": "gpsimd", "b11": "gpsimd"}

    def sbt(stack, name, shape, dtype):
        return stack.enter_context(nc.sbuf_tensor(name, shape, dtype))

    def pst(stack, name, shape, dtype):
        return stack.enter_context(nc.psum_tensor(name, shape, dtype))

    ident_b = sbt(st, "ident_b", [128, 128], BF16)
    perm_b = sbt(st, "perm_b", [128, 128], BF16)
    mask_b = sbt(st, "mask_b", [128, 128], F32)
    ones1 = sbt(st, "ones1", [1, 128], BF16)
    stat6 = sbt(st, "stat6", [128, 24], F32)
    stat2 = sbt(st, "stat2", [128, 2], F32)
    rsg = sbt(st, "rsg", [128, 1], F32)
    musig = sbt(st, "musig", [128, 2 * NTC], F32)
    zsum = sbt(st, "zsum", [128, 1], F32)
    rz = sbt(st, "rz", [128, 1], F32)
    negmax = sbt(st, "negmax", [128, 1], F32)
    epsc = sbt(st, "epsc", [128, 1], F32)
    nhalf = sbt(st, "nhalf", [128, 1], F32)

    io = dict(xin=xin, wq=wq, wk=wk, wv=wv, wo=wo, w1=w1, w2=w2,
              lnww=lnww, lnbb=lnbb, b1r=b1r, b2r=b2r, cqd=cqd, sqd=sqd,
              ckd=ckd, skd=skd, permd=permd, identd=identd, maskd=maskd,
              yout=yout, xscr=xscr, w1t_s=w1t_s, w2t_s=w2t_s, q_s=q_s,
              k_s=k_s, v_s=v_s, ao_s=ao_s)
    pers = dict(ident_b=ident_b, perm_b=perm_b, mask_b=mask_b, ones1=ones1,
                stat6=stat6, stat2=stat2, rsg=rsg, musig=musig, zsum=zsum,
                rz=rz, negmax=negmax, epsc=epsc, nhalf=nhalf)

    with st:
        p = Prog(nc)
        for dst, src in ((ident_b, identd), (perm_b, permd), (mask_b, maskd)):
            p.dma("sync", dst[:, :], src.ap(), dsem["blks"])
        p.wait("vector", dsem["blks"], dsem["blks"].n)
        p.op("vector", lambda E: E.memset(ones1[:, :], 1.0), writes=("ones1",))
        p.op("vector", lambda E: E.memset(epsc[:, :], EPS), writes=("epsc",))
        p.op("vector", lambda E: E.memset(nhalf[:, :], -0.5),
             writes=("nhalf",), inc=sem["dv"])
        p.wait("tensor", sem["dv"], sem["dv"].n)
        p.wait("scalar", sem["dv"], sem["dv"].n)
        with nc.Block() as blk:
            p.flush_into_block(blk)

        if "wprep" in phases:
            _phase_wprep(nc, sem, dsem, sbt, pst, io, pers)
        if "fa" in phases:
            _phase_ffn(nc, sem, dsem, sbt, pst, io, pers, xin, xscr, "fa")
        else:
            p = Prog(nc)
            for tc in range(NTC):
                p.dma("sync", xscr.ap()[tc * 128:(tc + 1) * 128, :],
                      xin.ap()[tc * 128:(tc + 1) * 128, :], dsem["blks"])
            p.wait("sync", dsem["blks"], dsem["blks"].n)
            with nc.Block() as blk:
                p.flush_into_block(blk)
        if "attn" in phases:
            _phase_attn(nc, sem, dsem, sbt, pst, io, pers)
        if "fb" in phases:
            _phase_ffn(nc, sem, dsem, sbt, pst, io, pers, xscr, yout, "fb")
        else:
            p = Prog(nc)
            for tc in range(NTC):
                p.dma("sync", yout.ap()[tc * 128:(tc + 1) * 128, :],
                      xscr.ap()[tc * 128:(tc + 1) * 128, :], dsem["blks"])
            p.wait("sync", dsem["blks"], dsem["blks"].n)
            with nc.Block() as blk:
                p.flush_into_block(blk)

        p = Prog(nc)
        for k in dnames:
            p.wait(dq.get(k, "sync"), dsem[k], dsem[k].n)
        with nc.Block() as blk:
            p.flush_into_block(blk)

    nc.finalize()
    return nc



def _phase_wprep(nc, sem, dsem, sbt, pst, io, pers):
    st = contextlib.ExitStack()
    wld = [sbt(st, "wp_ld%d" % i, [128, HID], F32) for i in range(2)]
    qbufs = dict(am=sbt(st, "wp_am", [128, 256], F32),
                 rsc=sbt(st, "wp_rsc", [128, 256], I32),
                 ssc=sbt(st, "wp_ssc", [128, 256], I32),
                 y=sbt(st, "wp_y", [128, HID], F32),
                 r=sbt(st, "wp_r", [128, HID], F32))
    q8 = sbt(st, "wp_q8", [128, HID], I8)
    qnat = sbt(st, "wp_qn", [128, HID], BF16)
    wfm = [sbt(st, "wp_fm%d" % i, [128, HID], BF16) for i in range(2)]
    tp = [pst(st, "wp_tp%d" % i, [128, 1024], BF16) for i in range(2)]
    ident_b = pers["ident_b"]

    p = Prog(nc)
    q_done = {}
    stg = {}
    specs = []
    for rc in range(HID // 128):
        specs.append((io["w1"], io["w1t_s"], rc, DM, 4, HID))
    for rc in range(DM // 128):
        specs.append((io["w2"], io["w2t_s"], rc, HID, 8, DM))

    with st:
        for g, (wt, scr, rc, ncols, nsplit, scr_cols) in enumerate(specs):
            buf = wld[g % 2]
            fmb = wfm[g % 2]
            sld = dsem["wld%d" % (g % 2)]
            sst = dsem["wst%d" % (g % 2)]
            if g >= 2:
                p.wait("sync", sem["dv"], q_done[g - 2])
            c_ld = p.dma("sync", buf[:, 0:ncols],
                         wt.ap()[rc * 128:(rc + 1) * 128, :], sld)
            p.wait("vector", sld, c_ld)
            nblk = ncols // 32
            q_done[g] = emit_mx_quant_core(p, sem, buf, slice(0, ncols), nblk,
                                           qbufs)
            emit_mx_q8(p, sem, qbufs, slice(0, ncols), q8, slice(0, ncols))
            c_qn = emit_mx_expand(p, sem, q8, slice(0, ncols), qbufs["ssc"],
                                  slice(0, nblk), nblk, qnat, slice(0, ncols),
                                  inc=sem["dv"])
            p.wait("tensor", sem["dv"], c_qn)
            if g >= 2:
                p.wait("scalar", sst, stg[g - 2])
            ntp = ncols // 128
            evs = {}
            for j in range(ntp):
                t = tp[j % 2]
                if j >= 2:
                    p.wait("tensor", sem["ac"], evs[j - 2])
                c_pe = p.op("tensor", lambda E, j=j, t=t: E.transpose(
                    t[:, 0:128], qnat[:, j * 128:(j + 1) * 128],
                    ident_b[:, :]), inc=sem["pe"])
                p.wait("scalar", sem["pe"], c_pe)
                evs[j] = p.op("scalar", lambda E, j=j, t=t, fmb=fmb: E.copy(
                    fmb[:, j * 128:(j + 1) * 128], t[:, 0:128]),
                    reads=(nm(t),), writes=(nm(fmb),), inc=sem["ac"])
            p.wait("vector", sem["pe"], sem["pe"].n)
            p.wait("gpsimd", sem["ac"], sem["ac"].n)
            ngrp = ntp // nsplit
            c_st = 0
            for sp_i in range(nsplit):
                out_ap = bass.AP(
                    scr, sp_i * ngrp * 128 * scr_cols + rc * 128,
                    [[scr_cols, 128], [128 * scr_cols, ngrp], [1, 128]])
                in_ap = fmb.ap().rearrange("p (c o) -> p c o", o=128)[
                    :, sp_i * ngrp:(sp_i + 1) * ngrp, :]
                c_st = p.dma("gpsimd", out_ap, in_ap, sst)
            stg[g] = c_st
        p.wait("sync", dsem["wst0"], dsem["wst0"].n)
        p.wait("sync", dsem["wst1"], dsem["wst1"].n)
        with nc.Block() as blk:
            p.flush_into_block(blk)



def _phase_ffn(nc, sem, dsem, sbt, pst, io, pers, src, dst, tag):
    st = contextlib.ExitStack()
    xt = [sbt(st, tag + "_x%d" % i, [128, DM], F32) for i in range(2)]
    lnw_t = sbt(st, tag + "_lnw", [128, DM], F32)
    lnb_t = sbt(st, tag + "_lnb", [128, DM], F32)
    qa_fm = [sbt(st, tag + "_qa%d" % k, [128, TBLK], BF16) for k in range(KC)]
    q8g = [sbt(st, tag + "_q8_%d" % i, [128, HID], I8) for i in range(SWTC)]
    sscg = [sbt(st, tag + "_sg%d" % i, [128, HID // 32], I32)
            for i in range(SWTC)]
    xbb = sbt(st, tag + "_xbb", [128, DM], BF16)
    ntile = sbt(st, tag + "_n", [128, DM], BF16)
    qbufs = dict(am=sbt(st, tag + "_am", [128, 64], F32),
                 rsc=sbt(st, tag + "_rsc", [128, 64], I32),
                 ssc=sbt(st, tag + "_ssc", [128, 64], I32),
                 y=sbt(st, tag + "_y", [128, DM], BF16),
                 r=sbt(st, tag + "_r", [128, DM], F32))
    t1 = qbufs["r"]
    q8a = sbt(st, tag + "_q8a", [128, DM], I8)
    w1sl = [sbt(st, tag + "_w1s%d" % i, [128, KC * OSL], BF16)
            for i in range(2)]
    b1sl = [sbt(st, tag + "_b1s%d" % i, [1, OSL], BF16) for i in range(2)]
    b2sl = sbt(st, tag + "_b2s", [1, DM], BF16)
    gcat = [sbt(st, tag + "_g%d" % i, [128, SWTC * OSL], BF16)
            for i in range(2)]
    h1b = [sbt(st, tag + "_h1b%d" % i, [128, OSL], BF16) for i in range(4)]
    gq = qbufs
    w2sl = [sbt(st, tag + "_w2s%d" % i, [128, DM], BF16) for i in range(2)]
    qgtm = [sbt(st, tag + "_qgtm%d" % i, [128, 128], BF16) for i in range(2)]
    qgfm = [sbt(st, tag + "_qgfm%d" % i, [128, 128], BF16) for i in range(3)]
    h2b = sbt(st, tag + "_h2b", [128, DM], BF16)
    fb = xbb

    ps_h1 = [pst(st, tag + "_ph1%d" % i, [128, 512], F32) for i in range(2)]
    ps_tp = [pst(st, tag + "_ptp%d" % i, [128, 1024], BF16) for i in range(2)]
    ps_h2 = pst(st, tag + "_ph2", [128, DM], F32)

    ident_b = pers["ident_b"]
    ones1 = pers["ones1"]
    stat6, stat2, rsg, musig = (pers["stat6"], pers["stat2"], pers["rsg"],
                                pers["musig"])
    epsc, nhalf = pers["epsc"], pers["nhalf"]
    p = Prog(nc)

    with st:
        c_lnw = p.dma("sync", lnw_t[:, :], io["lnww"].ap(), dsem["blks"])
        c_lnb = p.dma("sync", lnb_t[:, :], io["lnbb"].ap(), dsem["blks"])
        c_b2 = p.dma("gpsimd", b2sl[:, :], io["b2r"].ap(), dsem["blkp"])
        p.wait("vector", dsem["blks"], c_lnb)
        p.wait("tensor", dsem["blkp"], c_b2)

        xt_gate = [[], []]
        rsg_gate = 0
        f1_mm_last = 0
        f2_done_prev = 0
        h2mm_last = {}
        tp_used = {}
        w2_gate = {}

        for sw in range(NSW):
            tcs = [sw * SWTC + i for i in range(SWTC)]
            for i, tc in enumerate(tcs):
                xb = xt[i % 2]
                sx = dsem["xt%d" % (i % 2)]
                for s_, v_ in xt_gate[i % 2]:
                    p.wait("sync", s_, v_)
                c_x = p.dma("sync", xb[:, :],
                            src.ap()[tc * 128:(tc + 1) * 128, :], sx)
                xnm = nm(xb)
                p.wait("vector", sx, c_x)
                c_st = emit_ln_stats(p, sem, xb, stat6, stat2)
                c_rs = emit_rsig(p, sem, stat2, rsg, epsc, nhalf,
                                 gate_dv=max(c_st, rsg_gate))
                p.wait("vector", sem["ac"], c_rs)
                p.op("vector", lambda E, tc=tc: E.tensor_copy(
                    musig[:, 2 * tc:2 * tc + 1], stat2[:, 0:1]),
                    reads=("stat2",), writes=("musig",))
                p.op("vector", lambda E, tc=tc: E.tensor_copy(
                    musig[:, 2 * tc + 1:2 * tc + 2], rsg[:, :]),
                    reads=("rsg",), writes=("musig",))
                c_xbb = p.op("vector", lambda E, xb=xb, tc=tc: E.tensor_scalar(
                    xbb[:, :], xb[:, :], musig[:, 2 * tc:2 * tc + 1],
                    musig[:, 2 * tc + 1:2 * tc + 2], ALU.subtract, ALU.mult),
                    reads=(xnm, "musig"), writes=(nm(xbb),), inc=sem["dv"])
                xt_gate[i % 2] = [(sem["dv"], c_xbb)]
                c_st2 = emit_ln_stats(p, sem, xbb, stat6, stat2)
                c_rs2 = emit_rsig(p, sem, stat2, rsg, epsc, nhalf,
                                  gate_dv=c_st2)
                p.wait("vector", sem["ac"], c_rs2)
                rsg_gate = p.op("vector", lambda E: E.tensor_scalar(
                    t1[:, :], xbb[:, :], stat2[:, 0:1], rsg[:, 0:1],
                    ALU.subtract, ALU.mult),
                    reads=(nm(xbb), "stat2", "rsg"), writes=(nm(t1),),
                    inc=sem["dv"])
                p.op("vector", lambda E: E.tensor_tensor(
                    t1[:, :], t1[:, :], lnw_t[:, :], ALU.mult),
                    reads=(nm(t1), nm(lnw_t)), writes=(nm(t1),))
                p.op("vector", lambda E: E.tensor_tensor(
                    ntile[:, :], t1[:, :], lnb_t[:, :], ALU.add),
                    reads=(nm(t1), nm(lnb_t)), writes=(nm(ntile),))
                emit_mx_quant_core(p, sem, ntile, slice(0, DM), 64, qbufs)
                emit_mx_q8(p, sem, qbufs, slice(0, DM), q8a, slice(0, DM))
                c_qa = emit_mx_expand(p, sem, q8a, slice(0, DM),
                                      qbufs["ssc"], slice(0, 64), 64,
                                      ntile, slice(0, DM), inc=sem["dv"])
                p.wait("tensor", sem["dv"], c_qa)
                if i == 0 and f1_mm_last:
                    p.wait("scalar", sem["pe"], f1_mm_last)
                evs = {}
                for kc in range(KC):
                    t = ps_tp[kc % 2]
                    if kc >= 2:
                        p.wait("tensor", sem["ac"], evs[kc - 2])
                    c_pe = p.op("tensor", lambda E, kc=kc, t=t: E.transpose(
                        t[:, 0:128], ntile[:, kc * 128:(kc + 1) * 128],
                        ident_b[:, :]), inc=sem["pe"])
                    p.wait("scalar", sem["pe"], c_pe)
                    evs[kc] = p.op("scalar", lambda E, kc=kc, t=t, i=i:
                                   E.copy(qa_fm[kc][:, i * 128:(i + 1) * 128],
                                          t[:, 0:128]),
                                   reads=(nm(t),), writes=(nm(qa_fm[kc]),),
                                   inc=sem["ac"])
                p.wait("vector", sem["pe"], sem["pe"].n)
            c_qa_all = sem["ac"].n

            p.wait("tensor", sem["ac"], c_qa_all)
            mm_last = {}
            gq_done = {}
            h1_free = {}
            for osl in range(NOSL):
                wb = w1sl[osl % 2]
                bb1 = b1sl[osl % 2]
                sw1 = dsem["w1%d" % (osl % 2)]
                sb1 = dsem["b1%d" % (osl % 2)]
                if osl >= 2:
                    p.wait("sync", sem["pe"], mm_last[osl - 2])
                    p.wait("gpsimd", sem["pe"], mm_last[osl - 2])
                in_ap = bass.AP(io["w1t_s"], osl * OSL,
                                [[HID, 128], [128 * HID, KC], [1, OSL]])
                out_ap = wb.ap().rearrange("p (kc c) -> p kc c", c=OSL)
                c_w = 0
                for hh in range(2):
                    c_w = p.dma("sync", out_ap[:, hh * 8:(hh + 1) * 8, :],
                                in_ap[:, hh * 8:(hh + 1) * 8, :], sw1)
                c_b1 = p.dma("gpsimd", bb1[:, :],
                             io["b1r"].ap()[:, osl * OSL:(osl + 1) * OSL],
                             sb1)
                p.wait("tensor", sw1, c_w)
                p.wait("tensor", sb1, c_b1)
                gc = gcat[osl % 2]
                if osl >= 2:
                    p.wait("scalar", sem["dv"], gq_done[osl - 2])
                for i in range(SWTC):
                    idx = osl * SWTC + i
                    ph = ps_h1[idx % 2]
                    gate = h1_free.get(idx - 2)
                    if gate:
                        p.wait("tensor", sem["ac"], gate)
                    for kc in range(KC):
                        p.op("tensor", lambda E, kc=kc, i=i, ph=ph, wb=wb:
                             E.matmul(ph[:, 0:OSL],
                                      qa_fm[kc][:, i * 128:(i + 1) * 128],
                                      wb[:, kc * OSL:(kc + 1) * OSL],
                                      start=(kc == 0), stop=False))
                    c_mm = p.op("tensor", lambda E, ph=ph, bb1=bb1: E.matmul(
                        ph[:, 0:OSL], ones1[:, :], bb1[:, :],
                        start=False, stop=True), inc=sem["pe"])
                    mm_last[osl] = c_mm
                    hb = h1b[idx % 4]
                    p.wait("scalar", sem["pe"], c_mm)
                    p.op("scalar", lambda E, hb=hb, ph=ph: E.copy(
                        hb[:, :], ph[:, 0:OSL]), reads=(nm(ph),),
                        writes=(nm(hb),))
                    c_gl = p.op("scalar", lambda E, hb=hb, gc=gc, i=i:
                                E.activation(gc[:, i * OSL:(i + 1) * OSL],
                                             hb[:, :], ACTF.Gelu_apprx_tanh,
                                             bias=0.0, scale=1.0),
                                reads=(nm(hb),), writes=(nm(gc),),
                                inc=sem["ac"])
                    h1_free[idx] = c_gl
                p.wait("vector", sem["ac"], sem["ac"].n)
                emit_mx_quant_core(p, sem, gc, slice(0, SWTC * OSL), 64, gq)
                nbo = OSL // 32
                for i in range(SWTC):
                    emit_mx_q8(p, sem, gq, slice(i * OSL, (i + 1) * OSL),
                               q8g[i], slice(osl * OSL, (osl + 1) * OSL))
                    p.op("vector", lambda E, i=i, osl=osl: E.tensor_copy(
                        sscg[i][:, osl * nbo:(osl + 1) * nbo],
                        gq["ssc"][:, i * nbo:(i + 1) * nbo]),
                        reads=(nm(gq["ssc"]),), writes=(nm(sscg[i]),))
                gq_done[osl] = p.op("vector", lambda E: E.drain(),
                                    inc=sem["dv"])
            f1_mm_last = mm_last[NOSL - 1]

            h2_free = f2_done_prev
            for i, tc in enumerate(tcs):
                if h2_free:
                    p.wait("tensor", sem["ac"], h2_free)
                stage = {}

                def emit_stage(oc, i=i):
                    wb2 = w2sl[oc % 2]
                    sw2 = dsem["w2%d" % (oc % 2)]
                    gate = w2_gate.get(oc % 2)
                    if gate is not None:
                        p.wait("sync", sem["pe"], gate)
                    c_w2 = p.dma("sync", wb2[:, :],
                                 bass.AP(io["w2t_s"], oc * 128 * DM,
                                         [[DM, 128], [1, DM]]), sw2)
                    qt = qgtm[oc % 2]
                    gate = tp_used.get(oc % 2)
                    if gate is not None:
                        p.wait("vector", sem["pe"], gate)
                    c_ex = emit_mx_expand(p, sem, q8g[i],
                                          slice(oc * 128, (oc + 1) * 128),
                                          sscg[i], slice(oc * 4, (oc + 1) * 4),
                                          4, qt, slice(0, 128), inc=sem["dv"])
                    p.wait("tensor", sem["dv"], c_ex)
                    t = ps_tp[oc % 2]
                    c_tp = p.op("tensor", lambda E, t=t, qt=qt: E.transpose(
                        t[:, 0:128], qt[:, 0:128], ident_b[:, :]),
                        inc=sem["pe"])
                    tp_used[oc % 2] = c_tp
                    p.wait("scalar", sem["pe"], c_tp)
                    qf = qgfm[oc % 3]
                    gate = h2mm_last.get(oc % 3)
                    if gate is not None:
                        p.wait("scalar", sem["pe"], gate)
                    c_ev = p.op("scalar", lambda E, qf=qf, t=t: E.copy(
                        qf[:, :], t[:, 0:128]), reads=(nm(t),),
                        writes=(nm(qf),), inc=sem["ac"])
                    stage[oc] = (c_ev, qf, wb2, sw2, c_w2)

                emit_stage(0)
                for oc in range(OC):
                    if oc + 1 < OC:
                        emit_stage(oc + 1)
                    c_ev, qf, wb2, sw2, c_w2 = stage.pop(oc)
                    p.wait("tensor", sem["ac"], c_ev)
                    p.wait("tensor", sw2, c_w2)
                    for ds in range(4):
                        p.op("tensor", lambda E, ds=ds, qf=qf, wb2=wb2, oc=oc:
                             E.matmul(ps_h2[:, ds * 512:(ds + 1) * 512],
                                      qf[:, :],
                                      wb2[:, ds * 512:(ds + 1) * 512],
                                      start=(oc == 0), stop=False))
                    c_h2m = p.op("tensor", lambda E: E.drain(), inc=sem["pe"])
                    h2mm_last[oc % 3] = c_h2m
                    w2_gate[oc % 2] = c_h2m
                for hh in range(4):
                    c_b = p.op("tensor", lambda E, hh=hh: E.matmul(
                        ps_h2[:, hh * 512:(hh + 1) * 512], ones1[:, :],
                        b2sl[:, hh * 512:(hh + 1) * 512],
                        start=False, stop=True), inc=sem["pe"])
                p.wait("scalar", sem["pe"], c_b)
                c_h2b = p.op("scalar", lambda E: E.copy(
                    h2b[:, :], ps_h2[:, :]), reads=(nm(ps_h2),),
                    writes=(nm(h2b),), inc=sem["ac"])
                f2_done_prev = c_h2b
                h2_free = c_h2b
                xb = xt[i % 2]
                sx = dsem["xt%d" % (i % 2)]
                for s_, v_ in xt_gate[i % 2]:
                    p.wait("sync", s_, v_)
                c_x2 = p.dma("sync", xb[:, :],
                             src.ap()[tc * 128:(tc + 1) * 128, :], sx)
                p.wait("vector", sx, c_x2)
                p.wait("vector", sem["ac"], c_h2b)
                xnm = nm(xb)
                p.op("vector", lambda E, tc=tc, xb=xb: E.tensor_scalar(
                    fb[:, :], xb[:, :], musig[:, 2 * tc:2 * tc + 1],
                    musig[:, 2 * tc + 1:2 * tc + 2], ALU.subtract, ALU.mult),
                    reads=(xnm, "musig"), writes=(nm(fb),))
                p.op("vector", lambda E: E.tensor_tensor(
                    fb[:, :], fb[:, :], h2b[:, :], ALU.add),
                    reads=(nm(fb), nm(h2b)), writes=(nm(fb),))
                c_xu = p.op("vector", lambda E, xb=xb: E.scalar_tensor_tensor(
                    xb[:, :], fb[:, :], 0.5, xb[:, :], ALU.mult, ALU.add),
                    reads=(nm(fb), xnm), writes=(xnm,), inc=sem["dv"])
                p.wait("sync", sem["dv"], c_xu)
                c_ws = p.dma("sync", dst.ap()[tc * 128:(tc + 1) * 128, :],
                             xb[:, :], sx)
                xt_gate[i % 2] = [(sx, c_ws)]

        for k in ("xt0", "xt1"):
            p.wait("sync", dsem[k], dsem[k].n)
        with nc.Block() as blk:
            p.flush_into_block(blk)



def _phase_attn(nc, sem, dsem, sbt, pst, io, pers):
    ident_b = pers["ident_b"]
    perm_b = pers["perm_b"]
    mask_b = pers["mask_b"]
    stat6, stat2, rsg = pers["stat6"], pers["stat2"], pers["rsg"]
    zsum, rz, negmax = pers["zsum"], pers["rz"], pers["negmax"]
    epsc, nhalf = pers["epsc"], pers["nhalf"]
    xscr = io["xscr"]
    p = Prog(nc)

    stA = contextlib.ExitStack()
    a_fm = [sbt(stA, "a_fm%d" % k, [128, S], BF16) for k in range(KC)]
    xbuf = [sbt(stA, "at_x%d" % i, [128, DM], F32) for i in range(2)]
    atile = sbt(stA, "at_t", [128, DM], BF16)
    wcol = [sbt(stA, "pj_w%d" % i, [128, KC * 128], BF16) for i in range(2)]
    vwb = sbt(stA, "pj_vw", [128, KC * 512], BF16)
    pout = sbt(stA, "pj_o", [128, S], BF16)
    vout = sbt(stA, "pj_v", [128, 512], BF16)
    ps_tp = [pst(stA, "at_tp%d" % i, [128, 1024], BF16) for i in range(2)]
    ps_p = pst(stA, "pj_ps", [128, 512], F32)
    ps_v = pst(stA, "pj_pv", [128, 512], F32)

    with stA:
        c_at_prev = 0
        xb_gate = [0, 0]
        for tc in range(NTC):
            xb = xbuf[tc % 2]
            sx = dsem["xt%d" % (tc % 2)]
            if xb_gate[tc % 2]:
                p.wait("sync", sem["dv"], xb_gate[tc % 2])
            c_x = p.dma("sync", xb[:, :],
                        xscr.ap()[tc * 128:(tc + 1) * 128, :], sx)
            xnm = nm(xb)
            p.wait("vector", sx, c_x)
            c_st = emit_ln_stats(p, sem, xb, stat6, stat2)
            c_rs = emit_rsig(p, sem, stat2, rsg, epsc, nhalf,
                             gate_dv=max(c_st, c_at_prev))
            p.wait("vector", sem["ac"], c_rs)
            c_at = p.op("vector", lambda E, xb=xb: E.tensor_scalar(
                atile[:, :], xb[:, :], stat2[:, 0:1], rsg[:, 0:1],
                ALU.subtract, ALU.mult), reads=(xnm, "stat2", "rsg"),
                writes=(nm(atile),), inc=sem["dv"])
            c_at_prev = c_at
            xb_gate[tc % 2] = c_at
            p.wait("tensor", sem["dv"], c_at)
            evs = {}
            for kc in range(KC):
                t = ps_tp[kc % 2]
                if kc >= 2:
                    p.wait("tensor", sem["ac"], evs[kc - 2])
                c_pe = p.op("tensor", lambda E, kc=kc, t=t: E.transpose(
                    t[:, 0:128], atile[:, kc * 128:(kc + 1) * 128],
                    ident_b[:, :]), inc=sem["pe"])
                p.wait("scalar", sem["pe"], c_pe)
                evs[kc] = p.op("scalar", lambda E, kc=kc, t=t, tc=tc: E.copy(
                    a_fm[kc][:, tc * 128:(tc + 1) * 128], t[:, 0:128]),
                    reads=(nm(t),), writes=(nm(a_fm[kc]),), inc=sem["ac"])
            p.wait("vector", sem["pe"], sem["pe"].n)
        c_afm = sem["ac"].n
        p.wait("tensor", sem["ac"], c_afm)

        proj_fin = {}
        spl_prev = 0
        for sidx in range(20):
            wb = wcol[sidx % 2]
            swc = dsem["wc%d" % (sidx % 2)]
            gate = proj_fin.get(sidx - 2)
            if gate:
                p.wait("gpsimd", sem["pe"], gate)
            if sidx < 16:
                wt, col0, spill, sp_i = io["wq"], sidx * 128, io["q_s"], sidx
                wcols = 2048
            else:
                wt, col0, spill, sp_i = (io["wk"], (sidx - 16) * 128,
                                         io["k_s"], sidx - 16)
                wcols = 512
            in_ap = bass.AP(wt, col0,
                            [[wcols, 128], [128 * wcols, KC], [1, 128]])
            out_ap = wb.ap().rearrange("p (kc c) -> p kc c", c=128)
            c_wl = p.dma("gpsimd", out_ap, in_ap, swc)
            p.wait("tensor", swc, c_wl)
            for ts in range(4):
                if spl_prev:
                    p.wait("scalar", dsem["spl"], spl_prev)
                for kc in range(KC):
                    p.op("tensor", lambda E, kc=kc, ts=ts, wb=wb: E.matmul(
                        ps_p[:, :], wb[:, kc * 128:(kc + 1) * 128],
                        a_fm[kc][:, ts * 512:(ts + 1) * 512],
                        start=(kc == 0), stop=(kc == KC - 1)))
                c_mm = p.op("tensor", lambda E: E.drain(), inc=sem["pe"])
                p.wait("scalar", sem["pe"], c_mm)
                c_ev = p.op("scalar", lambda E, ts=ts: E.copy(
                    pout[:, ts * 512:(ts + 1) * 512], ps_p[:, :]),
                    reads=(nm(ps_p),), writes=(nm(pout),), inc=sem["ac"])
                p.wait("tensor", sem["ac"], c_ev)
            proj_fin[sidx] = sem["pe"].n
            p.wait("gpsimd", sem["ac"], sem["ac"].n)
            spl_prev = p.dma("gpsimd",
                             bass.AP(spill, sp_i * 128 * S,
                                     [[S, 128], [1, S]]),
                             pout[:, :], dsem["spl"])
        in_ap = bass.AP(io["wv"], 0, [[512, 128], [128 * 512, KC], [1, 512]])
        c_vw = p.dma("gpsimd",
                     vwb.ap().rearrange("p (kc c) -> p kc c", c=512),
                     in_ap, dsem["vwl"])
        p.wait("tensor", dsem["vwl"], c_vw)
        vsp_prev = 0
        for tc in range(NTC):
            for kc in range(KC):
                p.op("tensor", lambda E, kc=kc, tc=tc: E.matmul(
                    ps_v[:, :], a_fm[kc][:, tc * 128:(tc + 1) * 128],
                    vwb[:, kc * 512:(kc + 1) * 512],
                    start=(kc == 0), stop=(kc == KC - 1)))
            c_mm = p.op("tensor", lambda E: E.drain(), inc=sem["pe"])
            p.wait("scalar", sem["pe"], c_mm)
            if vsp_prev:
                p.wait("scalar", dsem["vsp"], vsp_prev)
            c_ev = p.op("scalar", lambda E: E.copy(vout[:, :], ps_v[:, :]),
                        reads=(nm(ps_v),), writes=(nm(vout),), inc=sem["ac"])
            p.wait("tensor", sem["ac"], c_ev)
            p.wait("gpsimd", sem["ac"], c_ev)
            out_ap = bass.AP(io["v_s"], tc * 128 * 128,
                             [[128, 128], [NTC * 128 * 128, 4], [1, 128]])
            vsp_prev = p.dma("gpsimd", out_ap,
                             vout.ap().rearrange("p (g c) -> p g c", c=128),
                             dsem["vsp"])
        p.wait("sync", dsem["vsp"], dsem["vsp"].n)
        p.wait("sync", dsem["spl"], dsem["spl"].n)

    stC = contextlib.ExitStack()
    tabs = {name: sbt(stC, "tab_" + name, [128, S], F32)
            for name in ("cq", "sq", "ck", "sk")}
    kst = sbt(stC, "sc_k", [128, S], BF16)
    krot = sbt(stC, "sc_kr", [128, S], BF16)
    vst = sbt(stC, "sc_v", [128, NTC * 128], BF16)
    qst = [sbt(stC, "sc_q%d" % i, [128, S], BF16) for i in range(2)]
    qrot = sbt(stC, "sc_qr", [128, S], BF16)
    t1r = sbt(stC, "sc_t1", [128, 512], F32)
    t2r = sbt(stC, "sc_t2", [128, 512], F32)
    esb = sbt(stC, "sc_e", [128, S], BF16)
    et = [sbt(stC, "sc_et%d" % i, [128, 128], BF16) for i in range(3)]
    ao = sbt(stC, "sc_ao", [128, S], BF16)
    ps_sc = pst(stC, "sc_ps", [128, S], F32)
    ps_rp = pst(stC, "sc_rp", [128, 512], F32)
    ps_tp2 = [pst(stC, "sc_tp%d" % i, [128, 1024], BF16) for i in range(2)]
    ps_pv = pst(stC, "sc_pv", [128, 512], F32)

    def rope(src_t, dst_t, ctab, stab):
        for q4 in range(4):
            sl = slice(q4 * 512, (q4 + 1) * 512)
            c_pm = p.op("tensor", lambda E, sl=sl, src_t=src_t: E.matmul(
                ps_rp[:, :], perm_b[:, :], src_t[:, sl],
                start=True, stop=True), inc=sem["pe"])
            p.op("vector", lambda E, sl=sl, src_t=src_t, ctab=ctab:
                 E.tensor_tensor(t1r[:, :], src_t[:, sl], ctab[:, sl],
                                 ALU.mult),
                 reads=(nm(src_t), nm(ctab)), writes=(nm(t1r),))
            p.wait("vector", sem["pe"], c_pm)
            p.op("vector", lambda E, sl=sl, stab=stab:
                 E.scalar_tensor_tensor(
                     t2r[:, :], ps_rp[:, :], 0.0, stab[:, sl],
                     ALU.add, ALU.mult),
                 reads=(nm(ps_rp), nm(stab)), writes=(nm(t2r),))
            c_rt = p.op("vector", lambda E, sl=sl, dst_t=dst_t:
                        E.tensor_tensor(dst_t[:, sl], t1r[:, :], t2r[:, :],
                                        ALU.add),
                        reads=(nm(t1r), nm(t2r)), writes=(nm(dst_t),),
                        inc=sem["dv"])
            p.wait("tensor", sem["dv"], c_rt)
        return sem["dv"].n

    with stC:
        for name, srcd in (("cq", io["cqd"]), ("sq", io["sqd"]),
                           ("ck", io["ckd"]), ("sk", io["skd"])):
            p.dma("sync", tabs[name][:, :], srcd.ap(), dsem["blks"])
        p.wait("vector", dsem["blks"], dsem["blks"].n)
        c_exp_prev = 0
        c_nr_prev = 0
        c_pv_prev = 0
        gp_end = (0, 0)
        strip_end = {}
        strip_scores_end = 0
        aosp_prev = 0
        for gp in range(4):
            p.wait("sync", sem["pe"], gp_end[0])
            p.wait("sync", sem["dv"], gp_end[1])
            c_k = p.dma("sync", kst[:, :],
                        bass.AP(io["k_s"], gp * 128 * S, [[S, 128], [1, S]]),
                        dsem["kst"])
            in_ap = bass.AP(io["v_s"], gp * NTC * 128 * 128,
                            [[128, 128], [128 * 128, NTC], [1, 128]])
            c_v = p.dma("sync",
                        vst.ap().rearrange("p (t c) -> p t c", c=128),
                        in_ap, dsem["vst"])
            p.wait("vector", dsem["kst"], c_k)
            p.wait("tensor", dsem["kst"], c_k)
            p.wait("vector", sem["pe"], gp_end[0])
            rope(kst, krot, tabs["ck"], tabs["sk"])
            p.wait("tensor", dsem["vst"], c_v)
            for i in range(4):
                sidx = 4 * gp + i
                qb = qst[i % 2]
                sq = dsem["qs%d" % (i % 2)]
                g2 = strip_end.get(sidx - 2)
                if g2:
                    p.wait("sync", sem["pe"], g2[0])
                    p.wait("sync", sem["dv"], g2[1])
                c_q = p.dma("sync", qb[:, :],
                            bass.AP(io["q_s"], sidx * 128 * S,
                                    [[S, 128], [1, S]]), sq)
                p.wait("vector", sq, c_q)
                p.wait("tensor", sq, c_q)
                if sidx > 0:
                    p.wait("vector", sem["pe"], strip_scores_end)
                c_qrot = rope(qb, qrot, tabs["cq"], tabs["sq"])
                strip_end[sidx] = (sem["pe"].n, c_qrot)
                p.wait("tensor", sem["dv"], c_qrot)
                for hh in range(2):
                    rows = slice(hh * 64, (hh + 1) * 64)
                    for qtc in range(NTC):
                        L = (qtc + 1) * 128
                        if c_exp_prev:
                            p.wait("tensor", sem["ac"], c_exp_prev)
                        slabs = []
                        off = 0
                        while off < L:
                            n = min(512, L - off)
                            slabs.append((off, n))
                            off += n
                        ns = len(slabs)
                        for si, (off, n) in enumerate(slabs):
                            p.op("tensor", lambda E, off=off, n=n, rows=rows,
                                 qtc=qtc: E.matmul(
                                     ps_sc[:, off:off + n],
                                     qrot[rows, qtc * 128:(qtc + 1) * 128],
                                     krot[rows, off:off + n],
                                     start=True, stop=True))
                            c_sl = p.op("tensor", lambda E: E.drain(),
                                        inc=sem["pe"])
                            p.wait("vector", sem["pe"], c_sl)
                            if si == ns - 1:
                                p.op("vector", lambda E, qtc=qtc:
                                     E.tensor_tensor(
                                         ps_sc[:, qtc * 128:(qtc + 1) * 128],
                                         ps_sc[:, qtc * 128:(qtc + 1) * 128],
                                         mask_b[:, :], ALU.add),
                                     reads=(nm(ps_sc), "mask_b"),
                                     writes=(nm(ps_sc),))
                            p.op("vector", lambda E, si=si, off=off, n=n:
                                 E.tensor_reduce(
                                     stat6[:, si:si + 1],
                                     ps_sc[:, off:off + n], AX.X, ALU.max),
                                 reads=(nm(ps_sc),), writes=("stat6",))
                        c_mx = p.op("vector", lambda E, ns=ns:
                                    E.tensor_reduce(
                                        negmax[:, :], stat6[:, 0:ns], AX.X,
                                        ALU.max, negate=True),
                                    reads=("stat6",), writes=("negmax",),
                                    inc=sem["dv"])
                        p.wait("scalar", sem["dv"], c_mx)
                        if c_nr_prev:
                            p.wait("scalar", sem["dv"], c_nr_prev)
                        if c_pv_prev:
                            p.wait("scalar", sem["pe"], c_pv_prev)
                        c_ex = p.op("scalar", lambda E, L=L: E.activation(
                            esb[:, 0:L], ps_sc[:, 0:L], ACTF.Exp,
                            bias=negmax[:, 0:1], scale=1.0,
                            accum_out=zsum[:, :]),
                            reads=(nm(ps_sc), "negmax"),
                            writes=(nm(esb), "zsum"), inc=sem["ac"])
                        c_exp_prev = c_ex
                        p.wait("vector", sem["ac"], c_ex)
                        p.op("vector", lambda E: E.reciprocal(
                            rz[:, :], zsum[:, :]), reads=("zsum",),
                            writes=("rz",))
                        c_nr = p.op("vector", lambda E, L=L: E.tensor_scalar(
                            esb[:, 0:L], esb[:, 0:L], rz[:, 0:1], None,
                            ALU.mult), reads=(nm(esb), "rz"),
                            writes=(nm(esb),), inc=sem["dv"])
                        c_nr_prev = c_nr
                        p.wait("tensor", sem["dv"], c_nr)
                        pvst = {}

                        def emit_tp(j):
                            t = ps_tp2[j % 2]
                            c_tp = p.op("tensor", lambda E, j=j, t=t:
                                        E.transpose(
                                            t[:, 0:128],
                                            esb[:, j * 128:(j + 1) * 128],
                                            ident_b[:, :]), inc=sem["pe"])
                            p.wait("scalar", sem["pe"], c_tp)
                            eb = et[j % 3]
                            c_ev = p.op("scalar", lambda E, eb=eb, t=t:
                                        E.copy(eb[:, :], t[:, 0:128]),
                                        reads=(nm(t),), writes=(nm(eb),),
                                        inc=sem["ac"])
                            pvst[j] = (c_ev, eb)

                        emit_tp(0)
                        for j in range(qtc + 1):
                            if j + 1 <= qtc:
                                emit_tp(j + 1)
                            c_ev, eb = pvst.pop(j)
                            p.wait("tensor", sem["ac"], c_ev)
                            p.op("tensor", lambda E, j=j, eb=eb,
                                 qtc=qtc: E.matmul(
                                     ps_pv[:, 0:128],
                                     vst[:, j * 128:(j + 1) * 128],
                                     eb[:, :],
                                     start=(j == 0), stop=(j == qtc)))
                        c_pv = p.op("tensor", lambda E: E.drain(),
                                    inc=sem["pe"])
                        c_pv_prev = c_pv
                        p.wait("scalar", sem["pe"], c_pv)
                        c_ao = p.op("scalar", lambda E, hh=hh, qtc=qtc:
                                    E.copy(ao[hh * 64:(hh + 1) * 64,
                                              qtc * 128:(qtc + 1) * 128],
                                           ps_pv[hh * 64:(hh + 1) * 64,
                                                 0:128]),
                                    reads=(nm(ps_pv),), writes=(nm(ao),),
                                    inc=sem["ac"])
                        p.wait("tensor", sem["ac"], c_ao)
                strip_scores_end = sem["pe"].n
                p.wait("gpsimd", sem["ac"], sem["ac"].n)
                if aosp_prev:
                    p.wait("scalar", dsem["aosp"], aosp_prev)
                aosp_prev = p.dma("gpsimd",
                                  bass.AP(io["ao_s"], sidx * 128 * S,
                                          [[S, 128], [1, S]]), ao[:, :],
                                  dsem["aosp"])
            gp_end = (sem["pe"].n, sem["dv"].n)
        p.wait("sync", dsem["aosp"], dsem["aosp"].n)

    stD = contextlib.ExitStack()
    wob = [sbt(stD, "op_w%d" % s, [128, DM], BF16) for s in range(16)]
    aoc = [sbt(stD, "op_a%d" % i, [128, 16 * 128], BF16) for i in range(2)]
    xb2 = [sbt(stD, "op_x%d" % i, [128, DM], F32) for i in range(2)]
    ps_o = pst(stD, "op_ps", [128, DM], F32)
    with stD:
        for s in range(16):
            p.dma("gpsimd", wob[s][:, :],
                  bass.AP(io["wo"], s * 128 * DM, [[DM, 128], [1, DM]]),
                  dsem["blkp"])
        c_wo = dsem["blkp"].n
        p.wait("tensor", dsem["blkp"], c_wo)
        res_done = {}
        for tc in range(NTC):
            ab = aoc[tc % 2]
            xb = xb2[tc % 2]
            sao = dsem["ao%d" % (tc % 2)]
            sxb = dsem["xb%d" % (tc % 2)]
            gate = res_done.get(tc - 2)
            if gate:
                p.wait("sync", sxb, gate)
            in_ap = bass.AP(io["ao_s"], tc * 128,
                            [[S, 128], [128 * S, 16], [1, 128]])
            c_ao = p.dma("sync",
                         ab.ap().rearrange("p (s c) -> p s c", c=128),
                         in_ap, sao)
            c_x = p.dma("sync", xb[:, :],
                        xscr.ap()[tc * 128:(tc + 1) * 128, :], sxb)
            p.wait("tensor", sao, c_ao)
            for s in range(16):
                for ds in range(4):
                    p.op("tensor", lambda E, s=s, ds=ds, ab=ab: E.matmul(
                        ps_o[:, ds * 512:(ds + 1) * 512],
                        ab[:, s * 128:(s + 1) * 128],
                        wob[s][:, ds * 512:(ds + 1) * 512],
                        start=(s == 0), stop=(s == 15)))
            c_om = p.op("tensor", lambda E: E.drain(), inc=sem["pe"])
            p.wait("vector", sem["pe"], c_om)
            p.wait("vector", sxb, c_x)
            xnm = nm(xb)
            c_xr = p.op("vector", lambda E, xb=xb: E.tensor_tensor(
                xb[:, :], xb[:, :], ps_o[:, :], ALU.add),
                reads=(xnm, nm(ps_o)), writes=(xnm,), inc=sem["dv"])
            p.wait("tensor", sem["dv"], c_xr)
            p.wait("sync", sem["dv"], c_xr)
            c_st = p.dma("sync", xscr.ap()[tc * 128:(tc + 1) * 128, :],
                         xb[:, :], sxb)
            res_done[tc] = c_st
        for k in ("xb0", "xb1"):
            p.wait("sync", dsem[k], dsem[k].n)
        with nc.Block() as blk:
            p.flush_into_block(blk)



def kernel(L0_attn_x, L0_attn_Wq, L0_attn_Wk, L0_attn_Wv, L0_attn_Wo,
           L0_ffn_ln_w, L0_ffn_ln_b, L0_ffn_W1, L0_ffn_b1, L0_ffn_W2,
           L0_ffn_b2):
    x = np.ascontiguousarray(np.asarray(L0_attn_x, dtype=np.float32))
    consts = _host_consts()
    perm = _head_perm()
    wq_p = np.ascontiguousarray(np.asarray(L0_attn_Wq, np.float32)[:, perm])
    wo_p = np.ascontiguousarray(np.asarray(L0_attn_Wo, np.float32)[perm, :])
    wk_ = np.ascontiguousarray(np.asarray(L0_attn_Wk, np.float32))
    wv_ = np.ascontiguousarray(np.asarray(L0_attn_Wv, np.float32))
    w1_ = np.ascontiguousarray(np.asarray(L0_ffn_W1, np.float32))
    w2_ = np.ascontiguousarray(np.asarray(L0_ffn_W2, np.float32))
    lnw = np.ascontiguousarray(np.broadcast_to(
        np.asarray(L0_ffn_ln_w, np.float32)[None, :], (128, DM)))
    lnb = np.ascontiguousarray(np.broadcast_to(
        np.asarray(L0_ffn_ln_b, np.float32)[None, :], (128, DM)))
    b1v = np.ascontiguousarray(np.asarray(L0_ffn_b1, np.float32)[None, :])
    b2v = np.ascontiguousarray(np.asarray(L0_ffn_b2, np.float32)[None, :])

    nc = build_nc()
    base = {"wq": wq_p, "wk": wk_, "wv": wv_, "wo": wo_p, "w1": w1_,
            "w2": w2_, "lnww": lnw, "lnbb": lnb, "b1r": b1v, "b2r": b2v,
            "cqd": consts["cq"], "sqd": consts["sq"], "ckd": consts["ck"],
            "skd": consts["sk"], "permd": consts["perm"],
            "identd": consts["ident"], "maskd": consts["mask"]}
    in_maps = [dict(base, xin=x[c]) for c in range(8)]
    res = run_bass_kernel_spmd(nc, in_maps, core_ids=list(range(8)))
    return np.stack([np.asarray(r["yout"], dtype=np.float32)
                     for r in res.results], axis=0)


# version 08
# speedup vs baseline: 1.6235x; 1.3554x over previous
import contextlib
import os
import numpy as np
import concourse.bass as bass
import concourse.mybir as mybir
from concourse.bass_utils import run_bass_kernel_spmd

try:
    import ml_dtypes
    _BF16_NP = ml_dtypes.bfloat16
except ImportError:
    _BF16_NP = None

F32 = mybir.dt.float32
BF16 = mybir.dt.bfloat16
I32 = mybir.dt.int32
I8 = mybir.dt.int8
ALU = mybir.AluOpType
AX = mybir.AxisListType
ACTF = mybir.ActivationFunctionType

S = 2048
DM = 2048
HID = 8192
NTC = S // 128
KC = DM // 128
OC = HID // 128
TBLK = 1024
NSW = S // TBLK
SWTC = TBLK // 128
OSL = 256
NOSL = HID // OSL
HD = 64
EPS = 1e-5
NEG = -1.0e30
MAGIC = 12582912.0
CLIPC = float(np.nextafter(np.float32(127.5), np.float32(0)))

ENGS = ("sync", "gpsimd", "vector", "scalar", "tensor")
SIM_DRAINS = True
PHASES = ("wprep", "fa", "attn", "fb")
DBG_SCRATCH = False


class Sem:
    def __init__(self, h):
        self.h = h
        self.n = 0

    def inc(self, k=1):
        self.n += k
        return self.n


class Prog:
    def __init__(self, nc):
        self.nc = nc
        self.q = {e: [] for e in ENGS}
        self.dirty = {e: set() for e in ENGS}

    def wait(self, eng, sem, val):
        if val > 0:
            self.q[eng].append(lambda E, s=sem, v=val: E.wait_ge(s.h, v))

    def op(self, eng, build, reads=(), writes=(), inc=None, inc_n=1):
        names = set(reads) | set(writes)
        if (SIM_DRAINS and eng in ("vector", "scalar", "gpsimd")
                and (names & self.dirty[eng])):
            self.q[eng].append(lambda E: E.drain())
            self.dirty[eng] = set()
        self.dirty[eng] |= set(writes)
        if inc is not None:
            def fn(E, b=build, s=inc, n=inc_n):
                b(E).then_inc(s.h, n)
            self.q[eng].append(fn)
            return inc.inc(inc_n)
        self.q[eng].append(build)
        return None

    def dma(self, eng, out_ap, in_ap, sem, **kw):
        def fn(E, o=out_ap, i=in_ap, s=sem, kw=kw):
            E.dma_start(o, i, **kw).then_inc(s.h, 16)
        self.q[eng].append(fn)
        return sem.inc(16)

    def flush_into_block(self, block):
        q = self.q

        @block.sync
        def _(e):
            for fn in q["sync"]:
                fn(e)

        @block.gpsimd
        def _(e):
            for fn in q["gpsimd"]:
                fn(e)

        @block.vector
        def _(e):
            for fn in q["vector"]:
                fn(e)

        @block.scalar
        def _(e):
            for fn in q["scalar"]:
                fn(e)

        @block.tensor
        def _(e):
            for fn in q["tensor"]:
                fn(e)


def nm(t):
    return t.ap().name if callable(getattr(t, "ap", None)) else t.name


def _host_consts():
    half = HD // 2
    inv = (1.0 / np.power(np.float32(10000.0),
                          np.arange(half, dtype=np.float32) / np.float32(half))
           ).astype(np.float32)
    pos = np.arange(S, dtype=np.float32)
    fr = np.outer(pos, inv).astype(np.float32)
    cos = np.cos(fr).astype(np.float32)
    sin = np.sin(fr).astype(np.float32)
    j_of_r = (np.arange(128) % HD) // 2
    C = cos[:, j_of_r].T.astype(np.float32)
    Sg = sin[:, j_of_r].T.astype(np.float32)
    sign = np.where(np.arange(128) % 2 == 0, -1.0, 1.0).astype(np.float32)
    Stab = (Sg * sign[:, None]).astype(np.float32)
    swap = np.arange(128)
    swap = swap + np.where(swap % 2 == 0, 1, -1)
    perm = np.zeros((128, 128), dtype=np.float32)
    perm[swap, np.arange(128)] = 1.0
    ident = np.eye(128, dtype=np.float32)
    ii = np.arange(128)
    mask = np.where(ii[None, :] <= ii[:, None], 0.0, NEG).astype(np.float32)
    if _BF16_NP is not None:
        perm = perm.astype(_BF16_NP)
        ident = ident.astype(_BF16_NP)
    return dict(cq=(C * np.float32(0.125)).astype(np.float32),
                sq=(Stab * np.float32(0.125)).astype(np.float32),
                ck=C, sk=Stab, perm=perm, ident=ident, mask=mask)


def _head_perm():
    order = []
    for gp in range(4):
        for i in range(4):
            for h in (8 * gp + i, 8 * gp + 4 + i):
                order.extend(range(h * HD, (h + 1) * HD))
    return np.asarray(order, dtype=np.int64)



def emit_mx_quant_core(p, sem, src_t, src_sl, nblk, bufs):
    am, rsc, ssc, y, r = (bufs["am"], bufs["rsc"], bufs["ssc"], bufs["y"],
                          bufs["r"])
    n = nblk * 32
    src3 = src_t.ap()[:, src_sl].rearrange("p (nb k) -> p nb k", k=32)
    p.op("vector", lambda E: E.tensor_reduce(
        am[:, 0:nblk], src3, AX.X, ALU.max, apply_absolute_value=True),
        reads=(nm(src_t),), writes=(nm(am),))
    p.op("vector", lambda E: E.tensor_single_scalar(
        rsc[:, 0:nblk], am.ap()[:, 0:nblk].bitcast(I32), 23,
        ALU.logical_shift_right), reads=(nm(am),), writes=(nm(rsc),))
    p.op("vector", lambda E: E.tensor_scalar_max(
        rsc[:, 0:nblk], rsc[:, 0:nblk], 64),
        reads=(nm(rsc),), writes=(nm(rsc),))
    p.op("vector", lambda E: E.tensor_scalar(
        ssc[:, 0:nblk], rsc[:, 0:nblk], -6, 8388608, ALU.add, ALU.mult),
        reads=(nm(rsc),), writes=(nm(ssc),))
    p.op("vector", lambda E: E.tensor_scalar(
        rsc[:, 0:nblk], rsc[:, 0:nblk], -260, -8388608, ALU.add, ALU.mult),
        reads=(nm(rsc),), writes=(nm(rsc),))
    rb = rsc.ap()[:, 0:nblk].bitcast(F32).unsqueeze(2).broadcast_to(
        [128, nblk, 32])
    y3 = y.ap()[:, 0:n].rearrange("p (nb k) -> p nb k", k=32)
    p.op("vector", lambda E: E.tensor_tensor(y3, src3, rb, ALU.mult),
         reads=(nm(src_t), nm(rsc)), writes=(nm(y),))
    return p.op("vector", lambda E: E.tensor_scalar(
        r[:, 0:n], y[:, 0:n], MAGIC, MAGIC, ALU.add, ALU.subtract),
        reads=(nm(y),), writes=(nm(r),), inc=sem["dv"])


def emit_mx_q8(p, sem, bufs, r_sl, q8_t, q8_sl, inc=None):
    r = bufs["r"]
    return p.op("vector", lambda E: E.tensor_scalar(
        q8_t.ap()[:, q8_sl], r[:, r_sl], 127.0, None, ALU.min),
        reads=(nm(r),), writes=(nm(q8_t),), inc=inc)


def emit_mx_expand(p, sem, q8_t, q8_sl, ssc_t, ssc_sl, nblk, out_t, out_sl,
                   inc=None):
    q83 = q8_t.ap()[:, q8_sl].rearrange("p (nb k) -> p nb k", k=32)
    sc3 = ssc_t.ap()[:, ssc_sl].bitcast(F32).unsqueeze(2).broadcast_to(
        [128, nblk, 32])
    o3 = out_t.ap()[:, out_sl].rearrange("p (nb k) -> p nb k", k=32)
    return p.op("vector", lambda E: E.scalar_tensor_tensor(
        o3, q83, 0.0, sc3, ALU.add, ALU.mult),
        reads=(nm(q8_t), nm(ssc_t)), writes=(nm(out_t),), inc=inc)


def emit_ln_stats(p, sem, src_t, stat6, stat2):
    for j in range(4):
        p.op("vector", lambda E, j=j, src_t=src_t: E.bn_stats(
            stat6[:, 6 * j:6 * (j + 1)], src_t[:, 512 * j:512 * (j + 1)]),
            reads=(nm(src_t),), writes=("stat6",))
    return p.op("vector", lambda E: E.bn_aggr(stat2[:, :], stat6[:, :]),
                reads=("stat6",), writes=("stat2",), inc=sem["dv"])


def emit_rsig(p, sem, stat2, rsg, epsc, nhalf, gate_dv=0):
    if gate_dv:
        p.wait("scalar", sem["dv"], gate_dv)
    p.op("scalar", lambda E: E.activation(
        rsg[:, :], stat2[:, 1:2], ACTF.Ln, bias=epsc[:, 0:1], scale=1.0),
        reads=("stat2",), writes=("rsg",))
    return p.op("scalar", lambda E: E.activation(
        rsg[:, :], rsg[:, :], ACTF.Exp, bias=0.0, scale=nhalf[:, 0:1]),
        reads=("rsg",), writes=("rsg",), inc=sem["ac"])



def build_nc(phases=None):
    phases = PHASES if phases is None else phases
    scratch_kind = "ExternalOutput" if DBG_SCRATCH else "Internal"
    nc = bass.Bass("TRN2", target_bir_lowering=False)

    xin = nc.dram_tensor("xin", [S, DM], F32, kind="ExternalInput")
    wq = nc.dram_tensor("wq", [DM, DM], F32, kind="ExternalInput")
    wk = nc.dram_tensor("wk", [DM, 512], F32, kind="ExternalInput")
    wv = nc.dram_tensor("wv", [DM, 512], F32, kind="ExternalInput")
    wo = nc.dram_tensor("wo", [DM, DM], F32, kind="ExternalInput")
    w1 = nc.dram_tensor("w1", [HID, DM], F32, kind="ExternalInput")
    w2 = nc.dram_tensor("w2", [DM, HID], F32, kind="ExternalInput")
    lnww = nc.dram_tensor("lnww", [128, DM], F32, kind="ExternalInput")
    lnbb = nc.dram_tensor("lnbb", [128, DM], F32, kind="ExternalInput")
    b1r = nc.dram_tensor("b1r", [1, HID], F32, kind="ExternalInput")
    b2r = nc.dram_tensor("b2r", [1, DM], F32, kind="ExternalInput")
    cqd = nc.dram_tensor("cqd", [128, S], F32, kind="ExternalInput")
    sqd = nc.dram_tensor("sqd", [128, S], F32, kind="ExternalInput")
    ckd = nc.dram_tensor("ckd", [128, S], F32, kind="ExternalInput")
    skd = nc.dram_tensor("skd", [128, S], F32, kind="ExternalInput")
    permd = nc.dram_tensor("permd", [128, 128], BF16, kind="ExternalInput")
    identd = nc.dram_tensor("identd", [128, 128], BF16, kind="ExternalInput")
    maskd = nc.dram_tensor("maskd", [128, 128], F32, kind="ExternalInput")
    yout = nc.dram_tensor("yout", [S, DM], F32, kind="ExternalOutput")

    xscr = nc.dram_tensor("xscr", [S, DM], F32, kind=scratch_kind)
    w1t_s = nc.dram_tensor("w1t_s", [KC, 128, HID], BF16, kind=scratch_kind)
    w2t_s = nc.dram_tensor("w2t_s", [OC, 128, DM], BF16, kind=scratch_kind)
    q_s = nc.dram_tensor("q_s", [16, 128, S], BF16, kind=scratch_kind)
    k_s = nc.dram_tensor("k_s", [4, 128, S], BF16, kind=scratch_kind)
    v_s = nc.dram_tensor("v_s", [4, NTC, 128, 128], BF16, kind=scratch_kind)
    ao_s = nc.dram_tensor("ao_s", [16, 128, S], BF16, kind=scratch_kind)

    st = contextlib.ExitStack()
    sem = {k: Sem(st.enter_context(nc.semaphore("sem_" + k)))
           for k in ("dv", "ac", "pe")}
    dnames = ("blks", "blkp", "xt0", "xt1", "w10", "w11", "b10", "b11",
              "w20", "w21", "wc0", "wc1", "vwl", "qs0", "qs1", "kst", "vst",
              "spl", "vsp", "aosp", "ao0", "ao1", "xb0", "xb1", "wld0",
              "wld1", "wst0", "wst1")
    dsem = {k: Sem(st.enter_context(nc.semaphore("dma_" + k)))
            for k in dnames}
    dq = {"blkp": "gpsimd", "spl": "gpsimd", "vsp": "gpsimd",
          "aosp": "gpsimd", "wst0": "gpsimd", "wst1": "gpsimd",
          "vwl": "gpsimd", "wc0": "gpsimd", "wc1": "gpsimd",
          "b10": "gpsimd", "b11": "gpsimd"}

    def sbt(stack, name, shape, dtype):
        return stack.enter_context(nc.sbuf_tensor(name, shape, dtype))

    def pst(stack, name, shape, dtype):
        return stack.enter_context(nc.psum_tensor(name, shape, dtype))

    ident_b = sbt(st, "ident_b", [128, 128], BF16)
    perm_b = sbt(st, "perm_b", [128, 128], BF16)
    mask_b = sbt(st, "mask_b", [128, 128], F32)
    ones1 = sbt(st, "ones1", [1, 128], BF16)
    stat6 = sbt(st, "stat6", [128, 24], F32)
    stat2 = sbt(st, "stat2", [128, 2], F32)
    rsg = sbt(st, "rsg", [128, 1], F32)
    musig = sbt(st, "musig", [128, 2 * NTC], F32)
    zsum = sbt(st, "zsum", [128, 1], F32)
    rz = sbt(st, "rz", [128, 1], F32)
    negmax = sbt(st, "negmax", [128, 2], F32)
    epsc = sbt(st, "epsc", [128, 1], F32)
    nhalf = sbt(st, "nhalf", [128, 1], F32)

    io = dict(xin=xin, wq=wq, wk=wk, wv=wv, wo=wo, w1=w1, w2=w2,
              lnww=lnww, lnbb=lnbb, b1r=b1r, b2r=b2r, cqd=cqd, sqd=sqd,
              ckd=ckd, skd=skd, permd=permd, identd=identd, maskd=maskd,
              yout=yout, xscr=xscr, w1t_s=w1t_s, w2t_s=w2t_s, q_s=q_s,
              k_s=k_s, v_s=v_s, ao_s=ao_s)
    pers = dict(ident_b=ident_b, perm_b=perm_b, mask_b=mask_b, ones1=ones1,
                stat6=stat6, stat2=stat2, rsg=rsg, musig=musig, zsum=zsum,
                rz=rz, negmax=negmax, epsc=epsc, nhalf=nhalf)

    with st:
        p = Prog(nc)
        for dst, src in ((ident_b, identd), (perm_b, permd), (mask_b, maskd)):
            p.dma("sync", dst[:, :], src.ap(), dsem["blks"])
        p.wait("vector", dsem["blks"], dsem["blks"].n)
        p.op("vector", lambda E: E.memset(ones1[:, :], 1.0), writes=("ones1",))
        p.op("vector", lambda E: E.memset(epsc[:, :], EPS), writes=("epsc",))
        p.op("vector", lambda E: E.memset(nhalf[:, :], -0.5),
             writes=("nhalf",), inc=sem["dv"])
        p.wait("tensor", sem["dv"], sem["dv"].n)
        p.wait("scalar", sem["dv"], sem["dv"].n)
        with nc.Block() as blk:
            p.flush_into_block(blk)

        if "wprep" in phases:
            _phase_wprep(nc, sem, dsem, sbt, pst, io, pers)
        if "fa" in phases:
            _phase_ffn(nc, sem, dsem, sbt, pst, io, pers, xin, xscr, "fa")
        else:
            p = Prog(nc)
            for tc in range(NTC):
                p.dma("sync", xscr.ap()[tc * 128:(tc + 1) * 128, :],
                      xin.ap()[tc * 128:(tc + 1) * 128, :], dsem["blks"])
            p.wait("sync", dsem["blks"], dsem["blks"].n)
            with nc.Block() as blk:
                p.flush_into_block(blk)
        if "attn" in phases:
            _phase_attn(nc, sem, dsem, sbt, pst, io, pers)
        if "fb" in phases:
            _phase_ffn(nc, sem, dsem, sbt, pst, io, pers, xscr, yout, "fb")
        else:
            p = Prog(nc)
            for tc in range(NTC):
                p.dma("sync", yout.ap()[tc * 128:(tc + 1) * 128, :],
                      xscr.ap()[tc * 128:(tc + 1) * 128, :], dsem["blks"])
            p.wait("sync", dsem["blks"], dsem["blks"].n)
            with nc.Block() as blk:
                p.flush_into_block(blk)

        p = Prog(nc)
        for k in dnames:
            p.wait(dq.get(k, "sync"), dsem[k], dsem[k].n)
        with nc.Block() as blk:
            p.flush_into_block(blk)

    nc.finalize()
    return nc



def _phase_wprep(nc, sem, dsem, sbt, pst, io, pers):
    st = contextlib.ExitStack()
    wld = [sbt(st, "wp_ld%d" % i, [128, HID], F32) for i in range(2)]
    qbufs = dict(am=sbt(st, "wp_am", [128, 256], F32),
                 rsc=sbt(st, "wp_rsc", [128, 256], I32),
                 ssc=sbt(st, "wp_ssc", [128, 256], I32),
                 y=sbt(st, "wp_y", [128, HID], F32),
                 r=sbt(st, "wp_r", [128, HID], F32))
    q8 = sbt(st, "wp_q8", [128, HID], I8)
    qnat = sbt(st, "wp_qn", [128, HID], BF16)
    wfm = [sbt(st, "wp_fm%d" % i, [128, HID], BF16) for i in range(2)]
    tp = [pst(st, "wp_tp%d" % i, [128, 1024], BF16) for i in range(2)]
    ident_b = pers["ident_b"]

    p = Prog(nc)
    q_done = {}
    stg = {}
    specs = []
    for rc in range(HID // 128):
        specs.append((io["w1"], io["w1t_s"], rc, DM, 4, HID))
    for rc in range(DM // 128):
        specs.append((io["w2"], io["w2t_s"], rc, HID, 8, DM))

    with st:
        for g, (wt, scr, rc, ncols, nsplit, scr_cols) in enumerate(specs):
            buf = wld[g % 2]
            fmb = wfm[g % 2]
            sld = dsem["wld%d" % (g % 2)]
            sst = dsem["wst%d" % (g % 2)]
            if g >= 2:
                p.wait("sync", sem["dv"], q_done[g - 2])
            c_ld = p.dma("sync", buf[:, 0:ncols],
                         wt.ap()[rc * 128:(rc + 1) * 128, :], sld)
            p.wait("vector", sld, c_ld)
            nblk = ncols // 32
            q_done[g] = emit_mx_quant_core(p, sem, buf, slice(0, ncols), nblk,
                                           qbufs)
            emit_mx_q8(p, sem, qbufs, slice(0, ncols), q8, slice(0, ncols))
            c_qn = emit_mx_expand(p, sem, q8, slice(0, ncols), qbufs["ssc"],
                                  slice(0, nblk), nblk, qnat, slice(0, ncols),
                                  inc=sem["dv"])
            p.wait("tensor", sem["dv"], c_qn)
            if g >= 2:
                p.wait("scalar", sst, stg[g - 2])
            ntp = ncols // 128
            evs = {}
            for g4 in range(ntp // 4):
                t = tp[g4 % 2]
                if g4 >= 2:
                    p.wait("tensor", sem["ac"], evs[g4 - 2])
                c_pe = 0
                for u in range(4):
                    j = g4 * 4 + u
                    c_pe = p.op("tensor", lambda E, j=j, t=t, u=u:
                                E.transpose(t[:, u * 128:(u + 1) * 128],
                                            qnat[:, j * 128:(j + 1) * 128],
                                            ident_b[:, :]), inc=sem["pe"])
                p.wait("scalar", sem["pe"], c_pe)
                evs[g4] = p.op("scalar", lambda E, g4=g4, t=t, fmb=fmb:
                               E.copy(fmb[:, g4 * 512:(g4 + 1) * 512],
                                      t[:, 0:512]),
                               reads=(nm(t),), writes=(nm(fmb),),
                               inc=sem["ac"])
            p.wait("vector", sem["pe"], sem["pe"].n)
            p.wait("gpsimd", sem["ac"], sem["ac"].n)
            ngrp = ntp // nsplit
            c_st = 0
            for sp_i in range(nsplit):
                out_ap = bass.AP(
                    scr, sp_i * ngrp * 128 * scr_cols + rc * 128,
                    [[scr_cols, 128], [128 * scr_cols, ngrp], [1, 128]])
                in_ap = fmb.ap().rearrange("p (c o) -> p c o", o=128)[
                    :, sp_i * ngrp:(sp_i + 1) * ngrp, :]
                c_st = p.dma("gpsimd", out_ap, in_ap, sst)
            stg[g] = c_st
        p.wait("sync", dsem["wst0"], dsem["wst0"].n)
        p.wait("sync", dsem["wst1"], dsem["wst1"].n)
        with nc.Block() as blk:
            p.flush_into_block(blk)



def _phase_ffn(nc, sem, dsem, sbt, pst, io, pers, src, dst, tag):
    st = contextlib.ExitStack()
    xt = [sbt(st, tag + "_x%d" % i, [128, DM], F32) for i in range(2)]
    lnw_t = sbt(st, tag + "_lnw", [128, DM], F32)
    lnb_t = sbt(st, tag + "_lnb", [128, DM], F32)
    qa_fm = sbt(st, tag + "_qa", [128, KC * TBLK], BF16)
    q8g = [sbt(st, tag + "_q8_%d" % i, [128, HID], I8) for i in range(SWTC)]
    sscg = [sbt(st, tag + "_sg%d" % i, [128, HID // 32], I32)
            for i in range(SWTC)]
    xbb = sbt(st, tag + "_xbb", [128, DM], BF16)
    ntile = sbt(st, tag + "_n", [128, DM], BF16)
    qbufs = dict(am=sbt(st, tag + "_am", [128, 64], F32),
                 rsc=sbt(st, tag + "_rsc", [128, 64], I32),
                 ssc=sbt(st, tag + "_ssc", [128, 64], I32),
                 y=sbt(st, tag + "_y", [128, DM], BF16),
                 r=sbt(st, tag + "_r", [128, DM], F32))
    t1 = qbufs["r"]
    q8a = sbt(st, tag + "_q8a", [128, DM], I8)
    w1sl = [sbt(st, tag + "_w1s%d" % i, [128, KC * OSL], BF16)
            for i in range(2)]
    b1sl = [sbt(st, tag + "_b1s%d" % i, [1, OSL], BF16) for i in range(2)]
    b2sl = sbt(st, tag + "_b2s", [1, DM], BF16)
    gcat = [sbt(st, tag + "_g%d" % i, [128, SWTC * OSL], BF16)
            for i in range(2)]
    h1b = [sbt(st, tag + "_h1b%d" % i, [128, OSL], BF16) for i in range(4)]
    gq = qbufs
    w2sl = [sbt(st, tag + "_w2s%d" % i, [128, DM], BF16) for i in range(2)]
    qgtm = [sbt(st, tag + "_qgtm%d" % i, [128, 128], BF16) for i in range(2)]
    h2b = sbt(st, tag + "_h2b", [128, DM], BF16)
    fb = xbb

    psA = pst(st, tag + "_psA", [128, DM], F32)
    psB = pst(st, tag + "_psB", [128, DM], F32)
    ps_h1 = [psB.ap()[:, 0:512], psB.ap()[:, 512:1024]]
    ps_tp = [psB.ap()[:, 1024:1536].bitcast(BF16),
             psB.ap()[:, 1536:2048].bitcast(BF16)]
    ps_h2 = psA

    ident_b = pers["ident_b"]
    ones1 = pers["ones1"]
    stat6, stat2, rsg, musig = (pers["stat6"], pers["stat2"], pers["rsg"],
                                pers["musig"])
    epsc, nhalf = pers["epsc"], pers["nhalf"]
    p = Prog(nc)

    with st:
        c_lnw = p.dma("sync", lnw_t[:, :], io["lnww"].ap(), dsem["blks"])
        c_lnb = p.dma("sync", lnb_t[:, :], io["lnbb"].ap(), dsem["blks"])
        c_b2 = p.dma("gpsimd", b2sl[:, :], io["b2r"].ap(), dsem["blkp"])
        p.wait("vector", dsem["blks"], c_lnb)
        p.wait("tensor", dsem["blkp"], c_b2)

        xt_gate = [[], []]
        rsg_gate = 0
        f1_mm_last = 0
        f2_pe_last = 0
        f2_done_prev = 0
        tp_used = {}
        w2_gate = {}

        for sw in range(NSW):
            tcs = [sw * SWTC + i for i in range(SWTC)]
            for i, tc in enumerate(tcs):
                xb = xt[i % 2]
                sx = dsem["xt%d" % (i % 2)]
                for s_, v_ in xt_gate[i % 2]:
                    p.wait("sync", s_, v_)
                c_x = p.dma("sync", xb[:, :],
                            src.ap()[tc * 128:(tc + 1) * 128, :], sx)
                xnm = nm(xb)
                p.wait("vector", sx, c_x)
                c_st = emit_ln_stats(p, sem, xb, stat6, stat2)
                c_rs = emit_rsig(p, sem, stat2, rsg, epsc, nhalf,
                                 gate_dv=max(c_st, rsg_gate))
                p.wait("vector", sem["ac"], c_rs)
                p.op("vector", lambda E, tc=tc: E.tensor_copy(
                    musig[:, 2 * tc:2 * tc + 1], stat2[:, 0:1]),
                    reads=("stat2",), writes=("musig",))
                p.op("vector", lambda E, tc=tc: E.tensor_copy(
                    musig[:, 2 * tc + 1:2 * tc + 2], rsg[:, :]),
                    reads=("rsg",), writes=("musig",))
                c_xbb = p.op("vector", lambda E, xb=xb, tc=tc: E.tensor_scalar(
                    xbb[:, :], xb[:, :], musig[:, 2 * tc:2 * tc + 1],
                    musig[:, 2 * tc + 1:2 * tc + 2], ALU.subtract, ALU.mult),
                    reads=(xnm, "musig"), writes=(nm(xbb),), inc=sem["dv"])
                xt_gate[i % 2] = [(sem["dv"], c_xbb)]
                c_st2 = emit_ln_stats(p, sem, xbb, stat6, stat2)
                c_rs2 = emit_rsig(p, sem, stat2, rsg, epsc, nhalf,
                                  gate_dv=c_st2)
                p.wait("vector", sem["ac"], c_rs2)
                rsg_gate = p.op("vector", lambda E: E.tensor_scalar(
                    t1[:, :], xbb[:, :], stat2[:, 0:1], rsg[:, 0:1],
                    ALU.subtract, ALU.mult),
                    reads=(nm(xbb), "stat2", "rsg"), writes=(nm(t1),),
                    inc=sem["dv"])
                p.op("vector", lambda E: E.tensor_tensor(
                    t1[:, :], t1[:, :], lnw_t[:, :], ALU.mult),
                    reads=(nm(t1), nm(lnw_t)), writes=(nm(t1),))
                p.op("vector", lambda E: E.tensor_tensor(
                    ntile[:, :], t1[:, :], lnb_t[:, :], ALU.add),
                    reads=(nm(t1), nm(lnb_t)), writes=(nm(ntile),))
                emit_mx_quant_core(p, sem, ntile, slice(0, DM), 64, qbufs)
                emit_mx_q8(p, sem, qbufs, slice(0, DM), q8a, slice(0, DM))
                c_qa = emit_mx_expand(p, sem, q8a, slice(0, DM),
                                      qbufs["ssc"], slice(0, 64), 64,
                                      ntile, slice(0, DM), inc=sem["dv"])
                p.wait("tensor", sem["dv"], c_qa)
                if i == 0 and f2_pe_last:
                    p.wait("scalar", sem["pe"], f2_pe_last)
                    p.wait("tensor", sem["ac"], f2_done_prev)
                evs = {}
                for g4 in range(KC // 4):
                    t = ps_tp[g4 % 2]
                    if g4 >= 2:
                        p.wait("tensor", sem["ac"], evs[g4 - 2])
                    c_pe = 0
                    for u in range(4):
                        kc = g4 * 4 + u
                        c_pe = p.op("tensor", lambda E, kc=kc, t=t, u=u:
                                    E.transpose(
                                        t[:, u * 128:(u + 1) * 128],
                                        ntile[:, kc * 128:(kc + 1) * 128],
                                        ident_b[:, :]), inc=sem["pe"])
                    p.wait("scalar", sem["pe"], c_pe)
                    out3 = bass.AP(
                        qa_fm, g4 * 4 * TBLK + i * 128,
                        [[KC * TBLK, 128], [TBLK, 4], [1, 128]])
                    in3 = t[:, 0:512].rearrange("p (f c) -> p f c", c=128)
                    evs[g4] = p.op("scalar", lambda E, out3=out3, in3=in3:
                                   E.copy(out3, in3),
                                   reads=(nm(t),), writes=(nm(qa_fm),),
                                   inc=sem["ac"])
                p.wait("vector", sem["pe"], sem["pe"].n)
            c_qa_all = sem["ac"].n

            p.wait("tensor", sem["ac"], c_qa_all)
            mm_last = {}
            gq_done = {}
            h1_free = {}
            for osl in range(NOSL):
                wb = w1sl[osl % 2]
                bb1 = b1sl[osl % 2]
                sw1 = dsem["w1%d" % (osl % 2)]
                sb1 = dsem["b1%d" % (osl % 2)]
                if osl >= 2:
                    p.wait("sync", sem["pe"], mm_last[osl - 2])
                    p.wait("gpsimd", sem["pe"], mm_last[osl - 2])
                in_ap = bass.AP(io["w1t_s"], osl * OSL,
                                [[HID, 128], [128 * HID, KC], [1, OSL]])
                out_ap = wb.ap().rearrange("p (kc c) -> p kc c", c=OSL)
                c_w = 0
                for hh in range(2):
                    c_w = p.dma("sync", out_ap[:, hh * 8:(hh + 1) * 8, :],
                                in_ap[:, hh * 8:(hh + 1) * 8, :], sw1)
                c_b1 = p.dma("gpsimd", bb1[:, :],
                             io["b1r"].ap()[:, osl * OSL:(osl + 1) * OSL],
                             sb1)
                p.wait("tensor", sw1, c_w)
                p.wait("tensor", sb1, c_b1)
                gc = gcat[osl % 2]
                if osl >= 2:
                    p.wait("scalar", sem["dv"], gq_done[osl - 2])
                for i in range(SWTC):
                    idx = osl * SWTC + i
                    ph = ps_h1[idx % 2]
                    gate = h1_free.get(idx - 2)
                    if gate:
                        p.wait("tensor", sem["ac"], gate)
                    for kc in range(KC):
                        p.op("tensor", lambda E, kc=kc, i=i, ph=ph, wb=wb:
                             E.matmul(ph[:, 0:OSL],
                                      qa_fm[:, kc * TBLK + i * 128:
                                            kc * TBLK + (i + 1) * 128],
                                      wb[:, kc * OSL:(kc + 1) * OSL],
                                      start=(kc == 0), stop=False))
                    c_mm = p.op("tensor", lambda E, ph=ph, bb1=bb1: E.matmul(
                        ph[:, 0:OSL], ones1[:, :], bb1[:, :],
                        start=False, stop=True), inc=sem["pe"])
                    mm_last[osl] = c_mm
                    hb = h1b[idx % 4]
                    p.wait("scalar", sem["pe"], c_mm)
                    p.op("scalar", lambda E, hb=hb, ph=ph: E.copy(
                        hb[:, :], ph[:, 0:OSL]), reads=(nm(ph),),
                        writes=(nm(hb),))
                    c_gl = p.op("scalar", lambda E, hb=hb, gc=gc, i=i:
                                E.activation(gc[:, i * OSL:(i + 1) * OSL],
                                             hb[:, :], ACTF.Gelu_apprx_tanh,
                                             bias=0.0, scale=1.0),
                                reads=(nm(hb),), writes=(nm(gc),),
                                inc=sem["ac"])
                    h1_free[idx] = c_gl
                p.wait("vector", sem["ac"], sem["ac"].n)
                emit_mx_quant_core(p, sem, gc, slice(0, SWTC * OSL), 64, gq)
                nbo = OSL // 32
                for i in range(SWTC):
                    emit_mx_q8(p, sem, gq, slice(i * OSL, (i + 1) * OSL),
                               q8g[i], slice(osl * OSL, (osl + 1) * OSL))
                    p.op("vector", lambda E, i=i, osl=osl: E.tensor_copy(
                        sscg[i][:, osl * nbo:(osl + 1) * nbo],
                        gq["ssc"][:, i * nbo:(i + 1) * nbo]),
                        reads=(nm(gq["ssc"]),), writes=(nm(sscg[i]),))
                gq_done[osl] = p.op("vector", lambda E: E.drain(),
                                    inc=sem["dv"])
            f1_mm_last = mm_last[NOSL - 1]

            h2_free = f2_done_prev
            for pr in range(SWTC // 2):
                i0, i1 = 2 * pr, 2 * pr + 1
                tc0, tc1 = tcs[i0], tcs[i1]
                if pr == 0:
                    p.wait("scalar", sem["pe"], f1_mm_last)
                else:
                    p.wait("scalar", sem["pe"], f2_pe_last)
                stg_ev = {}
                for g in range(32):
                    t = ps_tp[g % 2]
                    if g >= 2:
                        p.wait("tensor", sem["ac"], stg_ev[g - 2])
                    c_tp = 0
                    for u in range(4):
                        oc = g * 2 + u // 2
                        ch = u % 2
                        qt = qgtm[u % 2]
                        gate = tp_used.get(u % 2)
                        if gate is not None:
                            p.wait("vector", sem["pe"], gate)
                        c_ex = emit_mx_expand(
                            p, sem, q8g[2 * pr + ch],
                            slice(oc * 128, (oc + 1) * 128),
                            sscg[2 * pr + ch],
                            slice(oc * 4, (oc + 1) * 4),
                            4, qt, slice(0, 128), inc=sem["dv"])
                        p.wait("tensor", sem["dv"], c_ex)
                        c_tp = p.op("tensor", lambda E, t=t, qt=qt, u=u:
                                    E.transpose(t[:, u * 128:(u + 1) * 128],
                                                qt[:, 0:128], ident_b[:, :]),
                                    inc=sem["pe"])
                        tp_used[u % 2] = c_tp
                    p.wait("scalar", sem["pe"], c_tp)
                    stg_ev[g] = p.op(
                        "scalar", lambda E, g=g, t=t: E.copy(
                            qa_fm[:, g * 512:(g + 1) * 512], t[:, 0:512]),
                        reads=(nm(t),), writes=(nm(qa_fm),), inc=sem["ac"])
                c_stage = sem["ac"].n
                p.wait("tensor", sem["ac"], c_stage)
                if h2_free:
                    p.wait("tensor", sem["ac"], h2_free)
                for oc in range(OC):
                    wb2 = w2sl[oc % 2]
                    sw2 = dsem["w2%d" % (oc % 2)]
                    gate = w2_gate.get(oc % 2)
                    if gate is not None:
                        p.wait("sync", sem["pe"], gate)
                    c_w2 = p.dma("sync", wb2[:, :],
                                 bass.AP(io["w2t_s"], oc * 128 * DM,
                                         [[DM, 128], [1, DM]]), sw2)
                    p.wait("tensor", sw2, c_w2)
                    for ch, psX in ((0, psA), (1, psB)):
                        for ds in range(4):
                            p.op("tensor", lambda E, ds=ds, oc=oc, ch=ch,
                                 psX=psX, wb2=wb2: E.matmul(
                                     psX[:, ds * 512:(ds + 1) * 512],
                                     qa_fm[:, oc * 256 + ch * 128:
                                           oc * 256 + (ch + 1) * 128],
                                     wb2[:, ds * 512:(ds + 1) * 512],
                                     start=(oc == 0), stop=False))
                    c_h2m = p.op("tensor", lambda E: E.drain(), inc=sem["pe"])
                    w2_gate[oc % 2] = c_h2m
                for ch, psX in ((0, psA), (1, psB)):
                    for hh in range(4):
                        c_b = p.op("tensor", lambda E, hh=hh, psX=psX:
                                   E.matmul(psX[:, hh * 512:(hh + 1) * 512],
                                            ones1[:, :],
                                            b2sl[:, hh * 512:(hh + 1) * 512],
                                            start=False, stop=True),
                                   inc=sem["pe"])
                f2_pe_last = c_b
                h2_frees = []
                h2b_rd = 0
                for ch, psX, i, tc in ((0, psA, i0, tc0), (1, psB, i1, tc1)):
                    p.wait("scalar", sem["pe"], c_b)
                    if h2b_rd:
                        p.wait("scalar", sem["dv"], h2b_rd)
                    c_h2b = p.op("scalar", lambda E, psX=psX: E.copy(
                        h2b[:, :], psX[:, :]),
                        reads=(nm(psX),), writes=(nm(h2b),), inc=sem["ac"])
                    h2_frees.append(c_h2b)
                    hsrc = h2b
                    xb = xt[i % 2]
                    sx = dsem["xt%d" % (i % 2)]
                    for s_, v_ in xt_gate[i % 2]:
                        p.wait("sync", s_, v_)
                    c_x2 = p.dma("sync", xb[:, :],
                                 src.ap()[tc * 128:(tc + 1) * 128, :], sx)
                    p.wait("vector", sx, c_x2)
                    p.wait("vector", sem["ac"], c_h2b)
                    xnm = nm(xb)
                    p.op("vector", lambda E, tc=tc, xb=xb: E.tensor_scalar(
                        fb[:, :], xb[:, :], musig[:, 2 * tc:2 * tc + 1],
                        musig[:, 2 * tc + 1:2 * tc + 2], ALU.subtract,
                        ALU.mult),
                        reads=(xnm, "musig"), writes=(nm(fb),))
                    h2b_rd = p.op("vector", lambda E, hsrc=hsrc:
                                  E.tensor_tensor(
                                      fb[:, :], fb[:, :], hsrc[:, :],
                                      ALU.add),
                                  reads=(nm(fb), nm(hsrc)),
                                  writes=(nm(fb),), inc=sem["dv"])
                    c_xu = p.op("vector", lambda E, xb=xb:
                                E.scalar_tensor_tensor(
                                    xb[:, :], fb[:, :], 0.5, xb[:, :],
                                    ALU.mult, ALU.add),
                                reads=(nm(fb), xnm), writes=(xnm,),
                                inc=sem["dv"])
                    p.wait("sync", sem["dv"], c_xu)
                    c_ws = p.dma("sync",
                                 dst.ap()[tc * 128:(tc + 1) * 128, :],
                                 xb[:, :], sx)
                    xt_gate[i % 2] = [(sx, c_ws)]
                f2_done_prev = max(h2_frees)
                h2_free = f2_done_prev

        for k in ("xt0", "xt1"):
            p.wait("sync", dsem[k], dsem[k].n)
        with nc.Block() as blk:
            p.flush_into_block(blk)



def _phase_attn(nc, sem, dsem, sbt, pst, io, pers):
    ident_b = pers["ident_b"]
    perm_b = pers["perm_b"]
    mask_b = pers["mask_b"]
    stat6, stat2, rsg = pers["stat6"], pers["stat2"], pers["rsg"]
    zsum, rz, negmax = pers["zsum"], pers["rz"], pers["negmax"]
    epsc, nhalf = pers["epsc"], pers["nhalf"]
    xscr = io["xscr"]
    p = Prog(nc)

    stA = contextlib.ExitStack()
    a_fm = sbt(stA, "a_fm", [128, KC * S], BF16)
    xbuf = [sbt(stA, "at_x%d" % i, [128, DM], F32) for i in range(2)]
    atile = sbt(stA, "at_t", [128, DM], BF16)
    wcol = [sbt(stA, "pj_w%d" % i, [128, KC * 128], BF16) for i in range(2)]
    vwb = sbt(stA, "pj_vw", [128, KC * 512], BF16)
    pout = sbt(stA, "pj_o", [128, S], BF16)
    vout = sbt(stA, "pj_v", [128, 512], BF16)
    ps_tp = [pst(stA, "at_tp%d" % i, [128, 1024], BF16) for i in range(2)]
    ps_p = pst(stA, "pj_ps", [128, 512], F32)
    ps_v = pst(stA, "pj_pv", [128, 512], F32)

    with stA:
        c_at_prev = 0
        xb_gate = [0, 0]
        for tc in range(NTC):
            xb = xbuf[tc % 2]
            sx = dsem["xt%d" % (tc % 2)]
            if xb_gate[tc % 2]:
                p.wait("sync", sem["dv"], xb_gate[tc % 2])
            c_x = p.dma("sync", xb[:, :],
                        xscr.ap()[tc * 128:(tc + 1) * 128, :], sx)
            xnm = nm(xb)
            p.wait("vector", sx, c_x)
            c_st = emit_ln_stats(p, sem, xb, stat6, stat2)
            c_rs = emit_rsig(p, sem, stat2, rsg, epsc, nhalf,
                             gate_dv=max(c_st, c_at_prev))
            p.wait("vector", sem["ac"], c_rs)
            c_at = p.op("vector", lambda E, xb=xb: E.tensor_scalar(
                atile[:, :], xb[:, :], stat2[:, 0:1], rsg[:, 0:1],
                ALU.subtract, ALU.mult), reads=(xnm, "stat2", "rsg"),
                writes=(nm(atile),), inc=sem["dv"])
            c_at_prev = c_at
            xb_gate[tc % 2] = c_at
            p.wait("tensor", sem["dv"], c_at)
            evs = {}
            for g4 in range(KC // 4):
                t = ps_tp[g4 % 2]
                if g4 >= 2:
                    p.wait("tensor", sem["ac"], evs[g4 - 2])
                c_pe = 0
                for u in range(4):
                    kc = g4 * 4 + u
                    c_pe = p.op("tensor", lambda E, kc=kc, t=t, u=u:
                                E.transpose(t[:, u * 128:(u + 1) * 128],
                                            atile[:, kc * 128:(kc + 1) * 128],
                                            ident_b[:, :]), inc=sem["pe"])
                p.wait("scalar", sem["pe"], c_pe)
                out3 = bass.AP(a_fm, g4 * 4 * S + tc * 128,
                               [[KC * S, 128], [S, 4], [1, 128]])
                in3 = t.ap()[:, 0:512].rearrange("p (f c) -> p f c", c=128)
                evs[g4] = p.op("scalar", lambda E, out3=out3, in3=in3:
                               E.copy(out3, in3),
                               reads=(nm(t),), writes=(nm(a_fm),),
                               inc=sem["ac"])
            p.wait("vector", sem["pe"], sem["pe"].n)
        c_afm = sem["ac"].n
        p.wait("tensor", sem["ac"], c_afm)

        proj_fin = {}
        spl_prev = 0
        for sidx in range(20):
            wb = wcol[sidx % 2]
            swc = dsem["wc%d" % (sidx % 2)]
            gate = proj_fin.get(sidx - 2)
            if gate:
                p.wait("gpsimd", sem["pe"], gate)
            if sidx < 16:
                wt, col0, spill, sp_i = io["wq"], sidx * 128, io["q_s"], sidx
                wcols = 2048
            else:
                wt, col0, spill, sp_i = (io["wk"], (sidx - 16) * 128,
                                         io["k_s"], sidx - 16)
                wcols = 512
            in_ap = bass.AP(wt, col0,
                            [[wcols, 128], [128 * wcols, KC], [1, 128]])
            out_ap = wb.ap().rearrange("p (kc c) -> p kc c", c=128)
            c_wl = p.dma("gpsimd", out_ap, in_ap, swc)
            p.wait("tensor", swc, c_wl)
            for ts in range(4):
                if spl_prev:
                    p.wait("scalar", dsem["spl"], spl_prev)
                for kc in range(KC):
                    p.op("tensor", lambda E, kc=kc, ts=ts, wb=wb: E.matmul(
                        ps_p[:, :], wb[:, kc * 128:(kc + 1) * 128],
                        a_fm[:, kc * S + ts * 512:kc * S + (ts + 1) * 512],
                        start=(kc == 0), stop=(kc == KC - 1)))
                c_mm = p.op("tensor", lambda E: E.drain(), inc=sem["pe"])
                p.wait("scalar", sem["pe"], c_mm)
                c_ev = p.op("scalar", lambda E, ts=ts: E.copy(
                    pout[:, ts * 512:(ts + 1) * 512], ps_p[:, :]),
                    reads=(nm(ps_p),), writes=(nm(pout),), inc=sem["ac"])
                p.wait("tensor", sem["ac"], c_ev)
            proj_fin[sidx] = sem["pe"].n
            p.wait("gpsimd", sem["ac"], sem["ac"].n)
            spl_prev = p.dma("gpsimd",
                             bass.AP(spill, sp_i * 128 * S,
                                     [[S, 128], [1, S]]),
                             pout[:, :], dsem["spl"])
        in_ap = bass.AP(io["wv"], 0, [[512, 128], [128 * 512, KC], [1, 512]])
        c_vw = p.dma("gpsimd",
                     vwb.ap().rearrange("p (kc c) -> p kc c", c=512),
                     in_ap, dsem["vwl"])
        p.wait("tensor", dsem["vwl"], c_vw)
        vsp_prev = 0
        for tc in range(NTC):
            for kc in range(KC):
                p.op("tensor", lambda E, kc=kc, tc=tc: E.matmul(
                    ps_v[:, :],
                    a_fm[:, kc * S + tc * 128:kc * S + (tc + 1) * 128],
                    vwb[:, kc * 512:(kc + 1) * 512],
                    start=(kc == 0), stop=(kc == KC - 1)))
            c_mm = p.op("tensor", lambda E: E.drain(), inc=sem["pe"])
            p.wait("scalar", sem["pe"], c_mm)
            if vsp_prev:
                p.wait("scalar", dsem["vsp"], vsp_prev)
            c_ev = p.op("scalar", lambda E: E.copy(vout[:, :], ps_v[:, :]),
                        reads=(nm(ps_v),), writes=(nm(vout),), inc=sem["ac"])
            p.wait("tensor", sem["ac"], c_ev)
            p.wait("gpsimd", sem["ac"], c_ev)
            out_ap = bass.AP(io["v_s"], tc * 128 * 128,
                             [[128, 128], [NTC * 128 * 128, 4], [1, 128]])
            vsp_prev = p.dma("gpsimd", out_ap,
                             vout.ap().rearrange("p (g c) -> p g c", c=128),
                             dsem["vsp"])
        p.wait("sync", dsem["vsp"], dsem["vsp"].n)
        p.wait("sync", dsem["spl"], dsem["spl"].n)

    stC = contextlib.ExitStack()
    tabs = {name: sbt(stC, "tab_" + name, [128, S], F32)
            for name in ("cq", "sq", "ck", "sk")}
    kst = sbt(stC, "sc_k", [128, S], BF16)
    krot = sbt(stC, "sc_kr", [128, S], BF16)
    vst = sbt(stC, "sc_v", [128, NTC * 128], BF16)
    qst = [sbt(stC, "sc_q%d" % i, [128, S], BF16) for i in range(2)]
    qrot = sbt(stC, "sc_qr", [128, S], BF16)
    t1r = sbt(stC, "sc_t1", [128, 512], F32)
    t2r = sbt(stC, "sc_t2", [128, 512], F32)
    esb = [sbt(stC, "sc_e%d" % i, [128, S], BF16) for i in range(2)]
    etb = [sbt(stC, "sc_et%d" % i, [128, 512], BF16) for i in range(2)]
    ao = sbt(stC, "sc_ao", [128, S], BF16)
    ps_sc = pst(stC, "sc_ps", [128, S], F32)
    ps_rp = pst(stC, "sc_rp", [128, 512], F32)
    ps_tp2 = [pst(stC, "sc_tp%d" % i, [128, 1024], BF16) for i in range(2)]
    ps_pv = pst(stC, "sc_pv", [128, 512], F32)

    def rope(src_t, dst_t, ctab, stab):
        for q4 in range(4):
            sl = slice(q4 * 512, (q4 + 1) * 512)
            c_pm = p.op("tensor", lambda E, sl=sl, src_t=src_t: E.matmul(
                ps_rp[:, :], perm_b[:, :], src_t[:, sl],
                start=True, stop=True), inc=sem["pe"])
            p.op("vector", lambda E, sl=sl, src_t=src_t, ctab=ctab:
                 E.tensor_tensor(t1r[:, :], src_t[:, sl], ctab[:, sl],
                                 ALU.mult),
                 reads=(nm(src_t), nm(ctab)), writes=(nm(t1r),))
            p.wait("vector", sem["pe"], c_pm)
            p.op("vector", lambda E, sl=sl, stab=stab:
                 E.scalar_tensor_tensor(
                     t2r[:, :], ps_rp[:, :], 0.0, stab[:, sl],
                     ALU.add, ALU.mult),
                 reads=(nm(ps_rp), nm(stab)), writes=(nm(t2r),))
            c_rt = p.op("vector", lambda E, sl=sl, dst_t=dst_t:
                        E.tensor_tensor(dst_t[:, sl], t1r[:, :], t2r[:, :],
                                        ALU.add),
                        reads=(nm(t1r), nm(t2r)), writes=(nm(dst_t),),
                        inc=sem["dv"])
            p.wait("tensor", sem["dv"], c_rt)
        return sem["dv"].n

    with stC:
        for name, srcd in (("cq", io["cqd"]), ("sq", io["sqd"]),
                           ("ck", io["ckd"]), ("sk", io["skd"])):
            p.dma("sync", tabs[name][:, :], srcd.ap(), dsem["blks"])
        p.wait("vector", dsem["blks"], dsem["blks"].n)
        c_exp_prev = 0
        c_nr_prev = 0
        c_pv_prev = 0
        pv_gate = [0, 0]
        gp_end = (0, 0)
        strip_end = {}
        strip_scores_end = 0
        aosp_prev = 0
        for gp in range(4):
            p.wait("sync", sem["pe"], gp_end[0])
            p.wait("sync", sem["dv"], gp_end[1])
            c_k = p.dma("sync", kst[:, :],
                        bass.AP(io["k_s"], gp * 128 * S, [[S, 128], [1, S]]),
                        dsem["kst"])
            in_ap = bass.AP(io["v_s"], gp * NTC * 128 * 128,
                            [[128, 128], [128 * 128, NTC], [1, 128]])
            c_v = p.dma("sync",
                        vst.ap().rearrange("p (t c) -> p t c", c=128),
                        in_ap, dsem["vst"])
            p.wait("vector", dsem["kst"], c_k)
            p.wait("tensor", dsem["kst"], c_k)
            p.wait("vector", sem["pe"], gp_end[0])
            rope(kst, krot, tabs["ck"], tabs["sk"])
            p.wait("tensor", dsem["vst"], c_v)
            for i in range(4):
                sidx = 4 * gp + i
                qb = qst[i % 2]
                sq = dsem["qs%d" % (i % 2)]
                g2 = strip_end.get(sidx - 2)
                if g2:
                    p.wait("sync", sem["pe"], g2[0])
                    p.wait("sync", sem["dv"], g2[1])
                c_q = p.dma("sync", qb[:, :],
                            bass.AP(io["q_s"], sidx * 128 * S,
                                    [[S, 128], [1, S]]), sq)
                p.wait("vector", sq, c_q)
                p.wait("tensor", sq, c_q)
                if sidx > 0:
                    p.wait("vector", sem["pe"], strip_scores_end)
                c_qrot = rope(qb, qrot, tabs["cq"], tabs["sq"])
                strip_end[sidx] = (sem["pe"].n, c_qrot)
                p.wait("tensor", sem["dv"], c_qrot)
                def emit_tpv(n, qtc, hh, eb_src, c_nr):
                    p.wait("tensor", sem["dv"], c_nr)
                    ng = (qtc + 1 + 3) // 4
                    pvst = {}

                    def emit_grp(g, qtc=qtc, eb_src=eb_src):
                        t = ps_tp2[g % 2]
                        nt = min(4, qtc + 1 - g * 4)
                        c_tp = 0
                        for u in range(nt):
                            j = g * 4 + u
                            c_tp = p.op("tensor", lambda E, j=j, t=t,
                                        u=u, eb_src=eb_src: E.transpose(
                                            t[:, u * 128:(u + 1) * 128],
                                            eb_src[:, j * 128:(j + 1) * 128],
                                            ident_b[:, :]),
                                        inc=sem["pe"])
                        p.wait("scalar", sem["pe"], c_tp)
                        eb = etb[g % 2]
                        c_ev = p.op("scalar", lambda E, eb=eb, t=t,
                                    nt=nt: E.copy(eb[:, 0:nt * 128],
                                                  t[:, 0:nt * 128]),
                                    reads=(nm(t),), writes=(nm(eb),),
                                    inc=sem["ac"])
                        pvst[g] = (c_ev, eb, nt)

                    emit_grp(0)
                    for g in range(ng):
                        if g + 1 < ng:
                            emit_grp(g + 1)
                        c_ev, eb, nt = pvst.pop(g)
                        p.wait("tensor", sem["ac"], c_ev)
                        for u in range(nt):
                            j = g * 4 + u
                            p.op("tensor", lambda E, j=j, eb=eb, u=u,
                                 qtc=qtc: E.matmul(
                                     ps_pv[:, 0:128],
                                     vst[:, j * 128:(j + 1) * 128],
                                     eb[:, u * 128:(u + 1) * 128],
                                     start=(j == 0), stop=(j == qtc)))
                    c_pv = p.op("tensor", lambda E: E.drain(),
                                inc=sem["pe"])
                    p.wait("scalar", sem["pe"], c_pv)
                    c_ao = p.op("scalar", lambda E, hh=hh, qtc=qtc:
                                E.copy(ao[hh * 64:(hh + 1) * 64,
                                          qtc * 128:(qtc + 1) * 128],
                                       ps_pv[hh * 64:(hh + 1) * 64,
                                             0:128]),
                                reads=(nm(ps_pv),), writes=(nm(ao),),
                                inc=sem["ac"])
                    p.wait("tensor", sem["ac"], c_ao)
                    return c_pv

                pend = None
                nit = 0
                for hh in range(2):
                    rows = slice(hh * 64, (hh + 1) * 64)
                    for qtc in range(NTC):
                        L = (qtc + 1) * 128
                        n = nit
                        nit += 1
                        eb_cur = esb[n % 2]
                        if c_exp_prev:
                            p.wait("tensor", sem["ac"], c_exp_prev)
                        slabs = []
                        off = 0
                        while off < L:
                            w = min(512, L - off)
                            slabs.append((off, w))
                            off += w
                        ns = len(slabs)
                        for si, (off, w) in enumerate(slabs):
                            p.op("tensor", lambda E, off=off, w=w, rows=rows,
                                 qtc=qtc: E.matmul(
                                     ps_sc[:, off:off + w],
                                     qrot[rows, qtc * 128:(qtc + 1) * 128],
                                     krot[rows, off:off + w],
                                     start=True, stop=True))
                            c_sl = p.op("tensor", lambda E: E.drain(),
                                        inc=sem["pe"])
                            p.wait("vector", sem["pe"], c_sl)
                            if si == ns - 1:
                                p.op("vector", lambda E, qtc=qtc:
                                     E.tensor_tensor(
                                         ps_sc[:, qtc * 128:(qtc + 1) * 128],
                                         ps_sc[:, qtc * 128:(qtc + 1) * 128],
                                         mask_b[:, :], ALU.add),
                                     reads=(nm(ps_sc), "mask_b"),
                                     writes=(nm(ps_sc),))
                            p.op("vector", lambda E, si=si, off=off, w=w:
                                 E.tensor_reduce(
                                     stat6[:, si:si + 1],
                                     ps_sc[:, off:off + w], AX.X, ALU.max),
                                 reads=(nm(ps_sc),), writes=("stat6",))
                        c_mx = p.op("vector", lambda E, ns=ns, n=n:
                                    E.tensor_reduce(
                                        negmax[:, n % 2:n % 2 + 1],
                                        stat6[:, 0:ns], AX.X,
                                        ALU.max, negate=True),
                                    reads=("stat6",), writes=("negmax",),
                                    inc=sem["dv"])
                        if pend is not None:
                            c_pv_prev = emit_tpv(*pend)
                            pv_gate[(n - 1) % 2] = c_pv_prev
                        p.wait("scalar", sem["dv"], c_mx)
                        if c_nr_prev:
                            p.wait("scalar", sem["dv"], c_nr_prev)
                        if pv_gate[n % 2]:
                            p.wait("scalar", sem["pe"], pv_gate[n % 2])
                        c_ex = p.op("scalar", lambda E, L=L, n=n,
                                    eb_cur=eb_cur: E.activation(
                                        eb_cur[:, 0:L], ps_sc[:, 0:L],
                                        ACTF.Exp,
                                        bias=negmax[:, n % 2:n % 2 + 1],
                                        scale=1.0, accum_out=zsum[:, :]),
                                    reads=(nm(ps_sc), "negmax"),
                                    writes=(nm(eb_cur), "zsum"),
                                    inc=sem["ac"])
                        c_exp_prev = c_ex
                        p.wait("vector", sem["ac"], c_ex)
                        p.op("vector", lambda E: E.reciprocal(
                            rz[:, :], zsum[:, :]), reads=("zsum",),
                            writes=("rz",))
                        c_nr = p.op("vector", lambda E, L=L, eb_cur=eb_cur:
                                    E.tensor_scalar(
                                        eb_cur[:, 0:L], eb_cur[:, 0:L],
                                        rz[:, 0:1], None, ALU.mult),
                                    reads=(nm(eb_cur), "rz"),
                                    writes=(nm(eb_cur),), inc=sem["dv"])
                        c_nr_prev = c_nr
                        pend = (n, qtc, hh, eb_cur, c_nr)
                if pend is not None:
                    c_pv_prev = emit_tpv(*pend)
                    pv_gate[(nit - 1) % 2] = c_pv_prev
                    pend = None
                strip_scores_end = sem["pe"].n
                p.wait("gpsimd", sem["ac"], sem["ac"].n)
                if aosp_prev:
                    p.wait("scalar", dsem["aosp"], aosp_prev)
                aosp_prev = p.dma("gpsimd",
                                  bass.AP(io["ao_s"], sidx * 128 * S,
                                          [[S, 128], [1, S]]), ao[:, :],
                                  dsem["aosp"])
            gp_end = (sem["pe"].n, sem["dv"].n)
        p.wait("sync", dsem["aosp"], dsem["aosp"].n)

    stD = contextlib.ExitStack()
    wob = [sbt(stD, "op_w%d" % s, [128, DM], BF16) for s in range(16)]
    aoc = [sbt(stD, "op_a%d" % i, [128, 16 * 128], BF16) for i in range(2)]
    xb2 = [sbt(stD, "op_x%d" % i, [128, DM], F32) for i in range(2)]
    ps_o = pst(stD, "op_ps", [128, DM], F32)
    with stD:
        for s in range(16):
            p.dma("gpsimd", wob[s][:, :],
                  bass.AP(io["wo"], s * 128 * DM, [[DM, 128], [1, DM]]),
                  dsem["blkp"])
        c_wo = dsem["blkp"].n
        p.wait("tensor", dsem["blkp"], c_wo)
        res_done = {}
        for tc in range(NTC):
            ab = aoc[tc % 2]
            xb = xb2[tc % 2]
            sao = dsem["ao%d" % (tc % 2)]
            sxb = dsem["xb%d" % (tc % 2)]
            gate = res_done.get(tc - 2)
            if gate:
                p.wait("sync", sxb, gate)
            in_ap = bass.AP(io["ao_s"], tc * 128,
                            [[S, 128], [128 * S, 16], [1, 128]])
            c_ao = p.dma("sync",
                         ab.ap().rearrange("p (s c) -> p s c", c=128),
                         in_ap, sao)
            c_x = p.dma("sync", xb[:, :],
                        xscr.ap()[tc * 128:(tc + 1) * 128, :], sxb)
            p.wait("tensor", sao, c_ao)
            for s in range(16):
                for ds in range(4):
                    p.op("tensor", lambda E, s=s, ds=ds, ab=ab: E.matmul(
                        ps_o[:, ds * 512:(ds + 1) * 512],
                        ab[:, s * 128:(s + 1) * 128],
                        wob[s][:, ds * 512:(ds + 1) * 512],
                        start=(s == 0), stop=(s == 15)))
            c_om = p.op("tensor", lambda E: E.drain(), inc=sem["pe"])
            p.wait("vector", sem["pe"], c_om)
            p.wait("vector", sxb, c_x)
            xnm = nm(xb)
            c_xr = p.op("vector", lambda E, xb=xb: E.tensor_tensor(
                xb[:, :], xb[:, :], ps_o[:, :], ALU.add),
                reads=(xnm, nm(ps_o)), writes=(xnm,), inc=sem["dv"])
            p.wait("tensor", sem["dv"], c_xr)
            p.wait("sync", sem["dv"], c_xr)
            c_st = p.dma("sync", xscr.ap()[tc * 128:(tc + 1) * 128, :],
                         xb[:, :], sxb)
            res_done[tc] = c_st
        for k in ("xb0", "xb1"):
            p.wait("sync", dsem[k], dsem[k].n)
        with nc.Block() as blk:
            p.flush_into_block(blk)



def kernel(L0_attn_x, L0_attn_Wq, L0_attn_Wk, L0_attn_Wv, L0_attn_Wo,
           L0_ffn_ln_w, L0_ffn_ln_b, L0_ffn_W1, L0_ffn_b1, L0_ffn_W2,
           L0_ffn_b2):
    x = np.ascontiguousarray(np.asarray(L0_attn_x, dtype=np.float32))
    consts = _host_consts()
    perm = _head_perm()
    wq_p = np.ascontiguousarray(np.asarray(L0_attn_Wq, np.float32)[:, perm])
    wo_p = np.ascontiguousarray(np.asarray(L0_attn_Wo, np.float32)[perm, :])
    wk_ = np.ascontiguousarray(np.asarray(L0_attn_Wk, np.float32))
    wv_ = np.ascontiguousarray(np.asarray(L0_attn_Wv, np.float32))
    w1_ = np.ascontiguousarray(np.asarray(L0_ffn_W1, np.float32))
    w2_ = np.ascontiguousarray(np.asarray(L0_ffn_W2, np.float32))
    lnw = np.ascontiguousarray(np.broadcast_to(
        np.asarray(L0_ffn_ln_w, np.float32)[None, :], (128, DM)))
    lnb = np.ascontiguousarray(np.broadcast_to(
        np.asarray(L0_ffn_ln_b, np.float32)[None, :], (128, DM)))
    b1v = np.ascontiguousarray(np.asarray(L0_ffn_b1, np.float32)[None, :])
    b2v = np.ascontiguousarray(np.asarray(L0_ffn_b2, np.float32)[None, :])

    nc = build_nc()
    base = {"wq": wq_p, "wk": wk_, "wv": wv_, "wo": wo_p, "w1": w1_,
            "w2": w2_, "lnww": lnw, "lnbb": lnb, "b1r": b1v, "b2r": b2v,
            "cqd": consts["cq"], "sqd": consts["sq"], "ckd": consts["ck"],
            "skd": consts["sk"], "permd": consts["perm"],
            "identd": consts["ident"], "maskd": consts["mask"]}
    in_maps = [dict(base, xin=x[c]) for c in range(8)]
    res = run_bass_kernel_spmd(nc, in_maps, core_ids=list(range(8)))
    return np.stack([np.asarray(r["yout"], dtype=np.float32)
                     for r in res.results], axis=0)


# version 10
# speedup vs baseline: 1.6661x; 1.0263x over previous
import contextlib
import os
import numpy as np
import concourse.bass as bass
import concourse.mybir as mybir
from concourse.bass_utils import run_bass_kernel_spmd

try:
    import ml_dtypes
    _BF16_NP = ml_dtypes.bfloat16
except ImportError:
    _BF16_NP = None

F32 = mybir.dt.float32
BF16 = mybir.dt.bfloat16
I32 = mybir.dt.int32
I8 = mybir.dt.int8
ALU = mybir.AluOpType
AX = mybir.AxisListType
ACTF = mybir.ActivationFunctionType

S = 2048
DM = 2048
HID = 8192
NTC = S // 128
KC = DM // 128
OC = HID // 128
TBLK = 1024
NSW = S // TBLK
SWTC = TBLK // 128
OSL = 256
NOSL = HID // OSL
HD = 64
EPS = 1e-5
NEG = -1.0e30
MAGIC = 12582912.0
CLIPC = float(np.nextafter(np.float32(127.5), np.float32(0)))

ENGS = ("sync", "gpsimd", "vector", "scalar", "tensor")
SIM_DRAINS = True
PHASES = ("wprep", "fa", "attn", "fb")
DBG_SCRATCH = False


class Sem:
    def __init__(self, h):
        self.h = h
        self.n = 0

    def inc(self, k=1):
        self.n += k
        return self.n


class Prog:
    def __init__(self, nc):
        self.nc = nc
        self.q = {e: [] for e in ENGS}
        self.dirty = {e: set() for e in ENGS}

    def wait(self, eng, sem, val):
        if val > 0:
            self.q[eng].append(lambda E, s=sem, v=val: E.wait_ge(s.h, v))

    def op(self, eng, build, reads=(), writes=(), inc=None, inc_n=1):
        names = set(reads) | set(writes)
        if (SIM_DRAINS and eng in ("vector", "scalar", "gpsimd")
                and (names & self.dirty[eng])):
            self.q[eng].append(lambda E: E.drain())
            self.dirty[eng] = set()
        self.dirty[eng] |= set(writes)
        if inc is not None:
            def fn(E, b=build, s=inc, n=inc_n):
                b(E).then_inc(s.h, n)
            self.q[eng].append(fn)
            return inc.inc(inc_n)
        self.q[eng].append(build)
        return None

    def dma(self, eng, out_ap, in_ap, sem, **kw):
        def fn(E, o=out_ap, i=in_ap, s=sem, kw=kw):
            E.dma_start(o, i, **kw).then_inc(s.h, 16)
        self.q[eng].append(fn)
        return sem.inc(16)

    def flush_into_block(self, block):
        q = self.q

        @block.sync
        def _(e):
            for fn in q["sync"]:
                fn(e)

        @block.gpsimd
        def _(e):
            for fn in q["gpsimd"]:
                fn(e)

        @block.vector
        def _(e):
            for fn in q["vector"]:
                fn(e)

        @block.scalar
        def _(e):
            for fn in q["scalar"]:
                fn(e)

        @block.tensor
        def _(e):
            for fn in q["tensor"]:
                fn(e)


def nm(t):
    return t.ap().name if callable(getattr(t, "ap", None)) else t.name


def _host_consts():
    half = HD // 2
    inv = (1.0 / np.power(np.float32(10000.0),
                          np.arange(half, dtype=np.float32) / np.float32(half))
           ).astype(np.float32)
    pos = np.arange(S, dtype=np.float32)
    fr = np.outer(pos, inv).astype(np.float32)
    cos = np.cos(fr).astype(np.float32)
    sin = np.sin(fr).astype(np.float32)
    j_of_r = (np.arange(128) % HD) // 2
    C = cos[:, j_of_r].T.astype(np.float32)
    Sg = sin[:, j_of_r].T.astype(np.float32)
    sign = np.where(np.arange(128) % 2 == 0, -1.0, 1.0).astype(np.float32)
    Stab = (Sg * sign[:, None]).astype(np.float32)
    swap = np.arange(128)
    swap = swap + np.where(swap % 2 == 0, 1, -1)
    perm = np.zeros((128, 128), dtype=np.float32)
    perm[swap, np.arange(128)] = 1.0
    ident = np.eye(128, dtype=np.float32)
    ii = np.arange(128)
    mask = np.where(ii[None, :] <= ii[:, None], 0.0, NEG).astype(np.float32)
    if _BF16_NP is not None:
        perm = perm.astype(_BF16_NP)
        ident = ident.astype(_BF16_NP)
    return dict(cq=(C * np.float32(0.125)).astype(np.float32),
                sq=(Stab * np.float32(0.125)).astype(np.float32),
                ck=C, sk=Stab, perm=perm, ident=ident, mask=mask)


def _head_perm():
    order = []
    for gp in range(4):
        for i in range(4):
            for h in (8 * gp + i, 8 * gp + 4 + i):
                order.extend(range(h * HD, (h + 1) * HD))
    return np.asarray(order, dtype=np.int64)



def emit_mx_quant_core(p, sem, src_t, src_sl, nblk, bufs):
    am, rsc, ssc, y, r = (bufs["am"], bufs["rsc"], bufs["ssc"], bufs["y"],
                          bufs["r"])
    n = nblk * 32
    src3 = src_t.ap()[:, src_sl].rearrange("p (nb k) -> p nb k", k=32)
    p.op("vector", lambda E: E.tensor_reduce(
        am[:, 0:nblk], src3, AX.X, ALU.max, apply_absolute_value=True),
        reads=(nm(src_t),), writes=(nm(am),))
    p.op("vector", lambda E: E.tensor_single_scalar(
        rsc[:, 0:nblk], am.ap()[:, 0:nblk].bitcast(I32), 23,
        ALU.logical_shift_right), reads=(nm(am),), writes=(nm(rsc),))
    p.op("vector", lambda E: E.tensor_scalar_max(
        rsc[:, 0:nblk], rsc[:, 0:nblk], 64),
        reads=(nm(rsc),), writes=(nm(rsc),))
    p.op("vector", lambda E: E.tensor_scalar(
        ssc[:, 0:nblk], rsc[:, 0:nblk], -6, 8388608, ALU.add, ALU.mult),
        reads=(nm(rsc),), writes=(nm(ssc),))
    p.op("vector", lambda E: E.tensor_scalar(
        rsc[:, 0:nblk], rsc[:, 0:nblk], -260, -8388608, ALU.add, ALU.mult),
        reads=(nm(rsc),), writes=(nm(rsc),))
    rb = rsc.ap()[:, 0:nblk].bitcast(F32).unsqueeze(2).broadcast_to(
        [128, nblk, 32])
    y3 = y.ap()[:, 0:n].rearrange("p (nb k) -> p nb k", k=32)
    p.op("vector", lambda E: E.tensor_tensor(y3, src3, rb, ALU.mult),
         reads=(nm(src_t), nm(rsc)), writes=(nm(y),))
    return p.op("vector", lambda E: E.tensor_scalar(
        r[:, 0:n], y[:, 0:n], MAGIC, MAGIC, ALU.add, ALU.subtract),
        reads=(nm(y),), writes=(nm(r),), inc=sem["dv"])


def emit_mx_q8(p, sem, bufs, r_sl, q8_t, q8_sl, inc=None):
    r = bufs["r"]
    return p.op("vector", lambda E: E.tensor_scalar(
        q8_t.ap()[:, q8_sl], r[:, r_sl], 127.0, None, ALU.min),
        reads=(nm(r),), writes=(nm(q8_t),), inc=inc)


def emit_mx_expand(p, sem, q8_t, q8_sl, ssc_t, ssc_sl, nblk, out_t, out_sl,
                   inc=None):
    q83 = q8_t.ap()[:, q8_sl].rearrange("p (nb k) -> p nb k", k=32)
    sc3 = ssc_t.ap()[:, ssc_sl].bitcast(F32).unsqueeze(2).broadcast_to(
        [128, nblk, 32])
    o3 = out_t.ap()[:, out_sl].rearrange("p (nb k) -> p nb k", k=32)
    return p.op("vector", lambda E: E.scalar_tensor_tensor(
        o3, q83, 0.0, sc3, ALU.add, ALU.mult),
        reads=(nm(q8_t), nm(ssc_t)), writes=(nm(out_t),), inc=inc)


def emit_ln_stats(p, sem, src_t, stat6, stat2):
    for j in range(4):
        p.op("vector", lambda E, j=j, src_t=src_t: E.bn_stats(
            stat6[:, 6 * j:6 * (j + 1)], src_t[:, 512 * j:512 * (j + 1)]),
            reads=(nm(src_t),), writes=("stat6",))
    return p.op("vector", lambda E: E.bn_aggr(stat2[:, :], stat6[:, :]),
                reads=("stat6",), writes=("stat2",), inc=sem["dv"])


def emit_rsig(p, sem, stat2, rsg, epsc, nhalf, gate_dv=0):
    if gate_dv:
        p.wait("scalar", sem["dv"], gate_dv)
    p.op("scalar", lambda E: E.activation(
        rsg[:, :], stat2[:, 1:2], ACTF.Ln, bias=epsc[:, 0:1], scale=1.0),
        reads=("stat2",), writes=("rsg",))
    return p.op("scalar", lambda E: E.activation(
        rsg[:, :], rsg[:, :], ACTF.Exp, bias=0.0, scale=nhalf[:, 0:1]),
        reads=("rsg",), writes=("rsg",), inc=sem["ac"])



def build_nc(phases=None):
    phases = PHASES if phases is None else phases
    scratch_kind = "ExternalOutput" if DBG_SCRATCH else "Internal"
    nc = bass.Bass("TRN2", target_bir_lowering=False)

    xin = nc.dram_tensor("xin", [S, DM], F32, kind="ExternalInput")
    wq = nc.dram_tensor("wq", [DM, DM], F32, kind="ExternalInput")
    wk = nc.dram_tensor("wk", [DM, 512], F32, kind="ExternalInput")
    wv = nc.dram_tensor("wv", [DM, 512], F32, kind="ExternalInput")
    wo = nc.dram_tensor("wo", [DM, DM], F32, kind="ExternalInput")
    w1 = nc.dram_tensor("w1", [HID, DM], F32, kind="ExternalInput")
    w2 = nc.dram_tensor("w2", [DM, HID], F32, kind="ExternalInput")
    lnww = nc.dram_tensor("lnww", [128, DM], F32, kind="ExternalInput")
    lnbb = nc.dram_tensor("lnbb", [128, DM], F32, kind="ExternalInput")
    b1r = nc.dram_tensor("b1r", [1, HID], F32, kind="ExternalInput")
    b2r = nc.dram_tensor("b2r", [1, DM], F32, kind="ExternalInput")
    cqd = nc.dram_tensor("cqd", [128, S], F32, kind="ExternalInput")
    sqd = nc.dram_tensor("sqd", [128, S], F32, kind="ExternalInput")
    ckd = nc.dram_tensor("ckd", [128, S], F32, kind="ExternalInput")
    skd = nc.dram_tensor("skd", [128, S], F32, kind="ExternalInput")
    permd = nc.dram_tensor("permd", [128, 128], BF16, kind="ExternalInput")
    identd = nc.dram_tensor("identd", [128, 128], BF16, kind="ExternalInput")
    maskd = nc.dram_tensor("maskd", [128, 128], F32, kind="ExternalInput")
    yout = nc.dram_tensor("yout", [S, DM], F32, kind="ExternalOutput")

    xscr = nc.dram_tensor("xscr", [S, DM], F32, kind=scratch_kind)
    w1t_s = nc.dram_tensor("w1t_s", [KC, 128, HID], BF16, kind=scratch_kind)
    w2t_s = nc.dram_tensor("w2t_s", [OC, 128, DM], BF16, kind=scratch_kind)
    q_s = nc.dram_tensor("q_s", [16, 128, S], BF16, kind=scratch_kind)
    k_s = nc.dram_tensor("k_s", [4, 128, S], BF16, kind=scratch_kind)
    v_s = nc.dram_tensor("v_s", [4, NTC, 128, 128], BF16, kind=scratch_kind)
    ao_s = nc.dram_tensor("ao_s", [16, 128, S], BF16, kind=scratch_kind)

    st = contextlib.ExitStack()
    sem = {k: Sem(st.enter_context(nc.semaphore("sem_" + k)))
           for k in ("dv", "ac", "pe")}
    dnames = ("blks", "blkp", "xt0", "xt1", "w10", "w11", "b10", "b11",
              "w20", "w21", "wc0", "wc1", "vwl", "qs0", "qs1", "kst", "vst",
              "spl", "vsp", "aosp", "ao0", "ao1", "xb0", "xb1", "wld0",
              "wld1", "wst0", "wst1")
    dsem = {k: Sem(st.enter_context(nc.semaphore("dma_" + k)))
            for k in dnames}
    dq = {"blkp": "gpsimd", "spl": "gpsimd", "vsp": "gpsimd",
          "aosp": "gpsimd", "wst0": "gpsimd", "wst1": "gpsimd",
          "vwl": "gpsimd", "wc0": "gpsimd", "wc1": "gpsimd",
          "b10": "gpsimd", "b11": "gpsimd"}

    def sbt(stack, name, shape, dtype):
        return stack.enter_context(nc.sbuf_tensor(name, shape, dtype))

    def pst(stack, name, shape, dtype):
        return stack.enter_context(nc.psum_tensor(name, shape, dtype))

    ident_b = sbt(st, "ident_b", [128, 128], BF16)
    perm_b = sbt(st, "perm_b", [128, 128], BF16)
    mask_b = sbt(st, "mask_b", [128, 128], F32)
    ones1 = sbt(st, "ones1", [1, 128], BF16)
    stat6 = sbt(st, "stat6", [128, 24], F32)
    stat2 = sbt(st, "stat2", [128, 2], F32)
    rsg = sbt(st, "rsg", [128, 1], F32)
    musig = sbt(st, "musig", [128, 2 * NTC], F32)
    zsum = sbt(st, "zsum", [128, 1], F32)
    rz = sbt(st, "rz", [128, 1], F32)
    negmax = sbt(st, "negmax", [128, 2], F32)
    epsc = sbt(st, "epsc", [128, 1], F32)
    nhalf = sbt(st, "nhalf", [128, 1], F32)

    io = dict(xin=xin, wq=wq, wk=wk, wv=wv, wo=wo, w1=w1, w2=w2,
              lnww=lnww, lnbb=lnbb, b1r=b1r, b2r=b2r, cqd=cqd, sqd=sqd,
              ckd=ckd, skd=skd, permd=permd, identd=identd, maskd=maskd,
              yout=yout, xscr=xscr, w1t_s=w1t_s, w2t_s=w2t_s, q_s=q_s,
              k_s=k_s, v_s=v_s, ao_s=ao_s)
    pers = dict(ident_b=ident_b, perm_b=perm_b, mask_b=mask_b, ones1=ones1,
                stat6=stat6, stat2=stat2, rsg=rsg, musig=musig, zsum=zsum,
                rz=rz, negmax=negmax, epsc=epsc, nhalf=nhalf)

    with st:
        p = Prog(nc)
        for dst, src in ((ident_b, identd), (perm_b, permd), (mask_b, maskd)):
            p.dma("sync", dst[:, :], src.ap(), dsem["blks"])
        p.wait("vector", dsem["blks"], dsem["blks"].n)
        p.op("vector", lambda E: E.memset(ones1[:, :], 1.0), writes=("ones1",))
        p.op("vector", lambda E: E.memset(epsc[:, :], EPS), writes=("epsc",))
        p.op("vector", lambda E: E.memset(nhalf[:, :], -0.5),
             writes=("nhalf",), inc=sem["dv"])
        p.wait("tensor", sem["dv"], sem["dv"].n)
        p.wait("scalar", sem["dv"], sem["dv"].n)
        with nc.Block() as blk:
            p.flush_into_block(blk)

        merge_wp = "wprep" in phases and "fa" in phases
        if "wprep" in phases and not merge_wp:
            _phase_wprep(nc, sem, dsem, sbt, pst, io, pers)
        if "fa" in phases:
            _phase_ffn(nc, sem, dsem, sbt, pst, io, pers, xin, xscr, "fa",
                       wprep_merge=merge_wp)
        else:
            p = Prog(nc)
            for tc in range(NTC):
                p.dma("sync", xscr.ap()[tc * 128:(tc + 1) * 128, :],
                      xin.ap()[tc * 128:(tc + 1) * 128, :], dsem["blks"])
            p.wait("sync", dsem["blks"], dsem["blks"].n)
            with nc.Block() as blk:
                p.flush_into_block(blk)
        if "attn" in phases:
            _phase_attn(nc, sem, dsem, sbt, pst, io, pers)
        if "fb" in phases:
            _phase_ffn(nc, sem, dsem, sbt, pst, io, pers, xscr, yout, "fb")
        else:
            p = Prog(nc)
            for tc in range(NTC):
                p.dma("sync", yout.ap()[tc * 128:(tc + 1) * 128, :],
                      xscr.ap()[tc * 128:(tc + 1) * 128, :], dsem["blks"])
            p.wait("sync", dsem["blks"], dsem["blks"].n)
            with nc.Block() as blk:
                p.flush_into_block(blk)

        p = Prog(nc)
        for k in dnames:
            p.wait(dq.get(k, "sync"), dsem[k], dsem[k].n)
        with nc.Block() as blk:
            p.flush_into_block(blk)

    nc.finalize()
    return nc



def _phase_wprep(nc, sem, dsem, sbt, pst, io, pers):
    st = contextlib.ExitStack()
    wld = [sbt(st, "wp_ld%d" % i, [128, HID], F32) for i in range(2)]
    qbufs = dict(am=sbt(st, "wp_am", [128, 256], F32),
                 rsc=sbt(st, "wp_rsc", [128, 256], I32),
                 ssc=sbt(st, "wp_ssc", [128, 256], I32),
                 y=sbt(st, "wp_y", [128, HID], F32),
                 r=sbt(st, "wp_r", [128, HID], F32))
    q8 = sbt(st, "wp_q8", [128, HID], I8)
    qnat = sbt(st, "wp_qn", [128, HID], BF16)
    wfm = [sbt(st, "wp_fm%d" % i, [128, HID], BF16) for i in range(2)]
    tp = [pst(st, "wp_tp%d" % i, [128, 1024], BF16) for i in range(2)]
    ident_b = pers["ident_b"]

    p = Prog(nc)
    q_done = {}
    stg = {}
    specs = []
    for rc in range(HID // 128):
        specs.append((io["w1"], io["w1t_s"], rc, DM, 4, HID))
    for rc in range(DM // 128):
        specs.append((io["w2"], io["w2t_s"], rc, HID, 8, DM))

    with st:
        for g, (wt, scr, rc, ncols, nsplit, scr_cols) in enumerate(specs):
            buf = wld[g % 2]
            fmb = wfm[g % 2]
            sld = dsem["wld%d" % (g % 2)]
            sst = dsem["wst%d" % (g % 2)]
            if g >= 2:
                p.wait("sync", sem["dv"], q_done[g - 2])
            c_ld = p.dma("sync", buf[:, 0:ncols],
                         wt.ap()[rc * 128:(rc + 1) * 128, :], sld)
            p.wait("vector", sld, c_ld)
            nblk = ncols // 32
            q_done[g] = emit_mx_quant_core(p, sem, buf, slice(0, ncols), nblk,
                                           qbufs)
            emit_mx_q8(p, sem, qbufs, slice(0, ncols), q8, slice(0, ncols))
            c_qn = emit_mx_expand(p, sem, q8, slice(0, ncols), qbufs["ssc"],
                                  slice(0, nblk), nblk, qnat, slice(0, ncols),
                                  inc=sem["dv"])
            p.wait("tensor", sem["dv"], c_qn)
            if g >= 2:
                p.wait("scalar", sst, stg[g - 2])
            ntp = ncols // 128
            evs = {}
            for g4 in range(ntp // 4):
                t = tp[g4 % 2]
                if g4 >= 2:
                    p.wait("tensor", sem["ac"], evs[g4 - 2])
                c_pe = 0
                for u in range(4):
                    j = g4 * 4 + u
                    c_pe = p.op("tensor", lambda E, j=j, t=t, u=u:
                                E.transpose(t[:, u * 128:(u + 1) * 128],
                                            qnat[:, j * 128:(j + 1) * 128],
                                            ident_b[:, :]), inc=sem["pe"])
                p.wait("scalar", sem["pe"], c_pe)
                evs[g4] = p.op("scalar", lambda E, g4=g4, t=t, fmb=fmb:
                               E.copy(fmb[:, g4 * 512:(g4 + 1) * 512],
                                      t[:, 0:512]),
                               reads=(nm(t),), writes=(nm(fmb),),
                               inc=sem["ac"])
            p.wait("vector", sem["pe"], sem["pe"].n)
            p.wait("gpsimd", sem["ac"], sem["ac"].n)
            ngrp = ntp // nsplit
            c_st = 0
            for sp_i in range(nsplit):
                out_ap = bass.AP(
                    scr, sp_i * ngrp * 128 * scr_cols + rc * 128,
                    [[scr_cols, 128], [128 * scr_cols, ngrp], [1, 128]])
                in_ap = fmb.ap().rearrange("p (c o) -> p c o", o=128)[
                    :, sp_i * ngrp:(sp_i + 1) * ngrp, :]
                c_st = p.dma("gpsimd", out_ap, in_ap, sst)
            stg[g] = c_st
        p.wait("sync", dsem["wst0"], dsem["wst0"].n)
        p.wait("sync", dsem["wst1"], dsem["wst1"].n)
        with nc.Block() as blk:
            p.flush_into_block(blk)



def _phase_ffn(nc, sem, dsem, sbt, pst, io, pers, src, dst, tag,
               wprep_merge=False):
    st = contextlib.ExitStack()
    xt = [sbt(st, tag + "_x%d" % i, [128, DM], F32) for i in range(2)]
    lnw_t = sbt(st, tag + "_lnw", [128, DM], F32)
    lnb_t = sbt(st, tag + "_lnb", [128, DM], F32)
    qa_fm = sbt(st, tag + "_qa", [128, KC * TBLK], BF16)
    q8g = [sbt(st, tag + "_q8_%d" % i, [128, HID], I8) for i in range(SWTC)]
    sscg = [sbt(st, tag + "_sg%d" % i, [128, HID // 32], I32)
            for i in range(SWTC)]
    xbb = sbt(st, tag + "_xbb", [128, DM], BF16)
    ntile = sbt(st, tag + "_n", [128, DM], BF16)
    qbufs = dict(am=sbt(st, tag + "_am", [128, 64], F32),
                 rsc=sbt(st, tag + "_rsc", [128, 64], I32),
                 ssc=sbt(st, tag + "_ssc", [128, 64], I32),
                 y=sbt(st, tag + "_y", [128, DM], BF16),
                 r=sbt(st, tag + "_r", [128, DM], F32))
    t1 = qbufs["r"]
    q8a = sbt(st, tag + "_q8a", [128, DM], I8)
    w1sl = [sbt(st, tag + "_w1s%d" % i, [128, KC * OSL], BF16)
            for i in range(2)]
    b1sl = [sbt(st, tag + "_b1s%d" % i, [1, OSL], BF16) for i in range(2)]
    b2sl = sbt(st, tag + "_b2s", [1, DM], BF16)
    gcat = [sbt(st, tag + "_g%d" % i, [128, SWTC * OSL], BF16)
            for i in range(2)]
    h1b = [sbt(st, tag + "_h1b%d" % i, [128, OSL], BF16) for i in range(4)]
    gq = qbufs
    w2sl = [sbt(st, tag + "_w2s%d" % i, [128, DM], BF16) for i in range(2)]
    qgtm = [sbt(st, tag + "_qgtm%d" % i, [128, 128], BF16) for i in range(2)]
    h2b = sbt(st, tag + "_h2b", [128, DM], BF16)
    fb = xbb

    psA = pst(st, tag + "_psA", [128, DM], F32)
    psB = pst(st, tag + "_psB", [128, DM], F32)
    ps_h1 = [psB.ap()[:, 0:512], psB.ap()[:, 512:1024]]
    ps_tp = [psB.ap()[:, 1024:1536].bitcast(BF16),
             psB.ap()[:, 1536:2048].bitcast(BF16)]
    ps_h2 = psA

    ident_b = pers["ident_b"]
    ones1 = pers["ones1"]
    stat6, stat2, rsg, musig = (pers["stat6"], pers["stat2"], pers["rsg"],
                                pers["musig"])
    epsc, nhalf = pers["epsc"], pers["nhalf"]
    p = Prog(nc)

    wp_on = wprep_merge and tag == "fa"
    wp_stg = {}
    wp_tp_hist = []
    if wp_on:
        wld = [lnw_t, lnb_t]
        wfm = [xbb, h2b]
        wp_specs = []
        for rc in range(HID // 128):
            wp_specs.append((io["w1"].ap()[rc * 128:(rc + 1) * 128, :],
                             io["w1t_s"], HID, 0, rc))
        for k in range(64):
            cc, rr = k // 16, k % 16
            wp_specs.append((io["w2"].ap()[rr * 128:(rr + 1) * 128,
                                           cc * DM:(cc + 1) * DM],
                             io["w2t_s"], DM, 16 * cc, rr))
        wp_order = []
        for j in range(32):
            wp_order += [2 * j, 2 * j + 1, 64 + 2 * j, 64 + 2 * j + 1]
        wp_qdone = {}
        wp_nt_pe = {0: 0, 1: 0}

        def emit_wp_spec(g):
            src_ap, scr, scr_cols, blkbase, rc = wp_specs[wp_order[g]]
            buf = wld[g % 2]
            fmb = wfm[g % 2]
            sld = dsem["wld%d" % (g % 2)]
            sst = dsem["wst%d" % (g % 2)]
            if g >= 2:
                p.wait("sync", sem["dv"], wp_qdone[g - 2])
            else:
                p.wait("sync", sem["dv"], sem["dv"].n)
            c_ld = p.dma("sync", buf[:, :], src_ap, sld)
            p.wait("vector", sld, c_ld)
            wpbufs = dict(am=qbufs["am"], rsc=qbufs["rsc"],
                          ssc=qbufs["ssc"], y=buf, r=qbufs["r"])
            wp_qdone[g] = emit_mx_quant_core(p, sem, buf, slice(0, DM), 64,
                                             wpbufs)
            emit_mx_q8(p, sem, wpbufs, slice(0, DM), q8a, slice(0, DM))
            buf2 = [ntile, qbufs["y"]][g % 2]
            p.wait("vector", sem["pe"], wp_nt_pe[g % 2])
            c_qn = emit_mx_expand(p, sem, q8a, slice(0, DM), qbufs["ssc"],
                                  slice(0, 64), 64, buf2, slice(0, DM),
                                  inc=sem["dv"])
            p.wait("tensor", sem["dv"], c_qn)
            if g >= 2:
                p.wait("scalar", sst, wp_stg[g - 2])
            c_ev = 0
            c_pe = 0
            for g4 in range(4):
                t = ps_tp[g4 % 2]
                if len(wp_tp_hist) >= 2:
                    p.wait("tensor", sem["ac"], wp_tp_hist[-2])
                for u in range(4):
                    jj = g4 * 4 + u
                    c_pe = p.op("tensor", lambda E, jj=jj, t=t, u=u,
                                buf2=buf2:
                                E.transpose(t[:, u * 128:(u + 1) * 128],
                                            buf2[:, jj * 128:(jj + 1) * 128],
                                            ident_b[:, :]), inc=sem["pe"])
                p.wait("scalar", sem["pe"], c_pe)
                c_ev = p.op("scalar", lambda E, g4=g4, t=t, fmb=fmb:
                            E.copy(fmb[:, g4 * 512:(g4 + 1) * 512],
                                   t[:, 0:512]),
                            reads=(nm(t),), writes=(nm(fmb),),
                            inc=sem["ac"])
                wp_tp_hist.append(c_ev)
            wp_nt_pe[g % 2] = c_pe
            p.wait("gpsimd", sem["ac"], c_ev)
            c_st = 0
            for sp_i in range(4):
                out_ap = bass.AP(
                    scr, (blkbase + sp_i * 4) * 128 * scr_cols + rc * 128,
                    [[scr_cols, 128], [128 * scr_cols, 4], [1, 128]])
                in_ap = fmb.ap().rearrange("p (c o) -> p c o", o=128)[
                    :, sp_i * 4:(sp_i + 1) * 4, :]
                c_st = p.dma("gpsimd", out_ap, in_ap, sst)
            wp_stg[g] = c_st

    with st:
        c_lnw = p.dma("sync", lnw_t[:, :], io["lnww"].ap(), dsem["blks"])
        c_lnb = p.dma("sync", lnb_t[:, :], io["lnbb"].ap(), dsem["blks"])
        c_b2 = p.dma("gpsimd", b2sl[:, :], io["b2r"].ap(), dsem["blkp"])
        p.wait("vector", dsem["blks"], c_lnb)
        p.wait("tensor", dsem["blkp"], c_b2)

        xt_gate = [[], []]
        rsg_gate = 0
        f1_mm_last = 0
        f2_pe_last = 0
        f2_done_prev = 0
        tp_used = {}
        w2_gate = {}

        for sw in range(NSW):
            tcs = [sw * SWTC + i for i in range(SWTC)]
            for i, tc in enumerate(tcs):
                xb = xt[i % 2]
                sx = dsem["xt%d" % (i % 2)]
                if wp_on and sw == 1 and i == 0:
                    p.wait("sync", sem["dv"], wp_qdone[126])
                    p.wait("sync", sem["dv"], wp_qdone[127])
                    p.dma("sync", lnw_t[:, :], io["lnww"].ap(),
                          dsem["blks"])
                    c_rl = p.dma("sync", lnb_t[:, :], io["lnbb"].ap(),
                                 dsem["blks"])
                    p.wait("vector", dsem["blks"], c_rl)
                for s_, v_ in xt_gate[i % 2]:
                    p.wait("sync", s_, v_)
                c_x = p.dma("sync", xb[:, :],
                            src.ap()[tc * 128:(tc + 1) * 128, :], sx)
                xnm = nm(xb)
                p.wait("vector", sx, c_x)
                c_st = emit_ln_stats(p, sem, xb, stat6, stat2)
                c_rs = emit_rsig(p, sem, stat2, rsg, epsc, nhalf,
                                 gate_dv=max(c_st, rsg_gate))
                p.wait("vector", sem["ac"], c_rs)
                p.op("vector", lambda E, tc=tc: E.tensor_copy(
                    musig[:, 2 * tc:2 * tc + 1], stat2[:, 0:1]),
                    reads=("stat2",), writes=("musig",))
                p.op("vector", lambda E, tc=tc: E.tensor_copy(
                    musig[:, 2 * tc + 1:2 * tc + 2], rsg[:, :]),
                    reads=("rsg",), writes=("musig",))
                c_xbb = p.op("vector", lambda E, xb=xb, tc=tc: E.tensor_scalar(
                    xbb[:, :], xb[:, :], musig[:, 2 * tc:2 * tc + 1],
                    musig[:, 2 * tc + 1:2 * tc + 2], ALU.subtract, ALU.mult),
                    reads=(xnm, "musig"), writes=(nm(xbb),), inc=sem["dv"])
                xt_gate[i % 2] = [(sem["dv"], c_xbb)]
                c_st2 = emit_ln_stats(p, sem, xbb, stat6, stat2)
                c_rs2 = emit_rsig(p, sem, stat2, rsg, epsc, nhalf,
                                  gate_dv=c_st2)
                p.wait("vector", sem["ac"], c_rs2)
                rsg_gate = p.op("vector", lambda E: E.tensor_scalar(
                    t1[:, :], xbb[:, :], stat2[:, 0:1], rsg[:, 0:1],
                    ALU.subtract, ALU.mult),
                    reads=(nm(xbb), "stat2", "rsg"), writes=(nm(t1),),
                    inc=sem["dv"])
                p.op("vector", lambda E: E.tensor_tensor(
                    t1[:, :], t1[:, :], lnw_t[:, :], ALU.mult),
                    reads=(nm(t1), nm(lnw_t)), writes=(nm(t1),))
                p.op("vector", lambda E: E.tensor_tensor(
                    ntile[:, :], t1[:, :], lnb_t[:, :], ALU.add),
                    reads=(nm(t1), nm(lnb_t)), writes=(nm(ntile),))
                emit_mx_quant_core(p, sem, ntile, slice(0, DM), 64, qbufs)
                emit_mx_q8(p, sem, qbufs, slice(0, DM), q8a, slice(0, DM))
                c_qa = emit_mx_expand(p, sem, q8a, slice(0, DM),
                                      qbufs["ssc"], slice(0, 64), 64,
                                      ntile, slice(0, DM), inc=sem["dv"])
                p.wait("tensor", sem["dv"], c_qa)
                if i == 0 and f2_pe_last:
                    p.wait("scalar", sem["pe"], f2_pe_last)
                    p.wait("tensor", sem["ac"], f2_done_prev)
                if wp_on and sw == 1 and i == 0:
                    p.wait("vector", sem["pe"], wp_nt_pe[0])
                    p.wait("vector", sem["pe"], wp_nt_pe[1])
                evs = {}
                for g4 in range(KC // 4):
                    t = ps_tp[g4 % 2]
                    if g4 >= 2:
                        p.wait("tensor", sem["ac"], evs[g4 - 2])
                    c_pe = 0
                    for u in range(4):
                        kc = g4 * 4 + u
                        c_pe = p.op("tensor", lambda E, kc=kc, t=t, u=u:
                                    E.transpose(
                                        t[:, u * 128:(u + 1) * 128],
                                        ntile[:, kc * 128:(kc + 1) * 128],
                                        ident_b[:, :]), inc=sem["pe"])
                    p.wait("scalar", sem["pe"], c_pe)
                    out3 = bass.AP(
                        qa_fm, g4 * 4 * TBLK + i * 128,
                        [[KC * TBLK, 128], [TBLK, 4], [1, 128]])
                    in3 = t[:, 0:512].rearrange("p (f c) -> p f c", c=128)
                    evs[g4] = p.op("scalar", lambda E, out3=out3, in3=in3:
                                   E.copy(out3, in3),
                                   reads=(nm(t),), writes=(nm(qa_fm),),
                                   inc=sem["ac"])
                    if wp_on and sw == 0:
                        wp_tp_hist.append(evs[g4])
                p.wait("vector", sem["pe"], sem["pe"].n)
            c_qa_all = sem["ac"].n
            if wp_on and sw == 0:
                wp_nt_pe[0] = sem["pe"].n

            p.wait("tensor", sem["ac"], c_qa_all)
            mm_last = {}
            gq_done = {}
            h1_free = {}
            for osl in range(NOSL):
                wb = w1sl[osl % 2]
                bb1 = b1sl[osl % 2]
                sw1 = dsem["w1%d" % (osl % 2)]
                sb1 = dsem["b1%d" % (osl % 2)]
                if wp_on and sw == 0 and osl == 0:
                    for _k in range(4):
                        emit_wp_spec(_k)
                if osl >= 2:
                    p.wait("sync", sem["pe"], mm_last[osl - 2])
                    p.wait("gpsimd", sem["pe"], mm_last[osl - 2])
                if wp_on and sw == 0:
                    p.wait("sync", dsem["wst0"], wp_stg[4 * osl])
                    p.wait("sync", dsem["wst1"], wp_stg[4 * osl + 1])
                in_ap = bass.AP(io["w1t_s"], osl * OSL,
                                [[HID, 128], [128 * HID, KC], [1, OSL]])
                out_ap = wb.ap().rearrange("p (kc c) -> p kc c", c=OSL)
                c_w = 0
                for hh in range(2):
                    c_w = p.dma("sync", out_ap[:, hh * 8:(hh + 1) * 8, :],
                                in_ap[:, hh * 8:(hh + 1) * 8, :], sw1)
                c_b1 = p.dma("gpsimd", bb1[:, :],
                             io["b1r"].ap()[:, osl * OSL:(osl + 1) * OSL],
                             sb1)
                p.wait("tensor", sw1, c_w)
                p.wait("tensor", sb1, c_b1)
                gc = gcat[osl % 2]
                if osl >= 2:
                    p.wait("scalar", sem["dv"], gq_done[osl - 2])
                for i in range(SWTC):
                    idx = osl * SWTC + i
                    ph = ps_h1[idx % 2]
                    gate = h1_free.get(idx - 2)
                    if gate:
                        p.wait("tensor", sem["ac"], gate)
                    for kc in range(KC):
                        p.op("tensor", lambda E, kc=kc, i=i, ph=ph, wb=wb:
                             E.matmul(ph[:, 0:OSL],
                                      qa_fm[:, kc * TBLK + i * 128:
                                            kc * TBLK + (i + 1) * 128],
                                      wb[:, kc * OSL:(kc + 1) * OSL],
                                      start=(kc == 0), stop=False))
                    c_mm = p.op("tensor", lambda E, ph=ph, bb1=bb1: E.matmul(
                        ph[:, 0:OSL], ones1[:, :], bb1[:, :],
                        start=False, stop=True), inc=sem["pe"])
                    mm_last[osl] = c_mm
                    hb = h1b[idx % 4]
                    p.wait("scalar", sem["pe"], c_mm)
                    p.op("scalar", lambda E, hb=hb, ph=ph: E.copy(
                        hb[:, :], ph[:, 0:OSL]), reads=(nm(ph),),
                        writes=(nm(hb),))
                    c_gl = p.op("scalar", lambda E, hb=hb, gc=gc, i=i:
                                E.activation(gc[:, i * OSL:(i + 1) * OSL],
                                             hb[:, :], ACTF.Gelu_apprx_tanh,
                                             bias=0.0, scale=1.0),
                                reads=(nm(hb),), writes=(nm(gc),),
                                inc=sem["ac"])
                    h1_free[idx] = c_gl
                    if wp_on and sw == 0 and osl < NOSL - 1:
                        if i == 3:
                            emit_wp_spec(4 * (osl + 1))
                            emit_wp_spec(4 * (osl + 1) + 1)
                        elif i == SWTC - 1:
                            emit_wp_spec(4 * (osl + 1) + 2)
                            emit_wp_spec(4 * (osl + 1) + 3)
                p.wait("vector", sem["ac"], sem["ac"].n)
                emit_mx_quant_core(p, sem, gc, slice(0, SWTC * OSL), 64, gq)
                nbo = OSL // 32
                for i in range(SWTC):
                    emit_mx_q8(p, sem, gq, slice(i * OSL, (i + 1) * OSL),
                               q8g[i], slice(osl * OSL, (osl + 1) * OSL))
                    p.op("vector", lambda E, i=i, osl=osl: E.tensor_copy(
                        sscg[i][:, osl * nbo:(osl + 1) * nbo],
                        gq["ssc"][:, i * nbo:(i + 1) * nbo]),
                        reads=(nm(gq["ssc"]),), writes=(nm(sscg[i]),))
                gq_done[osl] = p.op("vector", lambda E: E.drain(),
                                    inc=sem["dv"])
            f1_mm_last = mm_last[NOSL - 1]

            h2_free = f2_done_prev
            for pr in range(SWTC // 2):
                i0, i1 = 2 * pr, 2 * pr + 1
                tc0, tc1 = tcs[i0], tcs[i1]
                if pr == 0:
                    p.wait("scalar", sem["pe"], f1_mm_last)
                else:
                    p.wait("scalar", sem["pe"], f2_pe_last)
                stg_ev = {}
                for g in range(32):
                    t = ps_tp[g % 2]
                    if g >= 2:
                        p.wait("tensor", sem["ac"], stg_ev[g - 2])
                    elif wp_on and sw == 0 and pr == 0 and wp_tp_hist:
                        p.wait("tensor", sem["ac"], wp_tp_hist[-2 + g])
                    c_tp = 0
                    for u in range(4):
                        oc = g * 2 + u // 2
                        ch = u % 2
                        qt = qgtm[u % 2]
                        gate = tp_used.get(u % 2)
                        if gate is not None:
                            p.wait("vector", sem["pe"], gate)
                        c_ex = emit_mx_expand(
                            p, sem, q8g[2 * pr + ch],
                            slice(oc * 128, (oc + 1) * 128),
                            sscg[2 * pr + ch],
                            slice(oc * 4, (oc + 1) * 4),
                            4, qt, slice(0, 128), inc=sem["dv"])
                        p.wait("tensor", sem["dv"], c_ex)
                        c_tp = p.op("tensor", lambda E, t=t, qt=qt, u=u:
                                    E.transpose(t[:, u * 128:(u + 1) * 128],
                                                qt[:, 0:128], ident_b[:, :]),
                                    inc=sem["pe"])
                        tp_used[u % 2] = c_tp
                    p.wait("scalar", sem["pe"], c_tp)
                    stg_ev[g] = p.op(
                        "scalar", lambda E, g=g, t=t: E.copy(
                            qa_fm[:, g * 512:(g + 1) * 512], t[:, 0:512]),
                        reads=(nm(t),), writes=(nm(qa_fm),), inc=sem["ac"])
                c_stage = sem["ac"].n
                p.wait("tensor", sem["ac"], c_stage)
                if h2_free:
                    p.wait("tensor", sem["ac"], h2_free)
                for oc in range(OC):
                    wb2 = w2sl[oc % 2]
                    sw2 = dsem["w2%d" % (oc % 2)]
                    gate = w2_gate.get(oc % 2)
                    if gate is not None:
                        p.wait("sync", sem["pe"], gate)
                    if wp_on and sw == 0:
                        cgrp = oc // 16
                        p.wait("sync", dsem["wst0"],
                               wp_stg[4 * (8 * cgrp + 7) + 2])
                        p.wait("sync", dsem["wst1"],
                               wp_stg[4 * (8 * cgrp + 7) + 3])
                    c_w2 = p.dma("sync", wb2[:, :],
                                 bass.AP(io["w2t_s"], oc * 128 * DM,
                                         [[DM, 128], [1, DM]]), sw2)
                    p.wait("tensor", sw2, c_w2)
                    for ch, psX in ((0, psA), (1, psB)):
                        for ds in range(4):
                            p.op("tensor", lambda E, ds=ds, oc=oc, ch=ch,
                                 psX=psX, wb2=wb2: E.matmul(
                                     psX[:, ds * 512:(ds + 1) * 512],
                                     qa_fm[:, oc * 256 + ch * 128:
                                           oc * 256 + (ch + 1) * 128],
                                     wb2[:, ds * 512:(ds + 1) * 512],
                                     start=(oc == 0), stop=False))
                    c_h2m = p.op("tensor", lambda E: E.drain(), inc=sem["pe"])
                    w2_gate[oc % 2] = c_h2m
                for ch, psX in ((0, psA), (1, psB)):
                    for hh in range(4):
                        c_b = p.op("tensor", lambda E, hh=hh, psX=psX:
                                   E.matmul(psX[:, hh * 512:(hh + 1) * 512],
                                            ones1[:, :],
                                            b2sl[:, hh * 512:(hh + 1) * 512],
                                            start=False, stop=True),
                                   inc=sem["pe"])
                f2_pe_last = c_b
                h2_frees = []
                h2b_rd = 0
                if wp_on and sw == 0 and pr == 0:
                    p.wait("scalar", dsem["wst1"], wp_stg[4 * 31 + 3])
                    p.wait("vector", dsem["wst0"], wp_stg[4 * 31 + 2])
                for ch, psX, i, tc in ((0, psA, i0, tc0), (1, psB, i1, tc1)):
                    p.wait("scalar", sem["pe"], c_b)
                    if h2b_rd:
                        p.wait("scalar", sem["dv"], h2b_rd)
                    c_h2b = p.op("scalar", lambda E, psX=psX: E.copy(
                        h2b[:, :], psX[:, :]),
                        reads=(nm(psX),), writes=(nm(h2b),), inc=sem["ac"])
                    h2_frees.append(c_h2b)
                    hsrc = h2b
                    xb = xt[i % 2]
                    sx = dsem["xt%d" % (i % 2)]
                    for s_, v_ in xt_gate[i % 2]:
                        p.wait("sync", s_, v_)
                    c_x2 = p.dma("sync", xb[:, :],
                                 src.ap()[tc * 128:(tc + 1) * 128, :], sx)
                    p.wait("vector", sx, c_x2)
                    p.wait("vector", sem["ac"], c_h2b)
                    xnm = nm(xb)
                    p.op("vector", lambda E, tc=tc, xb=xb: E.tensor_scalar(
                        fb[:, :], xb[:, :], musig[:, 2 * tc:2 * tc + 1],
                        musig[:, 2 * tc + 1:2 * tc + 2], ALU.subtract,
                        ALU.mult),
                        reads=(xnm, "musig"), writes=(nm(fb),))
                    h2b_rd = p.op("vector", lambda E, hsrc=hsrc:
                                  E.tensor_tensor(
                                      fb[:, :], fb[:, :], hsrc[:, :],
                                      ALU.add),
                                  reads=(nm(fb), nm(hsrc)),
                                  writes=(nm(fb),), inc=sem["dv"])
                    c_xu = p.op("vector", lambda E, xb=xb:
                                E.scalar_tensor_tensor(
                                    xb[:, :], fb[:, :], 0.5, xb[:, :],
                                    ALU.mult, ALU.add),
                                reads=(nm(fb), xnm), writes=(xnm,),
                                inc=sem["dv"])
                    p.wait("sync", sem["dv"], c_xu)
                    c_ws = p.dma("sync",
                                 dst.ap()[tc * 128:(tc + 1) * 128, :],
                                 xb[:, :], sx)
                    xt_gate[i % 2] = [(sx, c_ws)]
                f2_done_prev = max(h2_frees)
                h2_free = f2_done_prev

        for k in ("xt0", "xt1"):
            p.wait("sync", dsem[k], dsem[k].n)
        with nc.Block() as blk:
            p.flush_into_block(blk)



def _phase_attn(nc, sem, dsem, sbt, pst, io, pers):
    ident_b = pers["ident_b"]
    perm_b = pers["perm_b"]
    mask_b = pers["mask_b"]
    stat6, stat2, rsg = pers["stat6"], pers["stat2"], pers["rsg"]
    zsum, rz, negmax = pers["zsum"], pers["rz"], pers["negmax"]
    epsc, nhalf = pers["epsc"], pers["nhalf"]
    xscr = io["xscr"]
    p = Prog(nc)

    stA = contextlib.ExitStack()
    a_fm = sbt(stA, "a_fm", [128, KC * S], BF16)
    xbuf = [sbt(stA, "at_x%d" % i, [128, DM], F32) for i in range(2)]
    atile = sbt(stA, "at_t", [128, DM], BF16)
    wcol = [sbt(stA, "pj_w%d" % i, [128, KC * 128], BF16) for i in range(2)]
    vwb = sbt(stA, "pj_vw", [128, KC * 512], BF16)
    pout = sbt(stA, "pj_o", [128, S], BF16)
    vout = sbt(stA, "pj_v", [128, 512], BF16)
    ps_tp = [pst(stA, "at_tp%d" % i, [128, 1024], BF16) for i in range(2)]
    ps_p = pst(stA, "pj_ps", [128, 512], F32)
    ps_v = pst(stA, "pj_pv", [128, 512], F32)

    with stA:
        c_at_prev = 0
        xb_gate = [0, 0]
        for tc in range(NTC):
            xb = xbuf[tc % 2]
            sx = dsem["xt%d" % (tc % 2)]
            if xb_gate[tc % 2]:
                p.wait("sync", sem["dv"], xb_gate[tc % 2])
            c_x = p.dma("sync", xb[:, :],
                        xscr.ap()[tc * 128:(tc + 1) * 128, :], sx)
            xnm = nm(xb)
            p.wait("vector", sx, c_x)
            c_st = emit_ln_stats(p, sem, xb, stat6, stat2)
            c_rs = emit_rsig(p, sem, stat2, rsg, epsc, nhalf,
                             gate_dv=max(c_st, c_at_prev))
            p.wait("vector", sem["ac"], c_rs)
            c_at = p.op("vector", lambda E, xb=xb: E.tensor_scalar(
                atile[:, :], xb[:, :], stat2[:, 0:1], rsg[:, 0:1],
                ALU.subtract, ALU.mult), reads=(xnm, "stat2", "rsg"),
                writes=(nm(atile),), inc=sem["dv"])
            c_at_prev = c_at
            xb_gate[tc % 2] = c_at
            p.wait("tensor", sem["dv"], c_at)
            evs = {}
            for g4 in range(KC // 4):
                t = ps_tp[g4 % 2]
                if g4 >= 2:
                    p.wait("tensor", sem["ac"], evs[g4 - 2])
                c_pe = 0
                for u in range(4):
                    kc = g4 * 4 + u
                    c_pe = p.op("tensor", lambda E, kc=kc, t=t, u=u:
                                E.transpose(t[:, u * 128:(u + 1) * 128],
                                            atile[:, kc * 128:(kc + 1) * 128],
                                            ident_b[:, :]), inc=sem["pe"])
                p.wait("scalar", sem["pe"], c_pe)
                out3 = bass.AP(a_fm, g4 * 4 * S + tc * 128,
                               [[KC * S, 128], [S, 4], [1, 128]])
                in3 = t.ap()[:, 0:512].rearrange("p (f c) -> p f c", c=128)
                evs[g4] = p.op("scalar", lambda E, out3=out3, in3=in3:
                               E.copy(out3, in3),
                               reads=(nm(t),), writes=(nm(a_fm),),
                               inc=sem["ac"])
            p.wait("vector", sem["pe"], sem["pe"].n)
        c_afm = sem["ac"].n
        p.wait("tensor", sem["ac"], c_afm)

        proj_fin = {}
        spl_prev = 0
        for sidx in range(20):
            wb = wcol[sidx % 2]
            swc = dsem["wc%d" % (sidx % 2)]
            gate = proj_fin.get(sidx - 2)
            if gate:
                p.wait("gpsimd", sem["pe"], gate)
            if sidx < 16:
                wt, col0, spill, sp_i = io["wq"], sidx * 128, io["q_s"], sidx
                wcols = 2048
            else:
                wt, col0, spill, sp_i = (io["wk"], (sidx - 16) * 128,
                                         io["k_s"], sidx - 16)
                wcols = 512
            in_ap = bass.AP(wt, col0,
                            [[wcols, 128], [128 * wcols, KC], [1, 128]])
            out_ap = wb.ap().rearrange("p (kc c) -> p kc c", c=128)
            c_wl = p.dma("gpsimd", out_ap, in_ap, swc)
            p.wait("tensor", swc, c_wl)
            for ts in range(4):
                if spl_prev:
                    p.wait("scalar", dsem["spl"], spl_prev)
                for kc in range(KC):
                    p.op("tensor", lambda E, kc=kc, ts=ts, wb=wb: E.matmul(
                        ps_p[:, :], wb[:, kc * 128:(kc + 1) * 128],
                        a_fm[:, kc * S + ts * 512:kc * S + (ts + 1) * 512],
                        start=(kc == 0), stop=(kc == KC - 1)))
                c_mm = p.op("tensor", lambda E: E.drain(), inc=sem["pe"])
                p.wait("scalar", sem["pe"], c_mm)
                c_ev = p.op("scalar", lambda E, ts=ts: E.copy(
                    pout[:, ts * 512:(ts + 1) * 512], ps_p[:, :]),
                    reads=(nm(ps_p),), writes=(nm(pout),), inc=sem["ac"])
                p.wait("tensor", sem["ac"], c_ev)
            proj_fin[sidx] = sem["pe"].n
            p.wait("gpsimd", sem["ac"], sem["ac"].n)
            spl_prev = p.dma("gpsimd",
                             bass.AP(spill, sp_i * 128 * S,
                                     [[S, 128], [1, S]]),
                             pout[:, :], dsem["spl"])
        in_ap = bass.AP(io["wv"], 0, [[512, 128], [128 * 512, KC], [1, 512]])
        c_vw = p.dma("gpsimd",
                     vwb.ap().rearrange("p (kc c) -> p kc c", c=512),
                     in_ap, dsem["vwl"])
        p.wait("tensor", dsem["vwl"], c_vw)
        vsp_prev = 0
        for tc in range(NTC):
            for kc in range(KC):
                p.op("tensor", lambda E, kc=kc, tc=tc: E.matmul(
                    ps_v[:, :],
                    a_fm[:, kc * S + tc * 128:kc * S + (tc + 1) * 128],
                    vwb[:, kc * 512:(kc + 1) * 512],
                    start=(kc == 0), stop=(kc == KC - 1)))
            c_mm = p.op("tensor", lambda E: E.drain(), inc=sem["pe"])
            p.wait("scalar", sem["pe"], c_mm)
            if vsp_prev:
                p.wait("scalar", dsem["vsp"], vsp_prev)
            c_ev = p.op("scalar", lambda E: E.copy(vout[:, :], ps_v[:, :]),
                        reads=(nm(ps_v),), writes=(nm(vout),), inc=sem["ac"])
            p.wait("tensor", sem["ac"], c_ev)
            p.wait("gpsimd", sem["ac"], c_ev)
            out_ap = bass.AP(io["v_s"], tc * 128 * 128,
                             [[128, 128], [NTC * 128 * 128, 4], [1, 128]])
            vsp_prev = p.dma("gpsimd", out_ap,
                             vout.ap().rearrange("p (g c) -> p g c", c=128),
                             dsem["vsp"])
        p.wait("sync", dsem["vsp"], dsem["vsp"].n)
        p.wait("sync", dsem["spl"], dsem["spl"].n)

    stC = contextlib.ExitStack()
    tabs = {name: sbt(stC, "tab_" + name, [128, S], F32)
            for name in ("cq", "sq", "ck", "sk")}
    kst = sbt(stC, "sc_k", [128, S], BF16)
    krot = sbt(stC, "sc_kr", [128, S], BF16)
    vst = sbt(stC, "sc_v", [128, NTC * 128], BF16)
    qst = [sbt(stC, "sc_q%d" % i, [128, S], BF16) for i in range(2)]
    qrot = sbt(stC, "sc_qr", [128, S], BF16)
    t1r = sbt(stC, "sc_t1", [128, 512], F32)
    t2r = sbt(stC, "sc_t2", [128, 512], F32)
    esb = [sbt(stC, "sc_e%d" % i, [128, S], BF16) for i in range(2)]
    etb = [sbt(stC, "sc_et%d" % i, [128, 512], BF16) for i in range(2)]
    ao = sbt(stC, "sc_ao", [128, S], BF16)
    ps_sc = pst(stC, "sc_ps", [128, S], F32)
    ps_rp = pst(stC, "sc_rp", [128, 512], F32)
    ps_tp2 = [pst(stC, "sc_tp%d" % i, [128, 1024], BF16) for i in range(2)]
    ps_pv = pst(stC, "sc_pv", [128, 512], F32)

    def rope(src_t, dst_t, ctab, stab):
        for q4 in range(4):
            sl = slice(q4 * 512, (q4 + 1) * 512)
            c_pm = p.op("tensor", lambda E, sl=sl, src_t=src_t: E.matmul(
                ps_rp[:, :], perm_b[:, :], src_t[:, sl],
                start=True, stop=True), inc=sem["pe"])
            p.op("vector", lambda E, sl=sl, src_t=src_t, ctab=ctab:
                 E.tensor_tensor(t1r[:, :], src_t[:, sl], ctab[:, sl],
                                 ALU.mult),
                 reads=(nm(src_t), nm(ctab)), writes=(nm(t1r),))
            p.wait("vector", sem["pe"], c_pm)
            p.op("vector", lambda E, sl=sl, stab=stab:
                 E.scalar_tensor_tensor(
                     t2r[:, :], ps_rp[:, :], 0.0, stab[:, sl],
                     ALU.add, ALU.mult),
                 reads=(nm(ps_rp), nm(stab)), writes=(nm(t2r),))
            c_rt = p.op("vector", lambda E, sl=sl, dst_t=dst_t:
                        E.tensor_tensor(dst_t[:, sl], t1r[:, :], t2r[:, :],
                                        ALU.add),
                        reads=(nm(t1r), nm(t2r)), writes=(nm(dst_t),),
                        inc=sem["dv"])
            p.wait("tensor", sem["dv"], c_rt)
        return sem["dv"].n

    with stC:
        for name, srcd in (("cq", io["cqd"]), ("sq", io["sqd"]),
                           ("ck", io["ckd"]), ("sk", io["skd"])):
            p.dma("sync", tabs[name][:, :], srcd.ap(), dsem["blks"])
        p.wait("vector", dsem["blks"], dsem["blks"].n)
        c_exp_prev = 0
        c_nr_prev = 0
        c_pv_prev = 0
        pv_gate = [0, 0]
        gp_end = (0, 0)
        strip_end = {}
        strip_scores_end = 0
        aosp_prev = 0
        for gp in range(4):
            p.wait("sync", sem["pe"], gp_end[0])
            p.wait("sync", sem["dv"], gp_end[1])
            c_k = p.dma("sync", kst[:, :],
                        bass.AP(io["k_s"], gp * 128 * S, [[S, 128], [1, S]]),
                        dsem["kst"])
            in_ap = bass.AP(io["v_s"], gp * NTC * 128 * 128,
                            [[128, 128], [128 * 128, NTC], [1, 128]])
            c_v = p.dma("sync",
                        vst.ap().rearrange("p (t c) -> p t c", c=128),
                        in_ap, dsem["vst"])
            p.wait("vector", dsem["kst"], c_k)
            p.wait("tensor", dsem["kst"], c_k)
            p.wait("vector", sem["pe"], gp_end[0])
            rope(kst, krot, tabs["ck"], tabs["sk"])
            p.wait("tensor", dsem["vst"], c_v)
            for i in range(4):
                sidx = 4 * gp + i
                qb = qst[i % 2]
                sq = dsem["qs%d" % (i % 2)]
                g2 = strip_end.get(sidx - 2)
                if g2:
                    p.wait("sync", sem["pe"], g2[0])
                    p.wait("sync", sem["dv"], g2[1])
                c_q = p.dma("sync", qb[:, :],
                            bass.AP(io["q_s"], sidx * 128 * S,
                                    [[S, 128], [1, S]]), sq)
                p.wait("vector", sq, c_q)
                p.wait("tensor", sq, c_q)
                if sidx > 0:
                    p.wait("vector", sem["pe"], strip_scores_end)
                c_qrot = rope(qb, qrot, tabs["cq"], tabs["sq"])
                strip_end[sidx] = (sem["pe"].n, c_qrot)
                p.wait("tensor", sem["dv"], c_qrot)
                def emit_tpv(n, qtc, hh, eb_src, c_nr):
                    p.wait("tensor", sem["dv"], c_nr)
                    ng = (qtc + 1 + 3) // 4
                    pvst = {}

                    def emit_grp(g, qtc=qtc, eb_src=eb_src):
                        t = ps_tp2[g % 2]
                        nt = min(4, qtc + 1 - g * 4)
                        c_tp = 0
                        for u in range(nt):
                            j = g * 4 + u
                            c_tp = p.op("tensor", lambda E, j=j, t=t,
                                        u=u, eb_src=eb_src: E.transpose(
                                            t[:, u * 128:(u + 1) * 128],
                                            eb_src[:, j * 128:(j + 1) * 128],
                                            ident_b[:, :]),
                                        inc=sem["pe"])
                        p.wait("scalar", sem["pe"], c_tp)
                        eb = etb[g % 2]
                        c_ev = p.op("scalar", lambda E, eb=eb, t=t,
                                    nt=nt: E.copy(eb[:, 0:nt * 128],
                                                  t[:, 0:nt * 128]),
                                    reads=(nm(t),), writes=(nm(eb),),
                                    inc=sem["ac"])
                        pvst[g] = (c_ev, eb, nt)

                    emit_grp(0)
                    for g in range(ng):
                        if g + 1 < ng:
                            emit_grp(g + 1)
                        c_ev, eb, nt = pvst.pop(g)
                        p.wait("tensor", sem["ac"], c_ev)
                        for u in range(nt):
                            j = g * 4 + u
                            p.op("tensor", lambda E, j=j, eb=eb, u=u,
                                 qtc=qtc: E.matmul(
                                     ps_pv[:, 0:128],
                                     vst[:, j * 128:(j + 1) * 128],
                                     eb[:, u * 128:(u + 1) * 128],
                                     start=(j == 0), stop=(j == qtc)))
                    c_pv = p.op("tensor", lambda E: E.drain(),
                                inc=sem["pe"])
                    p.wait("scalar", sem["pe"], c_pv)
                    c_ao = p.op("scalar", lambda E, hh=hh, qtc=qtc:
                                E.copy(ao[hh * 64:(hh + 1) * 64,
                                          qtc * 128:(qtc + 1) * 128],
                                       ps_pv[hh * 64:(hh + 1) * 64,
                                             0:128]),
                                reads=(nm(ps_pv),), writes=(nm(ao),),
                                inc=sem["ac"])
                    p.wait("tensor", sem["ac"], c_ao)
                    return c_pv

                pend = None
                nit = 0
                for hh in range(2):
                    rows = slice(hh * 64, (hh + 1) * 64)
                    for qtc in range(NTC):
                        L = (qtc + 1) * 128
                        n = nit
                        nit += 1
                        eb_cur = esb[n % 2]
                        if c_exp_prev:
                            p.wait("tensor", sem["ac"], c_exp_prev)
                        slabs = []
                        off = 0
                        while off < L:
                            w = min(512, L - off)
                            slabs.append((off, w))
                            off += w
                        ns = len(slabs)
                        for si, (off, w) in enumerate(slabs):
                            p.op("tensor", lambda E, off=off, w=w, rows=rows,
                                 qtc=qtc: E.matmul(
                                     ps_sc[:, off:off + w],
                                     qrot[rows, qtc * 128:(qtc + 1) * 128],
                                     krot[rows, off:off + w],
                                     start=True, stop=True))
                            c_sl = p.op("tensor", lambda E: E.drain(),
                                        inc=sem["pe"])
                            p.wait("vector", sem["pe"], c_sl)
                            if si == ns - 1:
                                p.op("vector", lambda E, qtc=qtc:
                                     E.tensor_tensor(
                                         ps_sc[:, qtc * 128:(qtc + 1) * 128],
                                         ps_sc[:, qtc * 128:(qtc + 1) * 128],
                                         mask_b[:, :], ALU.add),
                                     reads=(nm(ps_sc), "mask_b"),
                                     writes=(nm(ps_sc),))
                            p.op("vector", lambda E, si=si, off=off, w=w:
                                 E.tensor_reduce(
                                     stat6[:, si:si + 1],
                                     ps_sc[:, off:off + w], AX.X, ALU.max),
                                 reads=(nm(ps_sc),), writes=("stat6",))
                        c_mx = p.op("vector", lambda E, ns=ns, n=n:
                                    E.tensor_reduce(
                                        negmax[:, n % 2:n % 2 + 1],
                                        stat6[:, 0:ns], AX.X,
                                        ALU.max, negate=True),
                                    reads=("stat6",), writes=("negmax",),
                                    inc=sem["dv"])
                        if pend is not None:
                            c_pv_prev = emit_tpv(*pend)
                            pv_gate[(n - 1) % 2] = c_pv_prev
                        p.wait("scalar", sem["dv"], c_mx)
                        if c_nr_prev:
                            p.wait("scalar", sem["dv"], c_nr_prev)
                        if pv_gate[n % 2]:
                            p.wait("scalar", sem["pe"], pv_gate[n % 2])
                        c_ex = p.op("scalar", lambda E, L=L, n=n,
                                    eb_cur=eb_cur: E.activation(
                                        eb_cur[:, 0:L], ps_sc[:, 0:L],
                                        ACTF.Exp,
                                        bias=negmax[:, n % 2:n % 2 + 1],
                                        scale=1.0, accum_out=zsum[:, :]),
                                    reads=(nm(ps_sc), "negmax"),
                                    writes=(nm(eb_cur), "zsum"),
                                    inc=sem["ac"])
                        c_exp_prev = c_ex
                        p.wait("vector", sem["ac"], c_ex)
                        p.op("vector", lambda E: E.reciprocal(
                            rz[:, :], zsum[:, :]), reads=("zsum",),
                            writes=("rz",))
                        c_nr = p.op("vector", lambda E, L=L, eb_cur=eb_cur:
                                    E.tensor_scalar(
                                        eb_cur[:, 0:L], eb_cur[:, 0:L],
                                        rz[:, 0:1], None, ALU.mult),
                                    reads=(nm(eb_cur), "rz"),
                                    writes=(nm(eb_cur),), inc=sem["dv"])
                        c_nr_prev = c_nr
                        pend = (n, qtc, hh, eb_cur, c_nr)
                if pend is not None:
                    c_pv_prev = emit_tpv(*pend)
                    pv_gate[(nit - 1) % 2] = c_pv_prev
                    pend = None
                strip_scores_end = sem["pe"].n
                p.wait("gpsimd", sem["ac"], sem["ac"].n)
                if aosp_prev:
                    p.wait("scalar", dsem["aosp"], aosp_prev)
                aosp_prev = p.dma("gpsimd",
                                  bass.AP(io["ao_s"], sidx * 128 * S,
                                          [[S, 128], [1, S]]), ao[:, :],
                                  dsem["aosp"])
            gp_end = (sem["pe"].n, sem["dv"].n)
        p.wait("sync", dsem["aosp"], dsem["aosp"].n)

    stD = contextlib.ExitStack()
    wob = [sbt(stD, "op_w%d" % s, [128, DM], BF16) for s in range(16)]
    aoc = [sbt(stD, "op_a%d" % i, [128, 16 * 128], BF16) for i in range(2)]
    xb2 = [sbt(stD, "op_x%d" % i, [128, DM], F32) for i in range(2)]
    ps_o = pst(stD, "op_ps", [128, DM], F32)
    with stD:
        for s in range(16):
            p.dma("gpsimd", wob[s][:, :],
                  bass.AP(io["wo"], s * 128 * DM, [[DM, 128], [1, DM]]),
                  dsem["blkp"])
        c_wo = dsem["blkp"].n
        p.wait("tensor", dsem["blkp"], c_wo)
        res_done = {}
        for tc in range(NTC):
            ab = aoc[tc % 2]
            xb = xb2[tc % 2]
            sao = dsem["ao%d" % (tc % 2)]
            sxb = dsem["xb%d" % (tc % 2)]
            gate = res_done.get(tc - 2)
            if gate:
                p.wait("sync", sxb, gate)
            in_ap = bass.AP(io["ao_s"], tc * 128,
                            [[S, 128], [128 * S, 16], [1, 128]])
            c_ao = p.dma("sync",
                         ab.ap().rearrange("p (s c) -> p s c", c=128),
                         in_ap, sao)
            c_x = p.dma("sync", xb[:, :],
                        xscr.ap()[tc * 128:(tc + 1) * 128, :], sxb)
            p.wait("tensor", sao, c_ao)
            for s in range(16):
                for ds in range(4):
                    p.op("tensor", lambda E, s=s, ds=ds, ab=ab: E.matmul(
                        ps_o[:, ds * 512:(ds + 1) * 512],
                        ab[:, s * 128:(s + 1) * 128],
                        wob[s][:, ds * 512:(ds + 1) * 512],
                        start=(s == 0), stop=(s == 15)))
            c_om = p.op("tensor", lambda E: E.drain(), inc=sem["pe"])
            p.wait("vector", sem["pe"], c_om)
            p.wait("vector", sxb, c_x)
            xnm = nm(xb)
            c_xr = p.op("vector", lambda E, xb=xb: E.tensor_tensor(
                xb[:, :], xb[:, :], ps_o[:, :], ALU.add),
                reads=(xnm, nm(ps_o)), writes=(xnm,), inc=sem["dv"])
            p.wait("tensor", sem["dv"], c_xr)
            p.wait("sync", sem["dv"], c_xr)
            c_st = p.dma("sync", xscr.ap()[tc * 128:(tc + 1) * 128, :],
                         xb[:, :], sxb)
            res_done[tc] = c_st
        for k in ("xb0", "xb1"):
            p.wait("sync", dsem[k], dsem[k].n)
        with nc.Block() as blk:
            p.flush_into_block(blk)



def kernel(L0_attn_x, L0_attn_Wq, L0_attn_Wk, L0_attn_Wv, L0_attn_Wo,
           L0_ffn_ln_w, L0_ffn_ln_b, L0_ffn_W1, L0_ffn_b1, L0_ffn_W2,
           L0_ffn_b2):
    x = np.ascontiguousarray(np.asarray(L0_attn_x, dtype=np.float32))
    consts = _host_consts()
    perm = _head_perm()
    wq_p = np.ascontiguousarray(np.asarray(L0_attn_Wq, np.float32)[:, perm])
    wo_p = np.ascontiguousarray(np.asarray(L0_attn_Wo, np.float32)[perm, :])
    wk_ = np.ascontiguousarray(np.asarray(L0_attn_Wk, np.float32))
    wv_ = np.ascontiguousarray(np.asarray(L0_attn_Wv, np.float32))
    w1_ = np.ascontiguousarray(np.asarray(L0_ffn_W1, np.float32))
    w2_ = np.ascontiguousarray(np.asarray(L0_ffn_W2, np.float32))
    lnw = np.ascontiguousarray(np.broadcast_to(
        np.asarray(L0_ffn_ln_w, np.float32)[None, :], (128, DM)))
    lnb = np.ascontiguousarray(np.broadcast_to(
        np.asarray(L0_ffn_ln_b, np.float32)[None, :], (128, DM)))
    b1v = np.ascontiguousarray(np.asarray(L0_ffn_b1, np.float32)[None, :])
    b2v = np.ascontiguousarray(np.asarray(L0_ffn_b2, np.float32)[None, :])

    nc = build_nc()
    base = {"wq": wq_p, "wk": wk_, "wv": wv_, "wo": wo_p, "w1": w1_,
            "w2": w2_, "lnww": lnw, "lnbb": lnb, "b1r": b1v, "b2r": b2v,
            "cqd": consts["cq"], "sqd": consts["sq"], "ckd": consts["ck"],
            "skd": consts["sk"], "permd": consts["perm"],
            "identd": consts["ident"], "maskd": consts["mask"]}
    in_maps = [dict(base, xin=x[c]) for c in range(8)]
    res = run_bass_kernel_spmd(nc, in_maps, core_ids=list(range(8)))
    return np.stack([np.asarray(r["yout"], dtype=np.float32)
                     for r in res.results], axis=0)


# version 11
# speedup vs baseline: 1.7257x; 1.0357x over previous
import contextlib
import os
import numpy as np
import concourse.bass as bass
import concourse.mybir as mybir
from concourse.bass_utils import run_bass_kernel_spmd

try:
    import ml_dtypes
    _BF16_NP = ml_dtypes.bfloat16
except ImportError:
    _BF16_NP = None

F32 = mybir.dt.float32
BF16 = mybir.dt.bfloat16
I32 = mybir.dt.int32
I8 = mybir.dt.int8
ALU = mybir.AluOpType
AX = mybir.AxisListType
ACTF = mybir.ActivationFunctionType

S = 2048
DM = 2048
HID = 8192
NTC = S // 128
KC = DM // 128
OC = HID // 128
TBLK = 1024
NSW = S // TBLK
SWTC = TBLK // 128
OSL = 256
NOSL = HID // OSL
HD = 64
EPS = 1e-5
NEG = -1.0e30
MAGIC = 12582912.0
CLIPC = float(np.nextafter(np.float32(127.5), np.float32(0)))

ENGS = ("sync", "gpsimd", "vector", "scalar", "tensor")
SIM_DRAINS = True
PHASES = ("wprep", "fa", "attn", "fb")
DBG_SCRATCH = False


class Sem:
    def __init__(self, h):
        self.h = h
        self.n = 0

    def inc(self, k=1):
        self.n += k
        return self.n


class Prog:
    def __init__(self, nc):
        self.nc = nc
        self.q = {e: [] for e in ENGS}
        self.dirty = {e: set() for e in ENGS}

    def wait(self, eng, sem, val):
        if val > 0:
            self.q[eng].append(lambda E, s=sem, v=val: E.wait_ge(s.h, v))

    def op(self, eng, build, reads=(), writes=(), inc=None, inc_n=1):
        names = set(reads) | set(writes)
        if (SIM_DRAINS and eng in ("vector", "scalar", "gpsimd")
                and (names & self.dirty[eng])):
            self.q[eng].append(lambda E: E.drain())
            self.dirty[eng] = set()
        self.dirty[eng] |= set(writes)
        if inc is not None:
            def fn(E, b=build, s=inc, n=inc_n):
                b(E).then_inc(s.h, n)
            self.q[eng].append(fn)
            return inc.inc(inc_n)
        self.q[eng].append(build)
        return None

    def dma(self, eng, out_ap, in_ap, sem, **kw):
        def fn(E, o=out_ap, i=in_ap, s=sem, kw=kw):
            E.dma_start(o, i, **kw).then_inc(s.h, 16)
        self.q[eng].append(fn)
        return sem.inc(16)

    def flush_into_block(self, block):
        q = self.q

        @block.sync
        def _(e):
            for fn in q["sync"]:
                fn(e)

        @block.gpsimd
        def _(e):
            for fn in q["gpsimd"]:
                fn(e)

        @block.vector
        def _(e):
            for fn in q["vector"]:
                fn(e)

        @block.scalar
        def _(e):
            for fn in q["scalar"]:
                fn(e)

        @block.tensor
        def _(e):
            for fn in q["tensor"]:
                fn(e)


def nm(t):
    return t.ap().name if callable(getattr(t, "ap", None)) else t.name


def _host_consts():
    half = HD // 2
    inv = (1.0 / np.power(np.float32(10000.0),
                          np.arange(half, dtype=np.float32) / np.float32(half))
           ).astype(np.float32)
    pos = np.arange(S, dtype=np.float32)
    fr = np.outer(pos, inv).astype(np.float32)
    cos = np.cos(fr).astype(np.float32)
    sin = np.sin(fr).astype(np.float32)
    j_of_r = (np.arange(128) % HD) // 2
    C = cos[:, j_of_r].T.astype(np.float32)
    Sg = sin[:, j_of_r].T.astype(np.float32)
    sign = np.where(np.arange(128) % 2 == 0, -1.0, 1.0).astype(np.float32)
    Stab = (Sg * sign[:, None]).astype(np.float32)
    swap = np.arange(128)
    swap = swap + np.where(swap % 2 == 0, 1, -1)
    perm = np.zeros((128, 128), dtype=np.float32)
    perm[swap, np.arange(128)] = 1.0
    ident = np.eye(128, dtype=np.float32)
    ii = np.arange(128)
    mask = np.where(ii[None, :] <= ii[:, None], 0.0, NEG).astype(np.float32)
    if _BF16_NP is not None:
        perm = perm.astype(_BF16_NP)
        ident = ident.astype(_BF16_NP)
    return dict(cq=(C * np.float32(0.125)).astype(np.float32),
                sq=(Stab * np.float32(0.125)).astype(np.float32),
                ck=C, sk=Stab, perm=perm, ident=ident, mask=mask)


def _head_perm():
    order = []
    for gp in range(4):
        for i in range(4):
            for h in (8 * gp + i, 8 * gp + 4 + i):
                order.extend(range(h * HD, (h + 1) * HD))
    return np.asarray(order, dtype=np.int64)



def emit_mx_quant_core(p, sem, src_t, src_sl, nblk, bufs):
    am, rsc, ssc, y, r = (bufs["am"], bufs["rsc"], bufs["ssc"], bufs["y"],
                          bufs["r"])
    n = nblk * 32
    src3 = src_t.ap()[:, src_sl].rearrange("p (nb k) -> p nb k", k=32)
    p.op("vector", lambda E: E.tensor_reduce(
        am[:, 0:nblk], src3, AX.X, ALU.max, apply_absolute_value=True),
        reads=(nm(src_t),), writes=(nm(am),))
    p.op("vector", lambda E: E.tensor_single_scalar(
        rsc[:, 0:nblk], am.ap()[:, 0:nblk].bitcast(I32), 23,
        ALU.logical_shift_right), reads=(nm(am),), writes=(nm(rsc),))
    p.op("vector", lambda E: E.tensor_scalar_max(
        rsc[:, 0:nblk], rsc[:, 0:nblk], 64),
        reads=(nm(rsc),), writes=(nm(rsc),))
    p.op("vector", lambda E: E.tensor_scalar(
        ssc[:, 0:nblk], rsc[:, 0:nblk], -6, 8388608, ALU.add, ALU.mult),
        reads=(nm(rsc),), writes=(nm(ssc),))
    p.op("vector", lambda E: E.tensor_scalar(
        rsc[:, 0:nblk], rsc[:, 0:nblk], -260, -8388608, ALU.add, ALU.mult),
        reads=(nm(rsc),), writes=(nm(rsc),))
    rb = rsc.ap()[:, 0:nblk].bitcast(F32).unsqueeze(2).broadcast_to(
        [128, nblk, 32])
    y3 = y.ap()[:, 0:n].rearrange("p (nb k) -> p nb k", k=32)
    p.op("vector", lambda E: E.tensor_tensor(y3, src3, rb, ALU.mult),
         reads=(nm(src_t), nm(rsc)), writes=(nm(y),))
    return p.op("vector", lambda E: E.tensor_scalar(
        r[:, 0:n], y[:, 0:n], MAGIC, MAGIC, ALU.add, ALU.subtract),
        reads=(nm(y),), writes=(nm(r),), inc=sem["dv"])


def emit_mx_q8(p, sem, bufs, r_sl, q8_t, q8_sl, inc=None):
    r = bufs["r"]
    return p.op("vector", lambda E: E.tensor_scalar(
        q8_t.ap()[:, q8_sl], r[:, r_sl], 127.0, None, ALU.min),
        reads=(nm(r),), writes=(nm(q8_t),), inc=inc)


def emit_mx_expand(p, sem, q8_t, q8_sl, ssc_t, ssc_sl, nblk, out_t, out_sl,
                   inc=None):
    q83 = q8_t.ap()[:, q8_sl].rearrange("p (nb k) -> p nb k", k=32)
    sc3 = ssc_t.ap()[:, ssc_sl].bitcast(F32).unsqueeze(2).broadcast_to(
        [128, nblk, 32])
    o3 = out_t.ap()[:, out_sl].rearrange("p (nb k) -> p nb k", k=32)
    return p.op("vector", lambda E: E.scalar_tensor_tensor(
        o3, q83, 0.0, sc3, ALU.add, ALU.mult),
        reads=(nm(q8_t), nm(ssc_t)), writes=(nm(out_t),), inc=inc)


def emit_ln_stats(p, sem, src_t, stat6, stat2):
    for j in range(4):
        p.op("vector", lambda E, j=j, src_t=src_t: E.bn_stats(
            stat6[:, 6 * j:6 * (j + 1)], src_t[:, 512 * j:512 * (j + 1)]),
            reads=(nm(src_t),), writes=("stat6",))
    return p.op("vector", lambda E: E.bn_aggr(stat2[:, :], stat6[:, :]),
                reads=("stat6",), writes=("stat2",), inc=sem["dv"])


def emit_rsig(p, sem, stat2, rsg, epsc, nhalf, gate_dv=0):
    if gate_dv:
        p.wait("scalar", sem["dv"], gate_dv)
    p.op("scalar", lambda E: E.activation(
        rsg[:, :], stat2[:, 1:2], ACTF.Ln, bias=epsc[:, 0:1], scale=1.0),
        reads=("stat2",), writes=("rsg",))
    return p.op("scalar", lambda E: E.activation(
        rsg[:, :], rsg[:, :], ACTF.Exp, bias=0.0, scale=nhalf[:, 0:1]),
        reads=("rsg",), writes=("rsg",), inc=sem["ac"])



def build_nc(phases=None):
    phases = PHASES if phases is None else phases
    scratch_kind = "ExternalOutput" if DBG_SCRATCH else "Internal"
    nc = bass.Bass("TRN2", target_bir_lowering=False)

    xin = nc.dram_tensor("xin", [S, DM], F32, kind="ExternalInput")
    wq = nc.dram_tensor("wq", [DM, DM], F32, kind="ExternalInput")
    wk = nc.dram_tensor("wk", [DM, 512], F32, kind="ExternalInput")
    wv = nc.dram_tensor("wv", [DM, 512], F32, kind="ExternalInput")
    wo = nc.dram_tensor("wo", [DM, DM], F32, kind="ExternalInput")
    w1 = nc.dram_tensor("w1", [HID, DM], F32, kind="ExternalInput")
    w2 = nc.dram_tensor("w2", [DM, HID], F32, kind="ExternalInput")
    lnww = nc.dram_tensor("lnww", [128, DM], F32, kind="ExternalInput")
    lnbb = nc.dram_tensor("lnbb", [128, DM], F32, kind="ExternalInput")
    b1r = nc.dram_tensor("b1r", [1, HID], F32, kind="ExternalInput")
    b2r = nc.dram_tensor("b2r", [1, DM], F32, kind="ExternalInput")
    cqd = nc.dram_tensor("cqd", [128, S], F32, kind="ExternalInput")
    sqd = nc.dram_tensor("sqd", [128, S], F32, kind="ExternalInput")
    ckd = nc.dram_tensor("ckd", [128, S], F32, kind="ExternalInput")
    skd = nc.dram_tensor("skd", [128, S], F32, kind="ExternalInput")
    permd = nc.dram_tensor("permd", [128, 128], BF16, kind="ExternalInput")
    identd = nc.dram_tensor("identd", [128, 128], BF16, kind="ExternalInput")
    maskd = nc.dram_tensor("maskd", [128, 128], F32, kind="ExternalInput")
    yout = nc.dram_tensor("yout", [S, DM], F32, kind="ExternalOutput")

    xscr = nc.dram_tensor("xscr", [S, DM], F32, kind=scratch_kind)
    w1t_s = nc.dram_tensor("w1t_s", [KC, 128, HID], BF16, kind=scratch_kind)
    w2t_s = nc.dram_tensor("w2t_s", [OC, 128, DM], BF16, kind=scratch_kind)
    q_s = nc.dram_tensor("q_s", [16, 128, S], BF16, kind=scratch_kind)
    k_s = nc.dram_tensor("k_s", [4, 128, S], BF16, kind=scratch_kind)
    v_s = nc.dram_tensor("v_s", [4, NTC, 128, 128], BF16, kind=scratch_kind)
    ao_s = nc.dram_tensor("ao_s", [16, 128, S], BF16, kind=scratch_kind)

    st = contextlib.ExitStack()
    sem = {k: Sem(st.enter_context(nc.semaphore("sem_" + k)))
           for k in ("dv", "ac", "pe")}
    dnames = ("blks", "blkp", "xt0", "xt1", "w10", "w11", "b10", "b11",
              "w20", "w21", "wc0", "wc1", "vwl", "qs0", "qs1", "kst", "vst",
              "spl", "vsp", "aosp", "ao0", "ao1", "xb0", "xb1", "wld0",
              "wld1", "wst0", "wst1")
    dsem = {k: Sem(st.enter_context(nc.semaphore("dma_" + k)))
            for k in dnames}
    dq = {"blkp": "gpsimd", "spl": "gpsimd", "vsp": "gpsimd",
          "aosp": "gpsimd", "wst0": "gpsimd", "wst1": "gpsimd",
          "vwl": "gpsimd", "wc0": "gpsimd", "wc1": "gpsimd",
          "b10": "gpsimd", "b11": "gpsimd"}

    def sbt(stack, name, shape, dtype):
        return stack.enter_context(nc.sbuf_tensor(name, shape, dtype))

    def pst(stack, name, shape, dtype):
        return stack.enter_context(nc.psum_tensor(name, shape, dtype))

    ident_b = sbt(st, "ident_b", [128, 128], BF16)
    perm_b = sbt(st, "perm_b", [128, 128], BF16)
    mask_b = sbt(st, "mask_b", [128, 128], F32)
    ones1 = sbt(st, "ones1", [1, 128], BF16)
    stat6 = sbt(st, "stat6", [128, 24], F32)
    stat2 = sbt(st, "stat2", [128, 2], F32)
    rsg = sbt(st, "rsg", [128, 1], F32)
    musig = sbt(st, "musig", [128, 2 * NTC], F32)
    zsum = sbt(st, "zsum", [128, 1], F32)
    rz = sbt(st, "rz", [128, 1], F32)
    negmax = sbt(st, "negmax", [128, 2], F32)
    epsc = sbt(st, "epsc", [128, 1], F32)
    nhalf = sbt(st, "nhalf", [128, 1], F32)

    io = dict(xin=xin, wq=wq, wk=wk, wv=wv, wo=wo, w1=w1, w2=w2,
              lnww=lnww, lnbb=lnbb, b1r=b1r, b2r=b2r, cqd=cqd, sqd=sqd,
              ckd=ckd, skd=skd, permd=permd, identd=identd, maskd=maskd,
              yout=yout, xscr=xscr, w1t_s=w1t_s, w2t_s=w2t_s, q_s=q_s,
              k_s=k_s, v_s=v_s, ao_s=ao_s)
    pers = dict(ident_b=ident_b, perm_b=perm_b, mask_b=mask_b, ones1=ones1,
                stat6=stat6, stat2=stat2, rsg=rsg, musig=musig, zsum=zsum,
                rz=rz, negmax=negmax, epsc=epsc, nhalf=nhalf)

    with st:
        p = Prog(nc)
        for dst, src in ((ident_b, identd), (perm_b, permd), (mask_b, maskd)):
            p.dma("sync", dst[:, :], src.ap(), dsem["blks"])
        p.wait("vector", dsem["blks"], dsem["blks"].n)
        p.op("vector", lambda E: E.memset(ones1[:, :], 1.0), writes=("ones1",))
        p.op("vector", lambda E: E.memset(epsc[:, :], EPS), writes=("epsc",))
        p.op("vector", lambda E: E.memset(nhalf[:, :], -0.5),
             writes=("nhalf",), inc=sem["dv"])
        p.wait("tensor", sem["dv"], sem["dv"].n)
        p.wait("scalar", sem["dv"], sem["dv"].n)
        with nc.Block() as blk:
            p.flush_into_block(blk)

        merge_wp = "wprep" in phases and "fa" in phases
        if "wprep" in phases and not merge_wp:
            _phase_wprep(nc, sem, dsem, sbt, pst, io, pers)
        if "fa" in phases:
            _phase_ffn(nc, sem, dsem, sbt, pst, io, pers, xin, xscr, "fa",
                       wprep_merge=merge_wp)
        else:
            p = Prog(nc)
            for tc in range(NTC):
                p.dma("sync", xscr.ap()[tc * 128:(tc + 1) * 128, :],
                      xin.ap()[tc * 128:(tc + 1) * 128, :], dsem["blks"])
            p.wait("sync", dsem["blks"], dsem["blks"].n)
            with nc.Block() as blk:
                p.flush_into_block(blk)
        if "attn" in phases:
            _phase_attn(nc, sem, dsem, sbt, pst, io, pers)
        if "fb" in phases:
            _phase_ffn(nc, sem, dsem, sbt, pst, io, pers, xscr, yout, "fb")
        else:
            p = Prog(nc)
            for tc in range(NTC):
                p.dma("sync", yout.ap()[tc * 128:(tc + 1) * 128, :],
                      xscr.ap()[tc * 128:(tc + 1) * 128, :], dsem["blks"])
            p.wait("sync", dsem["blks"], dsem["blks"].n)
            with nc.Block() as blk:
                p.flush_into_block(blk)

        p = Prog(nc)
        for k in dnames:
            p.wait(dq.get(k, "sync"), dsem[k], dsem[k].n)
        with nc.Block() as blk:
            p.flush_into_block(blk)

    nc.finalize()
    return nc



def _phase_wprep(nc, sem, dsem, sbt, pst, io, pers):
    st = contextlib.ExitStack()
    wld = [sbt(st, "wp_ld%d" % i, [128, HID], F32) for i in range(2)]
    qbufs = dict(am=sbt(st, "wp_am", [128, 256], F32),
                 rsc=sbt(st, "wp_rsc", [128, 256], I32),
                 ssc=sbt(st, "wp_ssc", [128, 256], I32),
                 y=sbt(st, "wp_y", [128, HID], F32),
                 r=sbt(st, "wp_r", [128, HID], F32))
    q8 = sbt(st, "wp_q8", [128, HID], I8)
    qnat = sbt(st, "wp_qn", [128, HID], BF16)
    wfm = [sbt(st, "wp_fm%d" % i, [128, HID], BF16) for i in range(2)]
    tp = [pst(st, "wp_tp%d" % i, [128, 1024], BF16) for i in range(2)]
    ident_b = pers["ident_b"]

    p = Prog(nc)
    q_done = {}
    stg = {}
    specs = []
    for rc in range(HID // 128):
        specs.append((io["w1"], io["w1t_s"], rc, DM, 4, HID))
    for rc in range(DM // 128):
        specs.append((io["w2"], io["w2t_s"], rc, HID, 8, DM))

    with st:
        for g, (wt, scr, rc, ncols, nsplit, scr_cols) in enumerate(specs):
            buf = wld[g % 2]
            fmb = wfm[g % 2]
            sld = dsem["wld%d" % (g % 2)]
            sst = dsem["wst%d" % (g % 2)]
            if g >= 2:
                p.wait("sync", sem["dv"], q_done[g - 2])
            c_ld = p.dma("sync", buf[:, 0:ncols],
                         wt.ap()[rc * 128:(rc + 1) * 128, :], sld)
            p.wait("vector", sld, c_ld)
            nblk = ncols // 32
            q_done[g] = emit_mx_quant_core(p, sem, buf, slice(0, ncols), nblk,
                                           qbufs)
            emit_mx_q8(p, sem, qbufs, slice(0, ncols), q8, slice(0, ncols))
            c_qn = emit_mx_expand(p, sem, q8, slice(0, ncols), qbufs["ssc"],
                                  slice(0, nblk), nblk, qnat, slice(0, ncols),
                                  inc=sem["dv"])
            p.wait("tensor", sem["dv"], c_qn)
            if g >= 2:
                p.wait("scalar", sst, stg[g - 2])
            ntp = ncols // 128
            evs = {}
            for g4 in range(ntp // 4):
                t = tp[g4 % 2]
                if g4 >= 2:
                    p.wait("tensor", sem["ac"], evs[g4 - 2])
                c_pe = 0
                for u in range(4):
                    j = g4 * 4 + u
                    c_pe = p.op("tensor", lambda E, j=j, t=t, u=u:
                                E.transpose(t[:, u * 128:(u + 1) * 128],
                                            qnat[:, j * 128:(j + 1) * 128],
                                            ident_b[:, :]), inc=sem["pe"])
                p.wait("scalar", sem["pe"], c_pe)
                evs[g4] = p.op("scalar", lambda E, g4=g4, t=t, fmb=fmb:
                               E.copy(fmb[:, g4 * 512:(g4 + 1) * 512],
                                      t[:, 0:512]),
                               reads=(nm(t),), writes=(nm(fmb),),
                               inc=sem["ac"])
            p.wait("vector", sem["pe"], sem["pe"].n)
            p.wait("gpsimd", sem["ac"], sem["ac"].n)
            ngrp = ntp // nsplit
            c_st = 0
            for sp_i in range(nsplit):
                out_ap = bass.AP(
                    scr, sp_i * ngrp * 128 * scr_cols + rc * 128,
                    [[scr_cols, 128], [128 * scr_cols, ngrp], [1, 128]])
                in_ap = fmb.ap().rearrange("p (c o) -> p c o", o=128)[
                    :, sp_i * ngrp:(sp_i + 1) * ngrp, :]
                c_st = p.dma("gpsimd", out_ap, in_ap, sst)
            stg[g] = c_st
        p.wait("sync", dsem["wst0"], dsem["wst0"].n)
        p.wait("sync", dsem["wst1"], dsem["wst1"].n)
        with nc.Block() as blk:
            p.flush_into_block(blk)



def _phase_ffn(nc, sem, dsem, sbt, pst, io, pers, src, dst, tag,
               wprep_merge=False):
    st = contextlib.ExitStack()
    xt = [sbt(st, tag + "_x%d" % i, [128, DM], F32) for i in range(2)]
    lnw_t = sbt(st, tag + "_lnw", [128, DM], F32)
    lnb_t = sbt(st, tag + "_lnb", [128, DM], F32)
    qa_fm = sbt(st, tag + "_qa", [128, KC * TBLK], BF16)
    q8g = [sbt(st, tag + "_q8_%d" % i, [128, HID], I8) for i in range(SWTC)]
    sscg = [sbt(st, tag + "_sg%d" % i, [128, HID // 32], I32)
            for i in range(SWTC)]
    xbb = sbt(st, tag + "_xbb", [128, DM], BF16)
    ntile = sbt(st, tag + "_n", [128, DM], BF16)
    qbufs = dict(am=sbt(st, tag + "_am", [128, 64], F32),
                 rsc=sbt(st, tag + "_rsc", [128, 64], I32),
                 ssc=sbt(st, tag + "_ssc", [128, 64], I32),
                 y=sbt(st, tag + "_y", [128, DM], BF16),
                 r=sbt(st, tag + "_r", [128, DM], F32))
    t1 = qbufs["r"]
    q8a = sbt(st, tag + "_q8a", [128, DM], I8)
    w1sl = [sbt(st, tag + "_w1s%d" % i, [128, KC * OSL], BF16)
            for i in range(2)]
    b1sl = [sbt(st, tag + "_b1s%d" % i, [1, OSL], BF16) for i in range(2)]
    b2sl = sbt(st, tag + "_b2s", [1, DM], BF16)
    gcat = [sbt(st, tag + "_g%d" % i, [128, SWTC * OSL], BF16)
            for i in range(2)]
    h1b = [sbt(st, tag + "_h1b%d" % i, [128, OSL], BF16) for i in range(4)]
    gq = qbufs
    w2sl = [sbt(st, tag + "_w2s%d" % i, [128, DM], BF16) for i in range(2)]
    qgtm = [sbt(st, tag + "_qgtm%d" % i, [128, 256], BF16) for i in range(4)]
    h2b = sbt(st, tag + "_h2b", [128, DM], BF16)
    fb = xbb

    psA = pst(st, tag + "_psA", [128, DM], F32)
    psB = pst(st, tag + "_psB", [128, DM], F32)
    ps_h1 = [psB.ap()[:, 0:512], psB.ap()[:, 512:1024]]
    ps_tp = [psB.ap()[:, 1024:1536].bitcast(BF16),
             psB.ap()[:, 1536:2048].bitcast(BF16)]
    ps_h2 = psA

    ident_b = pers["ident_b"]
    ones1 = pers["ones1"]
    stat6, stat2, rsg, musig = (pers["stat6"], pers["stat2"], pers["rsg"],
                                pers["musig"])
    epsc, nhalf = pers["epsc"], pers["nhalf"]
    p = Prog(nc)

    wp_on = wprep_merge and tag == "fa"
    wp_stg = {}
    wp_tp_hist = []
    if wp_on:
        wld = [lnw_t, lnb_t]
        wfm = [xbb, h2b]
        wp_specs = []
        for rc in range(HID // 128):
            wp_specs.append((io["w1"].ap()[rc * 128:(rc + 1) * 128, :],
                             io["w1t_s"], HID, 0, rc))
        for k in range(64):
            cc, rr = k // 16, k % 16
            wp_specs.append((io["w2"].ap()[rr * 128:(rr + 1) * 128,
                                           cc * DM:(cc + 1) * DM],
                             io["w2t_s"], DM, 16 * cc, rr))
        wp_order = []
        for j in range(32):
            wp_order += [2 * j, 2 * j + 1, 64 + 2 * j, 64 + 2 * j + 1]
        wp_qdone = {}
        wp_nt_pe = {0: 0, 1: 0}

        def emit_wp_spec(g):
            src_ap, scr, scr_cols, blkbase, rc = wp_specs[wp_order[g]]
            buf = wld[g % 2]
            fmb = wfm[g % 2]
            sld = dsem["wld%d" % (g % 2)]
            sst = dsem["wst%d" % (g % 2)]
            if g >= 2:
                p.wait("sync", sem["dv"], wp_qdone[g - 2])
            else:
                p.wait("sync", sem["dv"], sem["dv"].n)
            c_ld = p.dma("sync", buf[:, :], src_ap, sld)
            p.wait("vector", sld, c_ld)
            wpbufs = dict(am=qbufs["am"], rsc=qbufs["rsc"],
                          ssc=qbufs["ssc"], y=buf, r=qbufs["r"])
            wp_qdone[g] = emit_mx_quant_core(p, sem, buf, slice(0, DM), 64,
                                             wpbufs)
            emit_mx_q8(p, sem, wpbufs, slice(0, DM), q8a, slice(0, DM))
            buf2 = [ntile, qbufs["y"]][g % 2]
            p.wait("vector", sem["pe"], wp_nt_pe[g % 2])
            c_qn = emit_mx_expand(p, sem, q8a, slice(0, DM), qbufs["ssc"],
                                  slice(0, 64), 64, buf2, slice(0, DM),
                                  inc=sem["dv"])
            p.wait("tensor", sem["dv"], c_qn)
            if g >= 2:
                p.wait("scalar", sst, wp_stg[g - 2])
            c_ev = 0
            c_pe = 0
            for g4 in range(4):
                t = ps_tp[g4 % 2]
                if len(wp_tp_hist) >= 2:
                    p.wait("tensor", sem["ac"], wp_tp_hist[-2])
                for u in range(4):
                    jj = g4 * 4 + u
                    c_pe = p.op("tensor", lambda E, jj=jj, t=t, u=u,
                                buf2=buf2:
                                E.transpose(t[:, u * 128:(u + 1) * 128],
                                            buf2[:, jj * 128:(jj + 1) * 128],
                                            ident_b[:, :]), inc=sem["pe"])
                p.wait("scalar", sem["pe"], c_pe)
                c_ev = p.op("scalar", lambda E, g4=g4, t=t, fmb=fmb:
                            E.copy(fmb[:, g4 * 512:(g4 + 1) * 512],
                                   t[:, 0:512]),
                            reads=(nm(t),), writes=(nm(fmb),),
                            inc=sem["ac"])
                wp_tp_hist.append(c_ev)
            wp_nt_pe[g % 2] = c_pe
            p.wait("gpsimd", sem["ac"], c_ev)
            c_st = 0
            for sp_i in range(4):
                out_ap = bass.AP(
                    scr, (blkbase + sp_i * 4) * 128 * scr_cols + rc * 128,
                    [[scr_cols, 128], [128 * scr_cols, 4], [1, 128]])
                in_ap = fmb.ap().rearrange("p (c o) -> p c o", o=128)[
                    :, sp_i * 4:(sp_i + 1) * 4, :]
                c_st = p.dma("gpsimd", out_ap, in_ap, sst)
            wp_stg[g] = c_st

    with st:
        c_lnw = p.dma("sync", lnw_t[:, :], io["lnww"].ap(), dsem["blks"])
        c_lnb = p.dma("sync", lnb_t[:, :], io["lnbb"].ap(), dsem["blks"])
        c_b2 = p.dma("gpsimd", b2sl[:, :], io["b2r"].ap(), dsem["blkp"])
        p.wait("vector", dsem["blks"], c_lnb)
        p.wait("tensor", dsem["blkp"], c_b2)

        xt_gate = [[], []]
        rsg_gate = 0
        f1_mm_last = 0
        f2_pe_last = 0
        f2_done_prev = 0
        tp_used = {}
        w2_gate = {}

        for sw in range(NSW):
            tcs = [sw * SWTC + i for i in range(SWTC)]
            for i, tc in enumerate(tcs):
                xb = xt[i % 2]
                sx = dsem["xt%d" % (i % 2)]
                if wp_on and sw == 1 and i == 0:
                    p.wait("sync", sem["dv"], wp_qdone[126])
                    p.wait("sync", sem["dv"], wp_qdone[127])
                    p.dma("sync", lnw_t[:, :], io["lnww"].ap(),
                          dsem["blks"])
                    c_rl = p.dma("sync", lnb_t[:, :], io["lnbb"].ap(),
                                 dsem["blks"])
                    p.wait("vector", dsem["blks"], c_rl)
                for s_, v_ in xt_gate[i % 2]:
                    p.wait("sync", s_, v_)
                c_x = p.dma("sync", xb[:, :],
                            src.ap()[tc * 128:(tc + 1) * 128, :], sx)
                xnm = nm(xb)
                p.wait("vector", sx, c_x)
                c_st = emit_ln_stats(p, sem, xb, stat6, stat2)
                c_rs = emit_rsig(p, sem, stat2, rsg, epsc, nhalf,
                                 gate_dv=max(c_st, rsg_gate))
                p.wait("vector", sem["ac"], c_rs)
                p.op("vector", lambda E, tc=tc: E.tensor_copy(
                    musig[:, 2 * tc:2 * tc + 1], stat2[:, 0:1]),
                    reads=("stat2",), writes=("musig",))
                p.op("vector", lambda E, tc=tc: E.tensor_copy(
                    musig[:, 2 * tc + 1:2 * tc + 2], rsg[:, :]),
                    reads=("rsg",), writes=("musig",))
                c_xbb = p.op("vector", lambda E, xb=xb, tc=tc: E.tensor_scalar(
                    xbb[:, :], xb[:, :], musig[:, 2 * tc:2 * tc + 1],
                    musig[:, 2 * tc + 1:2 * tc + 2], ALU.subtract, ALU.mult),
                    reads=(xnm, "musig"), writes=(nm(xbb),), inc=sem["dv"])
                xt_gate[i % 2] = [(sem["dv"], c_xbb)]
                c_st2 = emit_ln_stats(p, sem, xbb, stat6, stat2)
                c_rs2 = emit_rsig(p, sem, stat2, rsg, epsc, nhalf,
                                  gate_dv=c_st2)
                p.wait("vector", sem["ac"], c_rs2)
                rsg_gate = p.op("vector", lambda E: E.tensor_scalar(
                    t1[:, :], xbb[:, :], stat2[:, 0:1], rsg[:, 0:1],
                    ALU.subtract, ALU.mult),
                    reads=(nm(xbb), "stat2", "rsg"), writes=(nm(t1),),
                    inc=sem["dv"])
                p.op("vector", lambda E: E.tensor_tensor(
                    t1[:, :], t1[:, :], lnw_t[:, :], ALU.mult),
                    reads=(nm(t1), nm(lnw_t)), writes=(nm(t1),))
                p.op("vector", lambda E: E.tensor_tensor(
                    ntile[:, :], t1[:, :], lnb_t[:, :], ALU.add),
                    reads=(nm(t1), nm(lnb_t)), writes=(nm(ntile),))
                emit_mx_quant_core(p, sem, ntile, slice(0, DM), 64, qbufs)
                emit_mx_q8(p, sem, qbufs, slice(0, DM), q8a, slice(0, DM))
                c_qa = emit_mx_expand(p, sem, q8a, slice(0, DM),
                                      qbufs["ssc"], slice(0, 64), 64,
                                      ntile, slice(0, DM), inc=sem["dv"])
                p.wait("tensor", sem["dv"], c_qa)
                if i == 0 and f2_pe_last:
                    p.wait("scalar", sem["pe"], f2_pe_last)
                    p.wait("tensor", sem["ac"], f2_done_prev)
                if wp_on and sw == 1 and i == 0:
                    p.wait("vector", sem["pe"], wp_nt_pe[0])
                    p.wait("vector", sem["pe"], wp_nt_pe[1])
                evs = {}
                for g4 in range(KC // 4):
                    t = ps_tp[g4 % 2]
                    if g4 >= 2:
                        p.wait("tensor", sem["ac"], evs[g4 - 2])
                    c_pe = 0
                    for u in range(4):
                        kc = g4 * 4 + u
                        c_pe = p.op("tensor", lambda E, kc=kc, t=t, u=u:
                                    E.transpose(
                                        t[:, u * 128:(u + 1) * 128],
                                        ntile[:, kc * 128:(kc + 1) * 128],
                                        ident_b[:, :]), inc=sem["pe"])
                    p.wait("scalar", sem["pe"], c_pe)
                    out3 = bass.AP(
                        qa_fm, g4 * 4 * TBLK + i * 128,
                        [[KC * TBLK, 128], [TBLK, 4], [1, 128]])
                    in3 = t[:, 0:512].rearrange("p (f c) -> p f c", c=128)
                    evs[g4] = p.op("scalar", lambda E, out3=out3, in3=in3:
                                   E.copy(out3, in3),
                                   reads=(nm(t),), writes=(nm(qa_fm),),
                                   inc=sem["ac"])
                    if wp_on and sw == 0:
                        wp_tp_hist.append(evs[g4])
                p.wait("vector", sem["pe"], sem["pe"].n)
            c_qa_all = sem["ac"].n
            if wp_on and sw == 0:
                wp_nt_pe[0] = sem["pe"].n

            p.wait("tensor", sem["ac"], c_qa_all)
            mm_last = {}
            gq_done = {}
            h1_free = {}
            for osl in range(NOSL):
                wb = w1sl[osl % 2]
                bb1 = b1sl[osl % 2]
                sw1 = dsem["w1%d" % (osl % 2)]
                sb1 = dsem["b1%d" % (osl % 2)]
                if wp_on and sw == 0 and osl == 0:
                    for _k in range(4):
                        emit_wp_spec(_k)
                if osl >= 2:
                    p.wait("sync", sem["pe"], mm_last[osl - 2])
                    p.wait("gpsimd", sem["pe"], mm_last[osl - 2])
                if wp_on and sw == 0:
                    p.wait("sync", dsem["wst0"], wp_stg[4 * osl])
                    p.wait("sync", dsem["wst1"], wp_stg[4 * osl + 1])
                in_ap = bass.AP(io["w1t_s"], osl * OSL,
                                [[HID, 128], [128 * HID, KC], [1, OSL]])
                out_ap = wb.ap().rearrange("p (kc c) -> p kc c", c=OSL)
                c_w = 0
                for hh in range(2):
                    c_w = p.dma("sync", out_ap[:, hh * 8:(hh + 1) * 8, :],
                                in_ap[:, hh * 8:(hh + 1) * 8, :], sw1)
                c_b1 = p.dma("gpsimd", bb1[:, :],
                             io["b1r"].ap()[:, osl * OSL:(osl + 1) * OSL],
                             sb1)
                p.wait("tensor", sw1, c_w)
                p.wait("tensor", sb1, c_b1)
                gc = gcat[osl % 2]
                if osl >= 2:
                    p.wait("scalar", sem["dv"], gq_done[osl - 2])
                for i in range(SWTC):
                    idx = osl * SWTC + i
                    ph = ps_h1[idx % 2]
                    gate = h1_free.get(idx - 2)
                    if gate:
                        p.wait("tensor", sem["ac"], gate)
                    for kc in range(KC):
                        p.op("tensor", lambda E, kc=kc, i=i, ph=ph, wb=wb:
                             E.matmul(ph[:, 0:OSL],
                                      qa_fm[:, kc * TBLK + i * 128:
                                            kc * TBLK + (i + 1) * 128],
                                      wb[:, kc * OSL:(kc + 1) * OSL],
                                      start=(kc == 0), stop=False))
                    c_mm = p.op("tensor", lambda E, ph=ph, bb1=bb1: E.matmul(
                        ph[:, 0:OSL], ones1[:, :], bb1[:, :],
                        start=False, stop=True), inc=sem["pe"])
                    mm_last[osl] = c_mm
                    hb = h1b[idx % 4]
                    p.wait("scalar", sem["pe"], c_mm)
                    p.op("scalar", lambda E, hb=hb, ph=ph: E.copy(
                        hb[:, :], ph[:, 0:OSL]), reads=(nm(ph),),
                        writes=(nm(hb),))
                    c_gl = p.op("scalar", lambda E, hb=hb, gc=gc, i=i:
                                E.activation(gc[:, i * OSL:(i + 1) * OSL],
                                             hb[:, :], ACTF.Gelu_apprx_tanh,
                                             bias=0.0, scale=1.0),
                                reads=(nm(hb),), writes=(nm(gc),),
                                inc=sem["ac"])
                    h1_free[idx] = c_gl
                    if wp_on and sw == 0 and osl < NOSL - 1:
                        if i == 3:
                            emit_wp_spec(4 * (osl + 1))
                            emit_wp_spec(4 * (osl + 1) + 1)
                        elif i == SWTC - 1:
                            emit_wp_spec(4 * (osl + 1) + 2)
                            emit_wp_spec(4 * (osl + 1) + 3)
                p.wait("vector", sem["ac"], sem["ac"].n)
                emit_mx_quant_core(p, sem, gc, slice(0, SWTC * OSL), 64, gq)
                nbo = OSL // 32
                for i in range(SWTC):
                    emit_mx_q8(p, sem, gq, slice(i * OSL, (i + 1) * OSL),
                               q8g[i], slice(osl * OSL, (osl + 1) * OSL))
                    p.op("vector", lambda E, i=i, osl=osl: E.tensor_copy(
                        sscg[i][:, osl * nbo:(osl + 1) * nbo],
                        gq["ssc"][:, i * nbo:(i + 1) * nbo]),
                        reads=(nm(gq["ssc"]),), writes=(nm(sscg[i]),))
                gq_done[osl] = p.op("vector", lambda E: E.drain(),
                                    inc=sem["dv"])
            f1_mm_last = mm_last[NOSL - 1]

            h2_free = f2_done_prev
            for pr in range(SWTC // 2):
                i0, i1 = 2 * pr, 2 * pr + 1
                tc0, tc1 = tcs[i0], tcs[i1]
                if pr == 0:
                    p.wait("scalar", sem["pe"], f1_mm_last)
                else:
                    p.wait("scalar", sem["pe"], f2_pe_last)
                stg_ev = {}
                for g in range(32):
                    t = ps_tp[g % 2]
                    if g >= 2:
                        p.wait("tensor", sem["ac"], stg_ev[g - 2])
                    elif wp_on and sw == 0 and pr == 0 and wp_tp_hist:
                        p.wait("tensor", sem["ac"], wp_tp_hist[-2 + g])
                    exd = {}
                    for ch in (0, 1):
                        bi = 2 * (g % 2) + ch
                        qt = qgtm[bi]
                        gate = tp_used.get(bi)
                        if gate is not None:
                            p.wait("vector", sem["pe"], gate)
                        exd[ch] = emit_mx_expand(
                            p, sem, q8g[2 * pr + ch],
                            slice(2 * g * 128, (2 * g + 2) * 128),
                            sscg[2 * pr + ch],
                            slice(2 * g * 4, (2 * g + 2) * 4),
                            8, qt, slice(0, 256), inc=sem["dv"])
                    p.wait("tensor", sem["dv"], max(exd.values()))
                    c_tp = 0
                    for u in range(4):
                        ch = u % 2
                        bi = 2 * (g % 2) + ch
                        qt = qgtm[bi]
                        half = u // 2
                        c_tp = p.op("tensor", lambda E, t=t, qt=qt, u=u,
                                    half=half: E.transpose(
                                        t[:, u * 128:(u + 1) * 128],
                                        qt[:, half * 128:(half + 1) * 128],
                                        ident_b[:, :]),
                                    inc=sem["pe"])
                        tp_used[bi] = c_tp
                    p.wait("scalar", sem["pe"], c_tp)
                    stg_ev[g] = p.op(
                        "scalar", lambda E, g=g, t=t: E.copy(
                            qa_fm[:, g * 512:(g + 1) * 512], t[:, 0:512]),
                        reads=(nm(t),), writes=(nm(qa_fm),), inc=sem["ac"])
                c_stage = sem["ac"].n
                p.wait("tensor", sem["ac"], c_stage)
                if h2_free:
                    p.wait("tensor", sem["ac"], h2_free)
                for oc in range(OC):
                    wb2 = w2sl[oc % 2]
                    sw2 = dsem["w2%d" % (oc % 2)]
                    gate = w2_gate.get(oc % 2)
                    if gate is not None:
                        p.wait("sync", sem["pe"], gate)
                    if wp_on and sw == 0:
                        cgrp = oc // 16
                        p.wait("sync", dsem["wst0"],
                               wp_stg[4 * (8 * cgrp + 7) + 2])
                        p.wait("sync", dsem["wst1"],
                               wp_stg[4 * (8 * cgrp + 7) + 3])
                    c_w2 = p.dma("sync", wb2[:, :],
                                 bass.AP(io["w2t_s"], oc * 128 * DM,
                                         [[DM, 128], [1, DM]]), sw2)
                    p.wait("tensor", sw2, c_w2)
                    for ch, psX in ((0, psA), (1, psB)):
                        for ds in range(4):
                            p.op("tensor", lambda E, ds=ds, oc=oc, ch=ch,
                                 psX=psX, wb2=wb2: E.matmul(
                                     psX[:, ds * 512:(ds + 1) * 512],
                                     qa_fm[:, oc * 256 + ch * 128:
                                           oc * 256 + (ch + 1) * 128],
                                     wb2[:, ds * 512:(ds + 1) * 512],
                                     start=(oc == 0), stop=False))
                    c_h2m = p.op("tensor", lambda E: E.drain(), inc=sem["pe"])
                    w2_gate[oc % 2] = c_h2m
                for ch, psX in ((0, psA), (1, psB)):
                    for hh in range(4):
                        c_b = p.op("tensor", lambda E, hh=hh, psX=psX:
                                   E.matmul(psX[:, hh * 512:(hh + 1) * 512],
                                            ones1[:, :],
                                            b2sl[:, hh * 512:(hh + 1) * 512],
                                            start=False, stop=True),
                                   inc=sem["pe"])
                f2_pe_last = c_b
                h2_frees = []
                h2b_rd = 0
                if wp_on and sw == 0 and pr == 0:
                    p.wait("scalar", dsem["wst1"], wp_stg[4 * 31 + 3])
                    p.wait("vector", dsem["wst0"], wp_stg[4 * 31 + 2])
                for ch, psX, i, tc in ((0, psA, i0, tc0), (1, psB, i1, tc1)):
                    p.wait("scalar", sem["pe"], c_b)
                    if h2b_rd:
                        p.wait("scalar", sem["dv"], h2b_rd)
                    c_h2b = p.op("scalar", lambda E, psX=psX: E.copy(
                        h2b[:, :], psX[:, :]),
                        reads=(nm(psX),), writes=(nm(h2b),), inc=sem["ac"])
                    h2_frees.append(c_h2b)
                    hsrc = h2b
                    xb = xt[i % 2]
                    sx = dsem["xt%d" % (i % 2)]
                    for s_, v_ in xt_gate[i % 2]:
                        p.wait("sync", s_, v_)
                    c_x2 = p.dma("sync", xb[:, :],
                                 src.ap()[tc * 128:(tc + 1) * 128, :], sx)
                    p.wait("vector", sx, c_x2)
                    p.wait("vector", sem["ac"], c_h2b)
                    xnm = nm(xb)
                    p.op("vector", lambda E, tc=tc, xb=xb: E.tensor_scalar(
                        fb[:, :], xb[:, :], musig[:, 2 * tc:2 * tc + 1],
                        musig[:, 2 * tc + 1:2 * tc + 2], ALU.subtract,
                        ALU.mult),
                        reads=(xnm, "musig"), writes=(nm(fb),))
                    h2b_rd = p.op("vector", lambda E, hsrc=hsrc:
                                  E.tensor_tensor(
                                      fb[:, :], fb[:, :], hsrc[:, :],
                                      ALU.add),
                                  reads=(nm(fb), nm(hsrc)),
                                  writes=(nm(fb),), inc=sem["dv"])
                    c_xu = p.op("vector", lambda E, xb=xb:
                                E.scalar_tensor_tensor(
                                    xb[:, :], fb[:, :], 0.5, xb[:, :],
                                    ALU.mult, ALU.add),
                                reads=(nm(fb), xnm), writes=(xnm,),
                                inc=sem["dv"])
                    p.wait("sync", sem["dv"], c_xu)
                    c_ws = p.dma("sync",
                                 dst.ap()[tc * 128:(tc + 1) * 128, :],
                                 xb[:, :], sx)
                    xt_gate[i % 2] = [(sx, c_ws)]
                f2_done_prev = max(h2_frees)
                h2_free = f2_done_prev

        for k in ("xt0", "xt1"):
            p.wait("sync", dsem[k], dsem[k].n)
        with nc.Block() as blk:
            p.flush_into_block(blk)



def _phase_attn(nc, sem, dsem, sbt, pst, io, pers):
    ident_b = pers["ident_b"]
    perm_b = pers["perm_b"]
    mask_b = pers["mask_b"]
    stat6, stat2, rsg = pers["stat6"], pers["stat2"], pers["rsg"]
    zsum, rz, negmax = pers["zsum"], pers["rz"], pers["negmax"]
    epsc, nhalf = pers["epsc"], pers["nhalf"]
    xscr = io["xscr"]
    p = Prog(nc)

    stA = contextlib.ExitStack()
    a_fm = sbt(stA, "a_fm", [128, KC * S], BF16)
    xbuf = [sbt(stA, "at_x%d" % i, [128, DM], F32) for i in range(2)]
    atile = sbt(stA, "at_t", [128, DM], BF16)
    wcol = [sbt(stA, "pj_w%d" % i, [128, KC * 128], BF16) for i in range(2)]
    vwb = sbt(stA, "pj_vw", [128, KC * 512], BF16)
    pout = sbt(stA, "pj_o", [128, S], BF16)
    vout = sbt(stA, "pj_v", [128, 512], BF16)
    ps_tp = [pst(stA, "at_tp%d" % i, [128, 1024], BF16) for i in range(2)]
    ps_p = pst(stA, "pj_ps", [128, 512], F32)
    ps_v = pst(stA, "pj_pv", [128, 512], F32)

    with stA:
        c_at_prev = 0
        xb_gate = [0, 0]
        for tc in range(NTC):
            xb = xbuf[tc % 2]
            sx = dsem["xt%d" % (tc % 2)]
            if xb_gate[tc % 2]:
                p.wait("sync", sem["dv"], xb_gate[tc % 2])
            c_x = p.dma("sync", xb[:, :],
                        xscr.ap()[tc * 128:(tc + 1) * 128, :], sx)
            xnm = nm(xb)
            p.wait("vector", sx, c_x)
            c_st = emit_ln_stats(p, sem, xb, stat6, stat2)
            c_rs = emit_rsig(p, sem, stat2, rsg, epsc, nhalf,
                             gate_dv=max(c_st, c_at_prev))
            p.wait("vector", sem["ac"], c_rs)
            c_at = p.op("vector", lambda E, xb=xb: E.tensor_scalar(
                atile[:, :], xb[:, :], stat2[:, 0:1], rsg[:, 0:1],
                ALU.subtract, ALU.mult), reads=(xnm, "stat2", "rsg"),
                writes=(nm(atile),), inc=sem["dv"])
            c_at_prev = c_at
            xb_gate[tc % 2] = c_at
            p.wait("tensor", sem["dv"], c_at)
            evs = {}
            for g4 in range(KC // 4):
                t = ps_tp[g4 % 2]
                if g4 >= 2:
                    p.wait("tensor", sem["ac"], evs[g4 - 2])
                c_pe = 0
                for u in range(4):
                    kc = g4 * 4 + u
                    c_pe = p.op("tensor", lambda E, kc=kc, t=t, u=u:
                                E.transpose(t[:, u * 128:(u + 1) * 128],
                                            atile[:, kc * 128:(kc + 1) * 128],
                                            ident_b[:, :]), inc=sem["pe"])
                p.wait("scalar", sem["pe"], c_pe)
                out3 = bass.AP(a_fm, g4 * 4 * S + tc * 128,
                               [[KC * S, 128], [S, 4], [1, 128]])
                in3 = t.ap()[:, 0:512].rearrange("p (f c) -> p f c", c=128)
                evs[g4] = p.op("scalar", lambda E, out3=out3, in3=in3:
                               E.copy(out3, in3),
                               reads=(nm(t),), writes=(nm(a_fm),),
                               inc=sem["ac"])
            p.wait("vector", sem["pe"], sem["pe"].n)
        c_afm = sem["ac"].n
        p.wait("tensor", sem["ac"], c_afm)

        proj_fin = {}
        spl_prev = 0
        for sidx in range(20):
            wb = wcol[sidx % 2]
            swc = dsem["wc%d" % (sidx % 2)]
            gate = proj_fin.get(sidx - 2)
            if gate:
                p.wait("gpsimd", sem["pe"], gate)
            if sidx < 16:
                wt, col0, spill, sp_i = io["wq"], sidx * 128, io["q_s"], sidx
                wcols = 2048
            else:
                wt, col0, spill, sp_i = (io["wk"], (sidx - 16) * 128,
                                         io["k_s"], sidx - 16)
                wcols = 512
            in_ap = bass.AP(wt, col0,
                            [[wcols, 128], [128 * wcols, KC], [1, 128]])
            out_ap = wb.ap().rearrange("p (kc c) -> p kc c", c=128)
            c_wl = p.dma("gpsimd", out_ap, in_ap, swc)
            p.wait("tensor", swc, c_wl)
            for ts in range(4):
                if spl_prev:
                    p.wait("scalar", dsem["spl"], spl_prev)
                for kc in range(KC):
                    p.op("tensor", lambda E, kc=kc, ts=ts, wb=wb: E.matmul(
                        ps_p[:, :], wb[:, kc * 128:(kc + 1) * 128],
                        a_fm[:, kc * S + ts * 512:kc * S + (ts + 1) * 512],
                        start=(kc == 0), stop=(kc == KC - 1)))
                c_mm = p.op("tensor", lambda E: E.drain(), inc=sem["pe"])
                p.wait("scalar", sem["pe"], c_mm)
                c_ev = p.op("scalar", lambda E, ts=ts: E.copy(
                    pout[:, ts * 512:(ts + 1) * 512], ps_p[:, :]),
                    reads=(nm(ps_p),), writes=(nm(pout),), inc=sem["ac"])
                p.wait("tensor", sem["ac"], c_ev)
            proj_fin[sidx] = sem["pe"].n
            p.wait("gpsimd", sem["ac"], sem["ac"].n)
            spl_prev = p.dma("gpsimd",
                             bass.AP(spill, sp_i * 128 * S,
                                     [[S, 128], [1, S]]),
                             pout[:, :], dsem["spl"])
        in_ap = bass.AP(io["wv"], 0, [[512, 128], [128 * 512, KC], [1, 512]])
        c_vw = p.dma("gpsimd",
                     vwb.ap().rearrange("p (kc c) -> p kc c", c=512),
                     in_ap, dsem["vwl"])
        p.wait("tensor", dsem["vwl"], c_vw)
        vsp_prev = 0
        for tc in range(NTC):
            for kc in range(KC):
                p.op("tensor", lambda E, kc=kc, tc=tc: E.matmul(
                    ps_v[:, :],
                    a_fm[:, kc * S + tc * 128:kc * S + (tc + 1) * 128],
                    vwb[:, kc * 512:(kc + 1) * 512],
                    start=(kc == 0), stop=(kc == KC - 1)))
            c_mm = p.op("tensor", lambda E: E.drain(), inc=sem["pe"])
            p.wait("scalar", sem["pe"], c_mm)
            if vsp_prev:
                p.wait("scalar", dsem["vsp"], vsp_prev)
            c_ev = p.op("scalar", lambda E: E.copy(vout[:, :], ps_v[:, :]),
                        reads=(nm(ps_v),), writes=(nm(vout),), inc=sem["ac"])
            p.wait("tensor", sem["ac"], c_ev)
            p.wait("gpsimd", sem["ac"], c_ev)
            out_ap = bass.AP(io["v_s"], tc * 128 * 128,
                             [[128, 128], [NTC * 128 * 128, 4], [1, 128]])
            vsp_prev = p.dma("gpsimd", out_ap,
                             vout.ap().rearrange("p (g c) -> p g c", c=128),
                             dsem["vsp"])
        p.wait("sync", dsem["vsp"], dsem["vsp"].n)
        p.wait("sync", dsem["spl"], dsem["spl"].n)

    stC = contextlib.ExitStack()
    tabs = {name: sbt(stC, "tab_" + name, [128, S], F32)
            for name in ("cq", "sq", "ck", "sk")}
    kst = sbt(stC, "sc_k", [128, S], BF16)
    krot = sbt(stC, "sc_kr", [128, S], BF16)
    vst = sbt(stC, "sc_v", [128, NTC * 128], BF16)
    qst = [sbt(stC, "sc_q%d" % i, [128, S], BF16) for i in range(2)]
    qrot = sbt(stC, "sc_qr", [128, S], BF16)
    t1r = sbt(stC, "sc_t1", [128, 512], F32)
    t2r = sbt(stC, "sc_t2", [128, 512], F32)
    esb = [sbt(stC, "sc_e%d" % i, [128, S], BF16) for i in range(2)]
    etb = [sbt(stC, "sc_et%d" % i, [128, 512], BF16) for i in range(2)]
    ao = sbt(stC, "sc_ao", [128, S], BF16)
    ps_sc = pst(stC, "sc_ps", [128, S], F32)
    ps_rp = pst(stC, "sc_rp", [128, 512], F32)
    ps_tp2 = [pst(stC, "sc_tp%d" % i, [128, 1024], BF16) for i in range(2)]
    ps_pv = pst(stC, "sc_pv", [128, 512], F32)

    def rope(src_t, dst_t, ctab, stab):
        for q4 in range(4):
            sl = slice(q4 * 512, (q4 + 1) * 512)
            c_pm = p.op("tensor", lambda E, sl=sl, src_t=src_t: E.matmul(
                ps_rp[:, :], perm_b[:, :], src_t[:, sl],
                start=True, stop=True), inc=sem["pe"])
            p.op("vector", lambda E, sl=sl, src_t=src_t, ctab=ctab:
                 E.tensor_tensor(t1r[:, :], src_t[:, sl], ctab[:, sl],
                                 ALU.mult),
                 reads=(nm(src_t), nm(ctab)), writes=(nm(t1r),))
            p.wait("vector", sem["pe"], c_pm)
            p.op("vector", lambda E, sl=sl, stab=stab:
                 E.scalar_tensor_tensor(
                     t2r[:, :], ps_rp[:, :], 0.0, stab[:, sl],
                     ALU.add, ALU.mult),
                 reads=(nm(ps_rp), nm(stab)), writes=(nm(t2r),))
            c_rt = p.op("vector", lambda E, sl=sl, dst_t=dst_t:
                        E.tensor_tensor(dst_t[:, sl], t1r[:, :], t2r[:, :],
                                        ALU.add),
                        reads=(nm(t1r), nm(t2r)), writes=(nm(dst_t),),
                        inc=sem["dv"])
            p.wait("tensor", sem["dv"], c_rt)
        return sem["dv"].n

    with stC:
        for name, srcd in (("cq", io["cqd"]), ("sq", io["sqd"]),
                           ("ck", io["ckd"]), ("sk", io["skd"])):
            p.dma("sync", tabs[name][:, :], srcd.ap(), dsem["blks"])
        p.wait("vector", dsem["blks"], dsem["blks"].n)
        c_exp_prev = 0
        c_nr_prev = 0
        c_pv_prev = 0
        pv_gate = [0, 0]
        gp_end = (0, 0)
        strip_end = {}
        strip_scores_end = 0
        aosp_prev = 0
        for gp in range(4):
            p.wait("sync", sem["pe"], gp_end[0])
            p.wait("sync", sem["dv"], gp_end[1])
            c_k = p.dma("sync", kst[:, :],
                        bass.AP(io["k_s"], gp * 128 * S, [[S, 128], [1, S]]),
                        dsem["kst"])
            in_ap = bass.AP(io["v_s"], gp * NTC * 128 * 128,
                            [[128, 128], [128 * 128, NTC], [1, 128]])
            c_v = p.dma("sync",
                        vst.ap().rearrange("p (t c) -> p t c", c=128),
                        in_ap, dsem["vst"])
            p.wait("vector", dsem["kst"], c_k)
            p.wait("tensor", dsem["kst"], c_k)
            p.wait("vector", sem["pe"], gp_end[0])
            rope(kst, krot, tabs["ck"], tabs["sk"])
            p.wait("tensor", dsem["vst"], c_v)
            for i in range(4):
                sidx = 4 * gp + i
                qb = qst[i % 2]
                sq = dsem["qs%d" % (i % 2)]
                g2 = strip_end.get(sidx - 2)
                if g2:
                    p.wait("sync", sem["pe"], g2[0])
                    p.wait("sync", sem["dv"], g2[1])
                c_q = p.dma("sync", qb[:, :],
                            bass.AP(io["q_s"], sidx * 128 * S,
                                    [[S, 128], [1, S]]), sq)
                p.wait("vector", sq, c_q)
                p.wait("tensor", sq, c_q)
                if sidx > 0:
                    p.wait("vector", sem["pe"], strip_scores_end)
                c_qrot = rope(qb, qrot, tabs["cq"], tabs["sq"])
                strip_end[sidx] = (sem["pe"].n, c_qrot)
                p.wait("tensor", sem["dv"], c_qrot)
                def emit_tpv(n, qtc, hh, eb_src, c_nr):
                    p.wait("tensor", sem["dv"], c_nr)
                    ng = (qtc + 1 + 3) // 4
                    pvst = {}

                    def emit_grp(g, qtc=qtc, eb_src=eb_src):
                        t = ps_tp2[g % 2]
                        nt = min(4, qtc + 1 - g * 4)
                        c_tp = 0
                        for u in range(nt):
                            j = g * 4 + u
                            c_tp = p.op("tensor", lambda E, j=j, t=t,
                                        u=u, eb_src=eb_src: E.transpose(
                                            t[:, u * 128:(u + 1) * 128],
                                            eb_src[:, j * 128:(j + 1) * 128],
                                            ident_b[:, :]),
                                        inc=sem["pe"])
                        p.wait("scalar", sem["pe"], c_tp)
                        eb = etb[g % 2]
                        c_ev = p.op("scalar", lambda E, eb=eb, t=t,
                                    nt=nt: E.copy(eb[:, 0:nt * 128],
                                                  t[:, 0:nt * 128]),
                                    reads=(nm(t),), writes=(nm(eb),),
                                    inc=sem["ac"])
                        pvst[g] = (c_ev, eb, nt)

                    emit_grp(0)
                    for g in range(ng):
                        if g + 1 < ng:
                            emit_grp(g + 1)
                        c_ev, eb, nt = pvst.pop(g)
                        p.wait("tensor", sem["ac"], c_ev)
                        for u in range(nt):
                            j = g * 4 + u
                            p.op("tensor", lambda E, j=j, eb=eb, u=u,
                                 qtc=qtc: E.matmul(
                                     ps_pv[:, 0:128],
                                     vst[:, j * 128:(j + 1) * 128],
                                     eb[:, u * 128:(u + 1) * 128],
                                     start=(j == 0), stop=(j == qtc)))
                    c_pv = p.op("tensor", lambda E: E.drain(),
                                inc=sem["pe"])
                    p.wait("scalar", sem["pe"], c_pv)
                    c_ao = p.op("scalar", lambda E, hh=hh, qtc=qtc:
                                E.copy(ao[hh * 64:(hh + 1) * 64,
                                          qtc * 128:(qtc + 1) * 128],
                                       ps_pv[hh * 64:(hh + 1) * 64,
                                             0:128]),
                                reads=(nm(ps_pv),), writes=(nm(ao),),
                                inc=sem["ac"])
                    p.wait("tensor", sem["ac"], c_ao)
                    return c_pv

                pend = None
                nit = 0
                for hh in range(2):
                    rows = slice(hh * 64, (hh + 1) * 64)
                    for qtc in range(NTC):
                        L = (qtc + 1) * 128
                        n = nit
                        nit += 1
                        eb_cur = esb[n % 2]
                        if c_exp_prev:
                            p.wait("tensor", sem["ac"], c_exp_prev)
                        slabs = []
                        off = 0
                        while off < L:
                            w = min(512, L - off)
                            slabs.append((off, w))
                            off += w
                        ns = len(slabs)
                        for si, (off, w) in enumerate(slabs):
                            p.op("tensor", lambda E, off=off, w=w, rows=rows,
                                 qtc=qtc: E.matmul(
                                     ps_sc[:, off:off + w],
                                     qrot[rows, qtc * 128:(qtc + 1) * 128],
                                     krot[rows, off:off + w],
                                     start=True, stop=True))
                            c_sl = p.op("tensor", lambda E: E.drain(),
                                        inc=sem["pe"])
                            p.wait("vector", sem["pe"], c_sl)
                            if si == ns - 1:
                                p.op("vector", lambda E, qtc=qtc:
                                     E.tensor_tensor(
                                         ps_sc[:, qtc * 128:(qtc + 1) * 128],
                                         ps_sc[:, qtc * 128:(qtc + 1) * 128],
                                         mask_b[:, :], ALU.add),
                                     reads=(nm(ps_sc), "mask_b"),
                                     writes=(nm(ps_sc),))
                            p.op("vector", lambda E, si=si, off=off, w=w:
                                 E.tensor_reduce(
                                     stat6[:, si:si + 1],
                                     ps_sc[:, off:off + w], AX.X, ALU.max),
                                 reads=(nm(ps_sc),), writes=("stat6",))
                        c_mx = p.op("vector", lambda E, ns=ns, n=n:
                                    E.tensor_reduce(
                                        negmax[:, n % 2:n % 2 + 1],
                                        stat6[:, 0:ns], AX.X,
                                        ALU.max, negate=True),
                                    reads=("stat6",), writes=("negmax",),
                                    inc=sem["dv"])
                        if pend is not None:
                            c_pv_prev = emit_tpv(*pend)
                            pv_gate[(n - 1) % 2] = c_pv_prev
                        p.wait("scalar", sem["dv"], c_mx)
                        if c_nr_prev:
                            p.wait("scalar", sem["dv"], c_nr_prev)
                        if pv_gate[n % 2]:
                            p.wait("scalar", sem["pe"], pv_gate[n % 2])
                        c_ex = p.op("scalar", lambda E, L=L, n=n,
                                    eb_cur=eb_cur: E.activation(
                                        eb_cur[:, 0:L], ps_sc[:, 0:L],
                                        ACTF.Exp,
                                        bias=negmax[:, n % 2:n % 2 + 1],
                                        scale=1.0, accum_out=zsum[:, :]),
                                    reads=(nm(ps_sc), "negmax"),
                                    writes=(nm(eb_cur), "zsum"),
                                    inc=sem["ac"])
                        c_exp_prev = c_ex
                        p.wait("vector", sem["ac"], c_ex)
                        p.op("vector", lambda E: E.reciprocal(
                            rz[:, :], zsum[:, :]), reads=("zsum",),
                            writes=("rz",))
                        c_nr = p.op("vector", lambda E, L=L, eb_cur=eb_cur:
                                    E.tensor_scalar(
                                        eb_cur[:, 0:L], eb_cur[:, 0:L],
                                        rz[:, 0:1], None, ALU.mult),
                                    reads=(nm(eb_cur), "rz"),
                                    writes=(nm(eb_cur),), inc=sem["dv"])
                        c_nr_prev = c_nr
                        pend = (n, qtc, hh, eb_cur, c_nr)
                if pend is not None:
                    c_pv_prev = emit_tpv(*pend)
                    pv_gate[(nit - 1) % 2] = c_pv_prev
                    pend = None
                strip_scores_end = sem["pe"].n
                p.wait("gpsimd", sem["ac"], sem["ac"].n)
                if aosp_prev:
                    p.wait("scalar", dsem["aosp"], aosp_prev)
                aosp_prev = p.dma("gpsimd",
                                  bass.AP(io["ao_s"], sidx * 128 * S,
                                          [[S, 128], [1, S]]), ao[:, :],
                                  dsem["aosp"])
            gp_end = (sem["pe"].n, sem["dv"].n)
        p.wait("sync", dsem["aosp"], dsem["aosp"].n)

    stD = contextlib.ExitStack()
    wob = [sbt(stD, "op_w%d" % s, [128, DM], BF16) for s in range(16)]
    aoc = [sbt(stD, "op_a%d" % i, [128, 16 * 128], BF16) for i in range(2)]
    xb2 = [sbt(stD, "op_x%d" % i, [128, DM], F32) for i in range(2)]
    ps_o = pst(stD, "op_ps", [128, DM], F32)
    with stD:
        for s in range(16):
            p.dma("gpsimd", wob[s][:, :],
                  bass.AP(io["wo"], s * 128 * DM, [[DM, 128], [1, DM]]),
                  dsem["blkp"])
        c_wo = dsem["blkp"].n
        p.wait("tensor", dsem["blkp"], c_wo)
        res_done = {}
        for tc in range(NTC):
            ab = aoc[tc % 2]
            xb = xb2[tc % 2]
            sao = dsem["ao%d" % (tc % 2)]
            sxb = dsem["xb%d" % (tc % 2)]
            gate = res_done.get(tc - 2)
            if gate:
                p.wait("sync", sxb, gate)
            in_ap = bass.AP(io["ao_s"], tc * 128,
                            [[S, 128], [128 * S, 16], [1, 128]])
            c_ao = p.dma("sync",
                         ab.ap().rearrange("p (s c) -> p s c", c=128),
                         in_ap, sao)
            c_x = p.dma("sync", xb[:, :],
                        xscr.ap()[tc * 128:(tc + 1) * 128, :], sxb)
            p.wait("tensor", sao, c_ao)
            for s in range(16):
                for ds in range(4):
                    p.op("tensor", lambda E, s=s, ds=ds, ab=ab: E.matmul(
                        ps_o[:, ds * 512:(ds + 1) * 512],
                        ab[:, s * 128:(s + 1) * 128],
                        wob[s][:, ds * 512:(ds + 1) * 512],
                        start=(s == 0), stop=(s == 15)))
            c_om = p.op("tensor", lambda E: E.drain(), inc=sem["pe"])
            p.wait("vector", sem["pe"], c_om)
            p.wait("vector", sxb, c_x)
            xnm = nm(xb)
            c_xr = p.op("vector", lambda E, xb=xb: E.tensor_tensor(
                xb[:, :], xb[:, :], ps_o[:, :], ALU.add),
                reads=(xnm, nm(ps_o)), writes=(xnm,), inc=sem["dv"])
            p.wait("tensor", sem["dv"], c_xr)
            p.wait("sync", sem["dv"], c_xr)
            c_st = p.dma("sync", xscr.ap()[tc * 128:(tc + 1) * 128, :],
                         xb[:, :], sxb)
            res_done[tc] = c_st
        for k in ("xb0", "xb1"):
            p.wait("sync", dsem[k], dsem[k].n)
        with nc.Block() as blk:
            p.flush_into_block(blk)



def kernel(L0_attn_x, L0_attn_Wq, L0_attn_Wk, L0_attn_Wv, L0_attn_Wo,
           L0_ffn_ln_w, L0_ffn_ln_b, L0_ffn_W1, L0_ffn_b1, L0_ffn_W2,
           L0_ffn_b2):
    x = np.ascontiguousarray(np.asarray(L0_attn_x, dtype=np.float32))
    consts = _host_consts()
    perm = _head_perm()
    wq_p = np.ascontiguousarray(np.asarray(L0_attn_Wq, np.float32)[:, perm])
    wo_p = np.ascontiguousarray(np.asarray(L0_attn_Wo, np.float32)[perm, :])
    wk_ = np.ascontiguousarray(np.asarray(L0_attn_Wk, np.float32))
    wv_ = np.ascontiguousarray(np.asarray(L0_attn_Wv, np.float32))
    w1_ = np.ascontiguousarray(np.asarray(L0_ffn_W1, np.float32))
    w2_ = np.ascontiguousarray(np.asarray(L0_ffn_W2, np.float32))
    lnw = np.ascontiguousarray(np.broadcast_to(
        np.asarray(L0_ffn_ln_w, np.float32)[None, :], (128, DM)))
    lnb = np.ascontiguousarray(np.broadcast_to(
        np.asarray(L0_ffn_ln_b, np.float32)[None, :], (128, DM)))
    b1v = np.ascontiguousarray(np.asarray(L0_ffn_b1, np.float32)[None, :])
    b2v = np.ascontiguousarray(np.asarray(L0_ffn_b2, np.float32)[None, :])

    nc = build_nc()
    base = {"wq": wq_p, "wk": wk_, "wv": wv_, "wo": wo_p, "w1": w1_,
            "w2": w2_, "lnww": lnw, "lnbb": lnb, "b1r": b1v, "b2r": b2v,
            "cqd": consts["cq"], "sqd": consts["sq"], "ckd": consts["ck"],
            "skd": consts["sk"], "permd": consts["perm"],
            "identd": consts["ident"], "maskd": consts["mask"]}
    in_maps = [dict(base, xin=x[c]) for c in range(8)]
    res = run_bass_kernel_spmd(nc, in_maps, core_ids=list(range(8)))
    return np.stack([np.asarray(r["yout"], dtype=np.float32)
                     for r in res.results], axis=0)


# version 12
# speedup vs baseline: 1.7331x; 1.0043x over previous
import contextlib
import os
import numpy as np
import concourse.bass as bass
import concourse.mybir as mybir
from concourse.bass_utils import run_bass_kernel_spmd

try:
    import ml_dtypes
    _BF16_NP = ml_dtypes.bfloat16
except ImportError:
    _BF16_NP = None

F32 = mybir.dt.float32
BF16 = mybir.dt.bfloat16
I32 = mybir.dt.int32
I8 = mybir.dt.int8
ALU = mybir.AluOpType
AX = mybir.AxisListType
ACTF = mybir.ActivationFunctionType

S = 2048
DM = 2048
HID = 8192
NTC = S // 128
KC = DM // 128
OC = HID // 128
TBLK = 1024
NSW = S // TBLK
SWTC = TBLK // 128
OSL = 256
NOSL = HID // OSL
HD = 64
EPS = 1e-5
NEG = -1.0e30
MAGIC = 12582912.0
CLIPC = float(np.nextafter(np.float32(127.5), np.float32(0)))

ENGS = ("sync", "gpsimd", "vector", "scalar", "tensor")
SIM_DRAINS = True
PHASES = ("wprep", "fa", "attn", "fb")
DBG_SCRATCH = False


class Sem:
    def __init__(self, h):
        self.h = h
        self.n = 0

    def inc(self, k=1):
        self.n += k
        return self.n


class Prog:
    def __init__(self, nc):
        self.nc = nc
        self.q = {e: [] for e in ENGS}
        self.dirty = {e: set() for e in ENGS}

    def wait(self, eng, sem, val):
        if val > 0:
            self.q[eng].append(lambda E, s=sem, v=val: E.wait_ge(s.h, v))

    def op(self, eng, build, reads=(), writes=(), inc=None, inc_n=1):
        names = set(reads) | set(writes)
        if (SIM_DRAINS and eng in ("vector", "scalar", "gpsimd")
                and (names & self.dirty[eng])):
            self.q[eng].append(lambda E: E.drain())
            self.dirty[eng] = set()
        self.dirty[eng] |= set(writes)
        if inc is not None:
            def fn(E, b=build, s=inc, n=inc_n):
                b(E).then_inc(s.h, n)
            self.q[eng].append(fn)
            return inc.inc(inc_n)
        self.q[eng].append(build)
        return None

    def dma(self, eng, out_ap, in_ap, sem, **kw):
        def fn(E, o=out_ap, i=in_ap, s=sem, kw=kw):
            E.dma_start(o, i, **kw).then_inc(s.h, 16)
        self.q[eng].append(fn)
        return sem.inc(16)

    def flush_into_block(self, block):
        q = self.q

        @block.sync
        def _(e):
            for fn in q["sync"]:
                fn(e)

        @block.gpsimd
        def _(e):
            for fn in q["gpsimd"]:
                fn(e)

        @block.vector
        def _(e):
            for fn in q["vector"]:
                fn(e)

        @block.scalar
        def _(e):
            for fn in q["scalar"]:
                fn(e)

        @block.tensor
        def _(e):
            for fn in q["tensor"]:
                fn(e)


def nm(t):
    return t.ap().name if callable(getattr(t, "ap", None)) else t.name


def _host_consts():
    half = HD // 2
    inv = (1.0 / np.power(np.float32(10000.0),
                          np.arange(half, dtype=np.float32) / np.float32(half))
           ).astype(np.float32)
    pos = np.arange(S, dtype=np.float32)
    fr = np.outer(pos, inv).astype(np.float32)
    cos = np.cos(fr).astype(np.float32)
    sin = np.sin(fr).astype(np.float32)
    j_of_r = (np.arange(128) % HD) // 2
    C = cos[:, j_of_r].T.astype(np.float32)
    Sg = sin[:, j_of_r].T.astype(np.float32)
    sign = np.where(np.arange(128) % 2 == 0, -1.0, 1.0).astype(np.float32)
    Stab = (Sg * sign[:, None]).astype(np.float32)
    swap = np.arange(128)
    swap = swap + np.where(swap % 2 == 0, 1, -1)
    perm = np.zeros((128, 128), dtype=np.float32)
    perm[swap, np.arange(128)] = 1.0
    ident = np.eye(128, dtype=np.float32)
    ii = np.arange(128)
    mask = np.where(ii[None, :] <= ii[:, None], 0.0, NEG).astype(np.float32)
    if _BF16_NP is not None:
        perm = perm.astype(_BF16_NP)
        ident = ident.astype(_BF16_NP)
    return dict(cq=(C * np.float32(0.125)).astype(np.float32),
                sq=(Stab * np.float32(0.125)).astype(np.float32),
                ck=C, sk=Stab, perm=perm, ident=ident, mask=mask)


def _head_perm():
    order = []
    for gp in range(4):
        for i in range(4):
            for h in (8 * gp + i, 8 * gp + 4 + i):
                order.extend(range(h * HD, (h + 1) * HD))
    return np.asarray(order, dtype=np.int64)



def emit_mx_quant_core(p, sem, src_t, src_sl, nblk, bufs):
    am, rsc, ssc, y, r = (bufs["am"], bufs["rsc"], bufs["ssc"], bufs["y"],
                          bufs["r"])
    n = nblk * 32
    src3 = src_t.ap()[:, src_sl].rearrange("p (nb k) -> p nb k", k=32)
    p.op("vector", lambda E: E.tensor_reduce(
        am[:, 0:nblk], src3, AX.X, ALU.max, apply_absolute_value=True),
        reads=(nm(src_t),), writes=(nm(am),))
    p.op("vector", lambda E: E.tensor_single_scalar(
        rsc[:, 0:nblk], am.ap()[:, 0:nblk].bitcast(I32), 23,
        ALU.logical_shift_right), reads=(nm(am),), writes=(nm(rsc),))
    p.op("vector", lambda E: E.tensor_scalar_max(
        rsc[:, 0:nblk], rsc[:, 0:nblk], 64),
        reads=(nm(rsc),), writes=(nm(rsc),))
    p.op("vector", lambda E: E.tensor_scalar(
        ssc[:, 0:nblk], rsc[:, 0:nblk], -6, 8388608, ALU.add, ALU.mult),
        reads=(nm(rsc),), writes=(nm(ssc),))
    p.op("vector", lambda E: E.tensor_scalar(
        rsc[:, 0:nblk], rsc[:, 0:nblk], -260, -8388608, ALU.add, ALU.mult),
        reads=(nm(rsc),), writes=(nm(rsc),))
    rb = rsc.ap()[:, 0:nblk].bitcast(F32).unsqueeze(2).broadcast_to(
        [128, nblk, 32])
    y3 = y.ap()[:, 0:n].rearrange("p (nb k) -> p nb k", k=32)
    p.op("vector", lambda E: E.tensor_tensor(y3, src3, rb, ALU.mult),
         reads=(nm(src_t), nm(rsc)), writes=(nm(y),))
    return p.op("vector", lambda E: E.tensor_scalar(
        r[:, 0:n], y[:, 0:n], MAGIC, MAGIC, ALU.add, ALU.subtract),
        reads=(nm(y),), writes=(nm(r),), inc=sem["dv"])


def emit_mx_q8(p, sem, bufs, r_sl, q8_t, q8_sl, inc=None):
    r = bufs["r"]
    return p.op("vector", lambda E: E.tensor_scalar(
        q8_t.ap()[:, q8_sl], r[:, r_sl], 127.0, None, ALU.min),
        reads=(nm(r),), writes=(nm(q8_t),), inc=inc)


def emit_mx_expand(p, sem, q8_t, q8_sl, ssc_t, ssc_sl, nblk, out_t, out_sl,
                   inc=None):
    q83 = q8_t.ap()[:, q8_sl].rearrange("p (nb k) -> p nb k", k=32)
    sc3 = ssc_t.ap()[:, ssc_sl].bitcast(F32).unsqueeze(2).broadcast_to(
        [128, nblk, 32])
    o3 = out_t.ap()[:, out_sl].rearrange("p (nb k) -> p nb k", k=32)
    return p.op("vector", lambda E: E.scalar_tensor_tensor(
        o3, q83, 0.0, sc3, ALU.add, ALU.mult),
        reads=(nm(q8_t), nm(ssc_t)), writes=(nm(out_t),), inc=inc)


def emit_ln_stats(p, sem, src_t, stat6, stat2):
    for j in range(4):
        p.op("vector", lambda E, j=j, src_t=src_t: E.bn_stats(
            stat6[:, 6 * j:6 * (j + 1)], src_t[:, 512 * j:512 * (j + 1)]),
            reads=(nm(src_t),), writes=("stat6",))
    return p.op("vector", lambda E: E.bn_aggr(stat2[:, :], stat6[:, :]),
                reads=("stat6",), writes=("stat2",), inc=sem["dv"])


def emit_rsig(p, sem, stat2, rsg, epsc, nhalf, gate_dv=0):
    if gate_dv:
        p.wait("scalar", sem["dv"], gate_dv)
    p.op("scalar", lambda E: E.activation(
        rsg[:, :], stat2[:, 1:2], ACTF.Ln, bias=epsc[:, 0:1], scale=1.0),
        reads=("stat2",), writes=("rsg",))
    return p.op("scalar", lambda E: E.activation(
        rsg[:, :], rsg[:, :], ACTF.Exp, bias=0.0, scale=nhalf[:, 0:1]),
        reads=("rsg",), writes=("rsg",), inc=sem["ac"])



def build_nc(phases=None):
    phases = PHASES if phases is None else phases
    scratch_kind = "ExternalOutput" if DBG_SCRATCH else "Internal"
    nc = bass.Bass("TRN2", target_bir_lowering=False)

    xin = nc.dram_tensor("xin", [S, DM], F32, kind="ExternalInput")
    wq = nc.dram_tensor("wq", [DM, DM], F32, kind="ExternalInput")
    wk = nc.dram_tensor("wk", [DM, 512], F32, kind="ExternalInput")
    wv = nc.dram_tensor("wv", [DM, 512], F32, kind="ExternalInput")
    wo = nc.dram_tensor("wo", [DM, DM], F32, kind="ExternalInput")
    w1 = nc.dram_tensor("w1", [HID, DM], F32, kind="ExternalInput")
    w2 = nc.dram_tensor("w2", [DM, HID], F32, kind="ExternalInput")
    lnww = nc.dram_tensor("lnww", [128, DM], F32, kind="ExternalInput")
    lnbb = nc.dram_tensor("lnbb", [128, DM], F32, kind="ExternalInput")
    b1r = nc.dram_tensor("b1r", [1, HID], F32, kind="ExternalInput")
    b2r = nc.dram_tensor("b2r", [1, DM], F32, kind="ExternalInput")
    cqd = nc.dram_tensor("cqd", [128, S], F32, kind="ExternalInput")
    sqd = nc.dram_tensor("sqd", [128, S], F32, kind="ExternalInput")
    ckd = nc.dram_tensor("ckd", [128, S], F32, kind="ExternalInput")
    skd = nc.dram_tensor("skd", [128, S], F32, kind="ExternalInput")
    permd = nc.dram_tensor("permd", [128, 128], BF16, kind="ExternalInput")
    identd = nc.dram_tensor("identd", [128, 128], BF16, kind="ExternalInput")
    maskd = nc.dram_tensor("maskd", [128, 128], F32, kind="ExternalInput")
    yout = nc.dram_tensor("yout", [S, DM], F32, kind="ExternalOutput")

    xscr = nc.dram_tensor("xscr", [S, DM], F32, kind=scratch_kind)
    w1t_s = nc.dram_tensor("w1t_s", [KC, 128, HID], BF16, kind=scratch_kind)
    w2t_s = nc.dram_tensor("w2t_s", [OC, 128, DM], BF16, kind=scratch_kind)
    q_s = nc.dram_tensor("q_s", [16, 128, S], BF16, kind=scratch_kind)
    k_s = nc.dram_tensor("k_s", [4, 128, S], BF16, kind=scratch_kind)
    v_s = nc.dram_tensor("v_s", [4, NTC, 128, 128], BF16, kind=scratch_kind)
    ao_s = nc.dram_tensor("ao_s", [16, 128, S], BF16, kind=scratch_kind)

    st = contextlib.ExitStack()
    sem = {k: Sem(st.enter_context(nc.semaphore("sem_" + k)))
           for k in ("dv", "ac", "pe")}
    dnames = ("blks", "blkp", "xt0", "xt1", "w10", "w11", "b10", "b11",
              "w20", "w21", "wc0", "wc1", "vwl", "qs0", "qs1", "kst", "vst",
              "spl", "vsp", "aosp", "ao0", "ao1", "xb0", "xb1", "wld0",
              "wld1", "wst0", "wst1")
    dsem = {k: Sem(st.enter_context(nc.semaphore("dma_" + k)))
            for k in dnames}
    dq = {"blkp": "gpsimd", "spl": "gpsimd", "vsp": "gpsimd",
          "aosp": "gpsimd", "wst0": "gpsimd", "wst1": "gpsimd",
          "vwl": "gpsimd", "wc0": "gpsimd", "wc1": "gpsimd",
          "b10": "gpsimd", "b11": "gpsimd"}

    def sbt(stack, name, shape, dtype):
        return stack.enter_context(nc.sbuf_tensor(name, shape, dtype))

    def pst(stack, name, shape, dtype):
        return stack.enter_context(nc.psum_tensor(name, shape, dtype))

    ident_b = sbt(st, "ident_b", [128, 128], BF16)
    perm_b = sbt(st, "perm_b", [128, 128], BF16)
    mask_b = sbt(st, "mask_b", [128, 128], F32)
    ones1 = sbt(st, "ones1", [1, 128], BF16)
    stat6 = sbt(st, "stat6", [128, 24], F32)
    stat2 = sbt(st, "stat2", [128, 2], F32)
    rsg = sbt(st, "rsg", [128, 1], F32)
    musig = sbt(st, "musig", [128, 2 * NTC], F32)
    zsum = sbt(st, "zsum", [128, 1], F32)
    rz = sbt(st, "rz", [128, 1], F32)
    negmax = sbt(st, "negmax", [128, 2], F32)
    epsc = sbt(st, "epsc", [128, 1], F32)
    nhalf = sbt(st, "nhalf", [128, 1], F32)

    io = dict(xin=xin, wq=wq, wk=wk, wv=wv, wo=wo, w1=w1, w2=w2,
              lnww=lnww, lnbb=lnbb, b1r=b1r, b2r=b2r, cqd=cqd, sqd=sqd,
              ckd=ckd, skd=skd, permd=permd, identd=identd, maskd=maskd,
              yout=yout, xscr=xscr, w1t_s=w1t_s, w2t_s=w2t_s, q_s=q_s,
              k_s=k_s, v_s=v_s, ao_s=ao_s)
    pers = dict(ident_b=ident_b, perm_b=perm_b, mask_b=mask_b, ones1=ones1,
                stat6=stat6, stat2=stat2, rsg=rsg, musig=musig, zsum=zsum,
                rz=rz, negmax=negmax, epsc=epsc, nhalf=nhalf)

    with st:
        p = Prog(nc)
        for dst, src in ((ident_b, identd), (perm_b, permd), (mask_b, maskd)):
            p.dma("sync", dst[:, :], src.ap(), dsem["blks"])
        p.wait("vector", dsem["blks"], dsem["blks"].n)
        p.op("vector", lambda E: E.memset(ones1[:, :], 1.0), writes=("ones1",))
        p.op("vector", lambda E: E.memset(epsc[:, :], EPS), writes=("epsc",))
        p.op("vector", lambda E: E.memset(nhalf[:, :], -0.5),
             writes=("nhalf",), inc=sem["dv"])
        p.wait("tensor", sem["dv"], sem["dv"].n)
        p.wait("scalar", sem["dv"], sem["dv"].n)
        with nc.Block() as blk:
            p.flush_into_block(blk)

        merge_wp = "wprep" in phases and "fa" in phases
        if "wprep" in phases and not merge_wp:
            _phase_wprep(nc, sem, dsem, sbt, pst, io, pers)
        if "fa" in phases:
            _phase_ffn(nc, sem, dsem, sbt, pst, io, pers, xin, xscr, "fa",
                       wprep_merge=merge_wp)
        else:
            p = Prog(nc)
            for tc in range(NTC):
                p.dma("sync", xscr.ap()[tc * 128:(tc + 1) * 128, :],
                      xin.ap()[tc * 128:(tc + 1) * 128, :], dsem["blks"])
            p.wait("sync", dsem["blks"], dsem["blks"].n)
            with nc.Block() as blk:
                p.flush_into_block(blk)
        if "attn" in phases:
            _phase_attn(nc, sem, dsem, sbt, pst, io, pers)
        if "fb" in phases:
            _phase_ffn(nc, sem, dsem, sbt, pst, io, pers, xscr, yout, "fb")
        else:
            p = Prog(nc)
            for tc in range(NTC):
                p.dma("sync", yout.ap()[tc * 128:(tc + 1) * 128, :],
                      xscr.ap()[tc * 128:(tc + 1) * 128, :], dsem["blks"])
            p.wait("sync", dsem["blks"], dsem["blks"].n)
            with nc.Block() as blk:
                p.flush_into_block(blk)

        p = Prog(nc)
        for k in dnames:
            p.wait(dq.get(k, "sync"), dsem[k], dsem[k].n)
        with nc.Block() as blk:
            p.flush_into_block(blk)

    nc.finalize()
    return nc



def _phase_wprep(nc, sem, dsem, sbt, pst, io, pers):
    st = contextlib.ExitStack()
    wld = [sbt(st, "wp_ld%d" % i, [128, HID], F32) for i in range(2)]
    qbufs = dict(am=sbt(st, "wp_am", [128, 256], F32),
                 rsc=sbt(st, "wp_rsc", [128, 256], I32),
                 ssc=sbt(st, "wp_ssc", [128, 256], I32),
                 y=sbt(st, "wp_y", [128, HID], F32),
                 r=sbt(st, "wp_r", [128, HID], F32))
    q8 = sbt(st, "wp_q8", [128, HID], I8)
    qnat = sbt(st, "wp_qn", [128, HID], BF16)
    wfm = [sbt(st, "wp_fm%d" % i, [128, HID], BF16) for i in range(2)]
    tp = [pst(st, "wp_tp%d" % i, [128, 1024], BF16) for i in range(2)]
    ident_b = pers["ident_b"]

    p = Prog(nc)
    q_done = {}
    stg = {}
    specs = []
    for rc in range(HID // 128):
        specs.append((io["w1"], io["w1t_s"], rc, DM, 4, HID))
    for rc in range(DM // 128):
        specs.append((io["w2"], io["w2t_s"], rc, HID, 8, DM))

    with st:
        for g, (wt, scr, rc, ncols, nsplit, scr_cols) in enumerate(specs):
            buf = wld[g % 2]
            fmb = wfm[g % 2]
            sld = dsem["wld%d" % (g % 2)]
            sst = dsem["wst%d" % (g % 2)]
            if g >= 2:
                p.wait("sync", sem["dv"], q_done[g - 2])
            c_ld = p.dma("sync", buf[:, 0:ncols],
                         wt.ap()[rc * 128:(rc + 1) * 128, :], sld)
            p.wait("vector", sld, c_ld)
            nblk = ncols // 32
            q_done[g] = emit_mx_quant_core(p, sem, buf, slice(0, ncols), nblk,
                                           qbufs)
            emit_mx_q8(p, sem, qbufs, slice(0, ncols), q8, slice(0, ncols))
            c_qn = emit_mx_expand(p, sem, q8, slice(0, ncols), qbufs["ssc"],
                                  slice(0, nblk), nblk, qnat, slice(0, ncols),
                                  inc=sem["dv"])
            p.wait("tensor", sem["dv"], c_qn)
            if g >= 2:
                p.wait("scalar", sst, stg[g - 2])
            ntp = ncols // 128
            evs = {}
            for g4 in range(ntp // 4):
                t = tp[g4 % 2]
                if g4 >= 2:
                    p.wait("tensor", sem["ac"], evs[g4 - 2])
                c_pe = 0
                for u in range(4):
                    j = g4 * 4 + u
                    c_pe = p.op("tensor", lambda E, j=j, t=t, u=u:
                                E.transpose(t[:, u * 128:(u + 1) * 128],
                                            qnat[:, j * 128:(j + 1) * 128],
                                            ident_b[:, :]), inc=sem["pe"])
                p.wait("scalar", sem["pe"], c_pe)
                evs[g4] = p.op("scalar", lambda E, g4=g4, t=t, fmb=fmb:
                               E.copy(fmb[:, g4 * 512:(g4 + 1) * 512],
                                      t[:, 0:512]),
                               reads=(nm(t),), writes=(nm(fmb),),
                               inc=sem["ac"])
            p.wait("vector", sem["pe"], sem["pe"].n)
            p.wait("gpsimd", sem["ac"], sem["ac"].n)
            ngrp = ntp // nsplit
            c_st = 0
            for sp_i in range(nsplit):
                out_ap = bass.AP(
                    scr, sp_i * ngrp * 128 * scr_cols + rc * 128,
                    [[scr_cols, 128], [128 * scr_cols, ngrp], [1, 128]])
                in_ap = fmb.ap().rearrange("p (c o) -> p c o", o=128)[
                    :, sp_i * ngrp:(sp_i + 1) * ngrp, :]
                c_st = p.dma("gpsimd", out_ap, in_ap, sst)
            stg[g] = c_st
        p.wait("sync", dsem["wst0"], dsem["wst0"].n)
        p.wait("sync", dsem["wst1"], dsem["wst1"].n)
        with nc.Block() as blk:
            p.flush_into_block(blk)



def _phase_ffn(nc, sem, dsem, sbt, pst, io, pers, src, dst, tag,
               wprep_merge=False):
    st = contextlib.ExitStack()
    xt = [sbt(st, tag + "_x%d" % i, [128, DM], F32) for i in range(2)]
    lnw_t = sbt(st, tag + "_lnw", [128, DM], F32)
    lnb_t = sbt(st, tag + "_lnb", [128, DM], F32)
    qa_fm = sbt(st, tag + "_qa", [128, KC * TBLK], BF16)
    q8g = [sbt(st, tag + "_q8_%d" % i, [128, HID], I8) for i in range(SWTC)]
    sscg = [sbt(st, tag + "_sg%d" % i, [128, HID // 32], I32)
            for i in range(SWTC)]
    xbb = sbt(st, tag + "_xbb", [128, DM], BF16)
    ntile = sbt(st, tag + "_n", [128, DM], BF16)
    qbufs = dict(am=sbt(st, tag + "_am", [128, 64], F32),
                 rsc=sbt(st, tag + "_rsc", [128, 64], I32),
                 ssc=sbt(st, tag + "_ssc", [128, 64], I32),
                 y=sbt(st, tag + "_y", [128, DM], BF16),
                 r=sbt(st, tag + "_r", [128, DM], F32))
    t1 = qbufs["r"]
    q8a = sbt(st, tag + "_q8a", [128, DM], I8)
    w1sl = [sbt(st, tag + "_w1s%d" % i, [128, KC * OSL], BF16)
            for i in range(2)]
    b1sl = [sbt(st, tag + "_b1s%d" % i, [1, OSL], BF16) for i in range(2)]
    b2sl = sbt(st, tag + "_b2s", [1, DM], BF16)
    gcat = [sbt(st, tag + "_g%d" % i, [128, SWTC * OSL], BF16)
            for i in range(2)]
    h1b = [sbt(st, tag + "_h1b%d" % i, [128, OSL], BF16) for i in range(4)]
    gq = qbufs
    w2sl = [sbt(st, tag + "_w2s%d" % i, [128, DM], BF16) for i in range(2)]
    qgtm = [sbt(st, tag + "_qgtm%d" % i, [128, 256], BF16) for i in range(4)]
    h2b = sbt(st, tag + "_h2b", [128, DM], BF16)
    fb = xbb

    psA = pst(st, tag + "_psA", [128, DM], F32)
    psB = pst(st, tag + "_psB", [128, DM], F32)
    ps_h1 = [psB.ap()[:, 0:512], psB.ap()[:, 512:1024]]
    ps_tp = [psB.ap()[:, 1024:1536].bitcast(BF16),
             psB.ap()[:, 1536:2048].bitcast(BF16)]
    ps_h2 = psA

    ident_b = pers["ident_b"]
    ones1 = pers["ones1"]
    stat6, stat2, rsg, musig = (pers["stat6"], pers["stat2"], pers["rsg"],
                                pers["musig"])
    epsc, nhalf = pers["epsc"], pers["nhalf"]
    p = Prog(nc)

    wp_on = wprep_merge and tag == "fa"
    wp_stg = {}
    wp_tp_hist = []
    if wp_on:
        wld = [lnw_t, lnb_t]
        wfm = [xbb, h2b]
        wp_specs = []
        for rc in range(HID // 128):
            wp_specs.append((io["w1"].ap()[rc * 128:(rc + 1) * 128, :],
                             io["w1t_s"], HID, 0, rc))
        for k in range(64):
            cc, rr = k // 16, k % 16
            wp_specs.append((io["w2"].ap()[rr * 128:(rr + 1) * 128,
                                           cc * DM:(cc + 1) * DM],
                             io["w2t_s"], DM, 16 * cc, rr))
        wp_order = []
        for j in range(32):
            wp_order += [2 * j, 2 * j + 1, 64 + 2 * j, 64 + 2 * j + 1]
        wp_qdone = {}
        wp_nt_pe = {0: 0, 1: 0}

        def emit_wp_spec(g):
            src_ap, scr, scr_cols, blkbase, rc = wp_specs[wp_order[g]]
            buf = wld[g % 2]
            fmb = wfm[g % 2]
            sld = dsem["wld%d" % (g % 2)]
            sst = dsem["wst%d" % (g % 2)]
            if g >= 2:
                p.wait("sync", sem["dv"], wp_qdone[g - 2])
            else:
                p.wait("sync", sem["dv"], sem["dv"].n)
            c_ld = p.dma("sync", buf[:, :], src_ap, sld)
            p.wait("vector", sld, c_ld)
            wpbufs = dict(am=qbufs["am"], rsc=qbufs["rsc"],
                          ssc=qbufs["ssc"], y=buf, r=qbufs["r"])
            wp_qdone[g] = emit_mx_quant_core(p, sem, buf, slice(0, DM), 64,
                                             wpbufs)
            emit_mx_q8(p, sem, wpbufs, slice(0, DM), q8a, slice(0, DM))
            buf2 = [ntile, qbufs["y"]][g % 2]
            p.wait("vector", sem["pe"], wp_nt_pe[g % 2])
            c_qn = emit_mx_expand(p, sem, q8a, slice(0, DM), qbufs["ssc"],
                                  slice(0, 64), 64, buf2, slice(0, DM),
                                  inc=sem["dv"])
            p.wait("tensor", sem["dv"], c_qn)
            if g >= 2:
                p.wait("scalar", sst, wp_stg[g - 2])
            c_ev = 0
            c_pe = 0
            for g4 in range(4):
                t = ps_tp[g4 % 2]
                if len(wp_tp_hist) >= 2:
                    p.wait("tensor", sem["ac"], wp_tp_hist[-2])
                for u in range(4):
                    jj = g4 * 4 + u
                    c_pe = p.op("tensor", lambda E, jj=jj, t=t, u=u,
                                buf2=buf2:
                                E.transpose(t[:, u * 128:(u + 1) * 128],
                                            buf2[:, jj * 128:(jj + 1) * 128],
                                            ident_b[:, :]), inc=sem["pe"])
                p.wait("scalar", sem["pe"], c_pe)
                c_ev = p.op("scalar", lambda E, g4=g4, t=t, fmb=fmb:
                            E.copy(fmb[:, g4 * 512:(g4 + 1) * 512],
                                   t[:, 0:512]),
                            reads=(nm(t),), writes=(nm(fmb),),
                            inc=sem["ac"])
                wp_tp_hist.append(c_ev)
            wp_nt_pe[g % 2] = c_pe
            p.wait("gpsimd", sem["ac"], c_ev)
            c_st = 0
            for sp_i in range(4):
                out_ap = bass.AP(
                    scr, (blkbase + sp_i * 4) * 128 * scr_cols + rc * 128,
                    [[scr_cols, 128], [128 * scr_cols, 4], [1, 128]])
                in_ap = fmb.ap().rearrange("p (c o) -> p c o", o=128)[
                    :, sp_i * 4:(sp_i + 1) * 4, :]
                c_st = p.dma("gpsimd", out_ap, in_ap, sst)
            wp_stg[g] = c_st

    with st:
        c_lnw = p.dma("sync", lnw_t[:, :], io["lnww"].ap(), dsem["blks"])
        c_lnb = p.dma("sync", lnb_t[:, :], io["lnbb"].ap(), dsem["blks"])
        c_b2 = p.dma("gpsimd", b2sl[:, :], io["b2r"].ap(), dsem["blkp"])
        p.wait("vector", dsem["blks"], c_lnb)
        p.wait("tensor", dsem["blkp"], c_b2)

        xt_gate = [[], []]
        rsg_gate = 0
        f1_mm_last = 0
        f2_pe_last = 0
        f2_done_prev = 0
        tp_used = {}
        w2_gate = {}

        for sw in range(NSW):
            tcs = [sw * SWTC + i for i in range(SWTC)]
            for i, tc in enumerate(tcs):
                xb = xt[i % 2]
                sx = dsem["xt%d" % (i % 2)]
                if wp_on and sw == 1 and i == 0:
                    p.wait("sync", sem["dv"], wp_qdone[126])
                    p.wait("sync", sem["dv"], wp_qdone[127])
                    p.dma("sync", lnw_t[:, :], io["lnww"].ap(),
                          dsem["blks"])
                    c_rl = p.dma("sync", lnb_t[:, :], io["lnbb"].ap(),
                                 dsem["blks"])
                    p.wait("vector", dsem["blks"], c_rl)
                for s_, v_ in xt_gate[i % 2]:
                    p.wait("sync", s_, v_)
                c_x = p.dma("sync", xb[:, :],
                            src.ap()[tc * 128:(tc + 1) * 128, :], sx)
                xnm = nm(xb)
                p.wait("vector", sx, c_x)
                c_st = emit_ln_stats(p, sem, xb, stat6, stat2)
                c_rs = emit_rsig(p, sem, stat2, rsg, epsc, nhalf,
                                 gate_dv=max(c_st, rsg_gate))
                p.wait("vector", sem["ac"], c_rs)
                p.op("vector", lambda E, tc=tc: E.tensor_copy(
                    musig[:, 2 * tc:2 * tc + 1], stat2[:, 0:1]),
                    reads=("stat2",), writes=("musig",))
                p.op("vector", lambda E, tc=tc: E.tensor_copy(
                    musig[:, 2 * tc + 1:2 * tc + 2], rsg[:, :]),
                    reads=("rsg",), writes=("musig",))
                c_xbb = p.op("vector", lambda E, xb=xb, tc=tc: E.tensor_scalar(
                    xbb[:, :], xb[:, :], musig[:, 2 * tc:2 * tc + 1],
                    musig[:, 2 * tc + 1:2 * tc + 2], ALU.subtract, ALU.mult),
                    reads=(xnm, "musig"), writes=(nm(xbb),), inc=sem["dv"])
                xt_gate[i % 2] = [(sem["dv"], c_xbb)]
                c_st2 = emit_ln_stats(p, sem, xbb, stat6, stat2)
                c_rs2 = emit_rsig(p, sem, stat2, rsg, epsc, nhalf,
                                  gate_dv=c_st2)
                p.wait("vector", sem["ac"], c_rs2)
                rsg_gate = p.op("vector", lambda E: E.tensor_scalar(
                    t1[:, :], xbb[:, :], stat2[:, 0:1], rsg[:, 0:1],
                    ALU.subtract, ALU.mult),
                    reads=(nm(xbb), "stat2", "rsg"), writes=(nm(t1),),
                    inc=sem["dv"])
                p.op("vector", lambda E: E.tensor_tensor(
                    t1[:, :], t1[:, :], lnw_t[:, :], ALU.mult),
                    reads=(nm(t1), nm(lnw_t)), writes=(nm(t1),))
                p.op("vector", lambda E: E.tensor_tensor(
                    ntile[:, :], t1[:, :], lnb_t[:, :], ALU.add),
                    reads=(nm(t1), nm(lnb_t)), writes=(nm(ntile),))
                emit_mx_quant_core(p, sem, ntile, slice(0, DM), 64, qbufs)
                emit_mx_q8(p, sem, qbufs, slice(0, DM), q8a, slice(0, DM))
                c_qa = emit_mx_expand(p, sem, q8a, slice(0, DM),
                                      qbufs["ssc"], slice(0, 64), 64,
                                      ntile, slice(0, DM), inc=sem["dv"])
                p.wait("tensor", sem["dv"], c_qa)
                if i == 0 and f2_pe_last:
                    p.wait("scalar", sem["pe"], f2_pe_last)
                    p.wait("tensor", sem["ac"], f2_done_prev)
                if wp_on and sw == 1 and i == 0:
                    p.wait("vector", sem["pe"], wp_nt_pe[0])
                    p.wait("vector", sem["pe"], wp_nt_pe[1])
                evs = {}
                for g4 in range(KC // 4):
                    t = ps_tp[g4 % 2]
                    if g4 >= 2:
                        p.wait("tensor", sem["ac"], evs[g4 - 2])
                    c_pe = 0
                    for u in range(4):
                        kc = g4 * 4 + u
                        c_pe = p.op("tensor", lambda E, kc=kc, t=t, u=u:
                                    E.transpose(
                                        t[:, u * 128:(u + 1) * 128],
                                        ntile[:, kc * 128:(kc + 1) * 128],
                                        ident_b[:, :]), inc=sem["pe"])
                    p.wait("scalar", sem["pe"], c_pe)
                    out3 = bass.AP(
                        qa_fm, g4 * 4 * TBLK + i * 128,
                        [[KC * TBLK, 128], [TBLK, 4], [1, 128]])
                    in3 = t[:, 0:512].rearrange("p (f c) -> p f c", c=128)
                    evs[g4] = p.op("scalar", lambda E, out3=out3, in3=in3:
                                   E.copy(out3, in3),
                                   reads=(nm(t),), writes=(nm(qa_fm),),
                                   inc=sem["ac"])
                    if wp_on and sw == 0:
                        wp_tp_hist.append(evs[g4])
                p.wait("vector", sem["pe"], sem["pe"].n)
            c_qa_all = sem["ac"].n
            if wp_on and sw == 0:
                wp_nt_pe[0] = sem["pe"].n

            p.wait("tensor", sem["ac"], c_qa_all)
            mm_last = {}
            gq_done = {}
            h1_free = {}
            for osl in range(NOSL):
                wb = w1sl[osl % 2]
                bb1 = b1sl[osl % 2]
                sw1 = dsem["w1%d" % (osl % 2)]
                sb1 = dsem["b1%d" % (osl % 2)]
                if wp_on and sw == 0 and osl == 0:
                    for _k in range(4):
                        emit_wp_spec(_k)
                if osl >= 2:
                    p.wait("sync", sem["pe"], mm_last[osl - 2])
                    p.wait("gpsimd", sem["pe"], mm_last[osl - 2])
                if wp_on and sw == 0:
                    p.wait("sync", dsem["wst0"], wp_stg[4 * osl])
                    p.wait("sync", dsem["wst1"], wp_stg[4 * osl + 1])
                in_ap = bass.AP(io["w1t_s"], osl * OSL,
                                [[HID, 128], [128 * HID, KC], [1, OSL]])
                out_ap = wb.ap().rearrange("p (kc c) -> p kc c", c=OSL)
                c_w = 0
                for hh in range(2):
                    c_w = p.dma("sync", out_ap[:, hh * 8:(hh + 1) * 8, :],
                                in_ap[:, hh * 8:(hh + 1) * 8, :], sw1)
                c_b1 = p.dma("gpsimd", bb1[:, :],
                             io["b1r"].ap()[:, osl * OSL:(osl + 1) * OSL],
                             sb1)
                p.wait("tensor", sw1, c_w)
                p.wait("tensor", sb1, c_b1)
                gc = gcat[osl % 2]
                if osl >= 2:
                    p.wait("scalar", sem["dv"], gq_done[osl - 2])
                for i in range(SWTC):
                    idx = osl * SWTC + i
                    ph = ps_h1[idx % 2]
                    gate = h1_free.get(idx - 2)
                    if gate:
                        p.wait("tensor", sem["ac"], gate)
                    for kc in range(KC):
                        p.op("tensor", lambda E, kc=kc, i=i, ph=ph, wb=wb:
                             E.matmul(ph[:, 0:OSL],
                                      qa_fm[:, kc * TBLK + i * 128:
                                            kc * TBLK + (i + 1) * 128],
                                      wb[:, kc * OSL:(kc + 1) * OSL],
                                      start=(kc == 0), stop=False))
                    c_mm = p.op("tensor", lambda E, ph=ph, bb1=bb1: E.matmul(
                        ph[:, 0:OSL], ones1[:, :], bb1[:, :],
                        start=False, stop=True), inc=sem["pe"])
                    mm_last[osl] = c_mm
                    hb = h1b[idx % 4]
                    p.wait("scalar", sem["pe"], c_mm)
                    p.op("scalar", lambda E, hb=hb, ph=ph: E.copy(
                        hb[:, :], ph[:, 0:OSL]), reads=(nm(ph),),
                        writes=(nm(hb),))
                    c_gl = p.op("scalar", lambda E, hb=hb, gc=gc, i=i:
                                E.activation(gc[:, i * OSL:(i + 1) * OSL],
                                             hb[:, :], ACTF.Gelu_apprx_tanh,
                                             bias=0.0, scale=1.0),
                                reads=(nm(hb),), writes=(nm(gc),),
                                inc=sem["ac"])
                    h1_free[idx] = c_gl
                    if wp_on and sw == 0 and osl < NOSL - 1:
                        if i == 3:
                            emit_wp_spec(4 * (osl + 1))
                            emit_wp_spec(4 * (osl + 1) + 1)
                        elif i == SWTC - 1:
                            emit_wp_spec(4 * (osl + 1) + 2)
                            emit_wp_spec(4 * (osl + 1) + 3)
                p.wait("vector", sem["ac"], sem["ac"].n)
                emit_mx_quant_core(p, sem, gc, slice(0, SWTC * OSL), 64, gq)
                nbo = OSL // 32
                for i in range(SWTC):
                    emit_mx_q8(p, sem, gq, slice(i * OSL, (i + 1) * OSL),
                               q8g[i], slice(osl * OSL, (osl + 1) * OSL))
                    p.op("vector", lambda E, i=i, osl=osl: E.tensor_copy(
                        sscg[i][:, osl * nbo:(osl + 1) * nbo],
                        gq["ssc"][:, i * nbo:(i + 1) * nbo]),
                        reads=(nm(gq["ssc"]),), writes=(nm(sscg[i]),))
                gq_done[osl] = p.op("vector", lambda E: E.drain(),
                                    inc=sem["dv"])
            f1_mm_last = mm_last[NOSL - 1]

            h2_free = f2_done_prev
            for pr in range(SWTC // 2):
                i0, i1 = 2 * pr, 2 * pr + 1
                tc0, tc1 = tcs[i0], tcs[i1]
                if pr == 0:
                    p.wait("scalar", sem["pe"], f1_mm_last)
                else:
                    p.wait("scalar", sem["pe"], f2_pe_last)
                stg_ev = {}
                for g in range(32):
                    t = ps_tp[g % 2]
                    if g >= 2:
                        p.wait("tensor", sem["ac"], stg_ev[g - 2])
                    elif wp_on and sw == 0 and pr == 0 and wp_tp_hist:
                        p.wait("tensor", sem["ac"], wp_tp_hist[-2 + g])
                    exd = {}
                    for ch in (0, 1):
                        bi = 2 * (g % 2) + ch
                        qt = qgtm[bi]
                        gate = tp_used.get(bi)
                        if gate is not None:
                            p.wait("vector", sem["pe"], gate)
                        exd[ch] = emit_mx_expand(
                            p, sem, q8g[2 * pr + ch],
                            slice(2 * g * 128, (2 * g + 2) * 128),
                            sscg[2 * pr + ch],
                            slice(2 * g * 4, (2 * g + 2) * 4),
                            8, qt, slice(0, 256), inc=sem["dv"])
                    p.wait("tensor", sem["dv"], max(exd.values()))
                    c_tp = 0
                    for u in range(4):
                        ch = u % 2
                        bi = 2 * (g % 2) + ch
                        qt = qgtm[bi]
                        half = u // 2
                        c_tp = p.op("tensor", lambda E, t=t, qt=qt, u=u,
                                    half=half: E.transpose(
                                        t[:, u * 128:(u + 1) * 128],
                                        qt[:, half * 128:(half + 1) * 128],
                                        ident_b[:, :]),
                                    inc=sem["pe"])
                        tp_used[bi] = c_tp
                    p.wait("scalar", sem["pe"], c_tp)
                    stg_ev[g] = p.op(
                        "scalar", lambda E, g=g, t=t: E.copy(
                            qa_fm[:, g * 512:(g + 1) * 512], t[:, 0:512]),
                        reads=(nm(t),), writes=(nm(qa_fm),), inc=sem["ac"])
                c_stage = sem["ac"].n
                p.wait("tensor", sem["ac"], c_stage)
                if h2_free:
                    p.wait("tensor", sem["ac"], h2_free)
                for oc in range(OC):
                    wb2 = w2sl[oc % 2]
                    sw2 = dsem["w2%d" % (oc % 2)]
                    gate = w2_gate.get(oc % 2)
                    if gate is not None:
                        p.wait("sync", sem["pe"], gate)
                    if wp_on and sw == 0:
                        cgrp = oc // 16
                        p.wait("sync", dsem["wst0"],
                               wp_stg[4 * (8 * cgrp + 7) + 2])
                        p.wait("sync", dsem["wst1"],
                               wp_stg[4 * (8 * cgrp + 7) + 3])
                    c_w2 = p.dma("sync", wb2[:, :],
                                 bass.AP(io["w2t_s"], oc * 128 * DM,
                                         [[DM, 128], [1, DM]]), sw2)
                    p.wait("tensor", sw2, c_w2)
                    for ch, psX in ((0, psA), (1, psB)):
                        for ds in range(4):
                            p.op("tensor", lambda E, ds=ds, oc=oc, ch=ch,
                                 psX=psX, wb2=wb2: E.matmul(
                                     psX[:, ds * 512:(ds + 1) * 512],
                                     qa_fm[:, oc * 256 + ch * 128:
                                           oc * 256 + (ch + 1) * 128],
                                     wb2[:, ds * 512:(ds + 1) * 512],
                                     start=(oc == 0), stop=False))
                    c_h2m = p.op("tensor", lambda E: E.drain(), inc=sem["pe"])
                    w2_gate[oc % 2] = c_h2m
                for ch, psX in ((0, psA), (1, psB)):
                    for hh in range(4):
                        c_b = p.op("tensor", lambda E, hh=hh, psX=psX:
                                   E.matmul(psX[:, hh * 512:(hh + 1) * 512],
                                            ones1[:, :],
                                            b2sl[:, hh * 512:(hh + 1) * 512],
                                            start=False, stop=True),
                                   inc=sem["pe"])
                f2_pe_last = c_b
                h2_frees = []
                h2b_rd = 0
                if wp_on and sw == 0 and pr == 0:
                    p.wait("scalar", dsem["wst1"], wp_stg[4 * 31 + 3])
                    p.wait("vector", dsem["wst0"], wp_stg[4 * 31 + 2])
                for ch, psX, i, tc in ((0, psA, i0, tc0), (1, psB, i1, tc1)):
                    p.wait("scalar", sem["pe"], c_b)
                    if h2b_rd:
                        p.wait("scalar", sem["dv"], h2b_rd)
                    c_h2b = p.op("scalar", lambda E, psX=psX: E.copy(
                        h2b[:, :], psX[:, :]),
                        reads=(nm(psX),), writes=(nm(h2b),), inc=sem["ac"])
                    h2_frees.append(c_h2b)
                    hsrc = h2b
                    xb = xt[i % 2]
                    sx = dsem["xt%d" % (i % 2)]
                    for s_, v_ in xt_gate[i % 2]:
                        p.wait("sync", s_, v_)
                    c_x2 = p.dma("sync", xb[:, :],
                                 src.ap()[tc * 128:(tc + 1) * 128, :], sx)
                    p.wait("vector", sx, c_x2)
                    p.wait("vector", sem["ac"], c_h2b)
                    xnm = nm(xb)
                    p.op("vector", lambda E, tc=tc, xb=xb: E.tensor_scalar(
                        fb[:, :], xb[:, :], musig[:, 2 * tc:2 * tc + 1],
                        musig[:, 2 * tc + 1:2 * tc + 2], ALU.subtract,
                        ALU.mult),
                        reads=(xnm, "musig"), writes=(nm(fb),))
                    h2b_rd = p.op("vector", lambda E, hsrc=hsrc:
                                  E.tensor_tensor(
                                      fb[:, :], fb[:, :], hsrc[:, :],
                                      ALU.add),
                                  reads=(nm(fb), nm(hsrc)),
                                  writes=(nm(fb),), inc=sem["dv"])
                    c_xu = p.op("vector", lambda E, xb=xb:
                                E.scalar_tensor_tensor(
                                    xb[:, :], fb[:, :], 0.5, xb[:, :],
                                    ALU.mult, ALU.add),
                                reads=(nm(fb), xnm), writes=(xnm,),
                                inc=sem["dv"])
                    p.wait("sync", sem["dv"], c_xu)
                    c_ws = p.dma("sync",
                                 dst.ap()[tc * 128:(tc + 1) * 128, :],
                                 xb[:, :], sx)
                    xt_gate[i % 2] = [(sx, c_ws)]
                f2_done_prev = max(h2_frees)
                h2_free = f2_done_prev

        for k in ("xt0", "xt1"):
            p.wait("sync", dsem[k], dsem[k].n)
        with nc.Block() as blk:
            p.flush_into_block(blk)



def _phase_attn(nc, sem, dsem, sbt, pst, io, pers):
    ident_b = pers["ident_b"]
    perm_b = pers["perm_b"]
    mask_b = pers["mask_b"]
    stat6, stat2, rsg = pers["stat6"], pers["stat2"], pers["rsg"]
    zsum, rz, negmax = pers["zsum"], pers["rz"], pers["negmax"]
    epsc, nhalf = pers["epsc"], pers["nhalf"]
    xscr = io["xscr"]
    p = Prog(nc)

    stA = contextlib.ExitStack()
    a_fm = sbt(stA, "a_fm", [128, KC * S], BF16)
    xbuf = [sbt(stA, "at_x%d" % i, [128, DM], F32) for i in range(2)]
    atile = sbt(stA, "at_t", [128, DM], BF16)
    wcol = [sbt(stA, "pj_w%d" % i, [128, KC * 128], BF16) for i in range(2)]
    vwb = sbt(stA, "pj_vw", [128, KC * 512], BF16)
    pout = sbt(stA, "pj_o", [128, S], BF16)
    vout = sbt(stA, "pj_v", [128, 512], BF16)
    ps_tp = [pst(stA, "at_tp%d" % i, [128, 1024], BF16) for i in range(2)]
    ps_p = pst(stA, "pj_ps", [128, 512], F32)
    ps_v = pst(stA, "pj_pv", [128, 512], F32)

    with stA:
        c_at_prev = 0
        xb_gate = [0, 0]
        for tc in range(NTC):
            xb = xbuf[tc % 2]
            sx = dsem["xt%d" % (tc % 2)]
            if xb_gate[tc % 2]:
                p.wait("sync", sem["dv"], xb_gate[tc % 2])
            c_x = p.dma("sync", xb[:, :],
                        xscr.ap()[tc * 128:(tc + 1) * 128, :], sx)
            xnm = nm(xb)
            p.wait("vector", sx, c_x)
            c_st = emit_ln_stats(p, sem, xb, stat6, stat2)
            c_rs = emit_rsig(p, sem, stat2, rsg, epsc, nhalf,
                             gate_dv=max(c_st, c_at_prev))
            p.wait("vector", sem["ac"], c_rs)
            c_at = p.op("vector", lambda E, xb=xb: E.tensor_scalar(
                atile[:, :], xb[:, :], stat2[:, 0:1], rsg[:, 0:1],
                ALU.subtract, ALU.mult), reads=(xnm, "stat2", "rsg"),
                writes=(nm(atile),), inc=sem["dv"])
            c_at_prev = c_at
            xb_gate[tc % 2] = c_at
            p.wait("tensor", sem["dv"], c_at)
            evs = {}
            for g4 in range(KC // 4):
                t = ps_tp[g4 % 2]
                if g4 >= 2:
                    p.wait("tensor", sem["ac"], evs[g4 - 2])
                c_pe = 0
                for u in range(4):
                    kc = g4 * 4 + u
                    c_pe = p.op("tensor", lambda E, kc=kc, t=t, u=u:
                                E.transpose(t[:, u * 128:(u + 1) * 128],
                                            atile[:, kc * 128:(kc + 1) * 128],
                                            ident_b[:, :]), inc=sem["pe"])
                p.wait("scalar", sem["pe"], c_pe)
                out3 = bass.AP(a_fm, g4 * 4 * S + tc * 128,
                               [[KC * S, 128], [S, 4], [1, 128]])
                in3 = t.ap()[:, 0:512].rearrange("p (f c) -> p f c", c=128)
                evs[g4] = p.op("scalar", lambda E, out3=out3, in3=in3:
                               E.copy(out3, in3),
                               reads=(nm(t),), writes=(nm(a_fm),),
                               inc=sem["ac"])
            p.wait("vector", sem["pe"], sem["pe"].n)
        c_afm = sem["ac"].n
        p.wait("tensor", sem["ac"], c_afm)

        proj_fin = {}
        spl_prev = 0
        for sidx in range(20):
            wb = wcol[sidx % 2]
            swc = dsem["wc%d" % (sidx % 2)]
            gate = proj_fin.get(sidx - 2)
            if gate:
                p.wait("gpsimd", sem["pe"], gate)
            if sidx < 16:
                wt, col0, spill, sp_i = io["wq"], sidx * 128, io["q_s"], sidx
                wcols = 2048
            else:
                wt, col0, spill, sp_i = (io["wk"], (sidx - 16) * 128,
                                         io["k_s"], sidx - 16)
                wcols = 512
            in_ap = bass.AP(wt, col0,
                            [[wcols, 128], [128 * wcols, KC], [1, 128]])
            out_ap = wb.ap().rearrange("p (kc c) -> p kc c", c=128)
            c_wl = p.dma("gpsimd", out_ap, in_ap, swc)
            p.wait("tensor", swc, c_wl)
            for ts in range(4):
                if spl_prev:
                    p.wait("scalar", dsem["spl"], spl_prev)
                for kc in range(KC):
                    p.op("tensor", lambda E, kc=kc, ts=ts, wb=wb: E.matmul(
                        ps_p[:, :], wb[:, kc * 128:(kc + 1) * 128],
                        a_fm[:, kc * S + ts * 512:kc * S + (ts + 1) * 512],
                        start=(kc == 0), stop=(kc == KC - 1)))
                c_mm = p.op("tensor", lambda E: E.drain(), inc=sem["pe"])
                p.wait("scalar", sem["pe"], c_mm)
                c_ev = p.op("scalar", lambda E, ts=ts: E.copy(
                    pout[:, ts * 512:(ts + 1) * 512], ps_p[:, :]),
                    reads=(nm(ps_p),), writes=(nm(pout),), inc=sem["ac"])
                p.wait("tensor", sem["ac"], c_ev)
            proj_fin[sidx] = sem["pe"].n
            p.wait("gpsimd", sem["ac"], sem["ac"].n)
            spl_prev = p.dma("gpsimd",
                             bass.AP(spill, sp_i * 128 * S,
                                     [[S, 128], [1, S]]),
                             pout[:, :], dsem["spl"])
        in_ap = bass.AP(io["wv"], 0, [[512, 128], [128 * 512, KC], [1, 512]])
        c_vw = p.dma("gpsimd",
                     vwb.ap().rearrange("p (kc c) -> p kc c", c=512),
                     in_ap, dsem["vwl"])
        p.wait("tensor", dsem["vwl"], c_vw)
        vsp_prev = 0
        for tc in range(NTC):
            for kc in range(KC):
                p.op("tensor", lambda E, kc=kc, tc=tc: E.matmul(
                    ps_v[:, :],
                    a_fm[:, kc * S + tc * 128:kc * S + (tc + 1) * 128],
                    vwb[:, kc * 512:(kc + 1) * 512],
                    start=(kc == 0), stop=(kc == KC - 1)))
            c_mm = p.op("tensor", lambda E: E.drain(), inc=sem["pe"])
            p.wait("scalar", sem["pe"], c_mm)
            if vsp_prev:
                p.wait("scalar", dsem["vsp"], vsp_prev)
            c_ev = p.op("scalar", lambda E: E.copy(vout[:, :], ps_v[:, :]),
                        reads=(nm(ps_v),), writes=(nm(vout),), inc=sem["ac"])
            p.wait("tensor", sem["ac"], c_ev)
            p.wait("gpsimd", sem["ac"], c_ev)
            out_ap = bass.AP(io["v_s"], tc * 128 * 128,
                             [[128, 128], [NTC * 128 * 128, 4], [1, 128]])
            vsp_prev = p.dma("gpsimd", out_ap,
                             vout.ap().rearrange("p (g c) -> p g c", c=128),
                             dsem["vsp"])
        p.wait("sync", dsem["vsp"], dsem["vsp"].n)
        p.wait("sync", dsem["spl"], dsem["spl"].n)

    stC = contextlib.ExitStack()
    tabs = {name: sbt(stC, "tab_" + name, [128, S], F32)
            for name in ("cq", "sq", "ck", "sk")}
    kst = sbt(stC, "sc_k", [128, S], BF16)
    krot = sbt(stC, "sc_kr", [128, S], BF16)
    vst = sbt(stC, "sc_v", [128, NTC * 128], BF16)
    qst = [sbt(stC, "sc_q%d" % i, [128, S], BF16) for i in range(2)]
    qrot = sbt(stC, "sc_qr", [128, S], BF16)
    t1r = sbt(stC, "sc_t1", [128, 512], F32)
    t2r = sbt(stC, "sc_t2", [128, 512], F32)
    esb = [sbt(stC, "sc_e%d" % i, [128, S], BF16) for i in range(2)]
    etb = [sbt(stC, "sc_et%d" % i, [128, 512], BF16) for i in range(2)]
    ao = sbt(stC, "sc_ao", [128, S], BF16)
    ps_sc = pst(stC, "sc_ps", [128, S], F32)
    ps_rp = pst(stC, "sc_rp", [128, 512], F32)
    ps_tp2 = [pst(stC, "sc_tp%d" % i, [128, 1024], BF16) for i in range(2)]
    ps_pv = pst(stC, "sc_pv", [128, 512], F32)

    def rope(src_t, dst_t, ctab, stab):
        for q4 in range(4):
            sl = slice(q4 * 512, (q4 + 1) * 512)
            c_pm = p.op("tensor", lambda E, sl=sl, src_t=src_t: E.matmul(
                ps_rp[:, :], perm_b[:, :], src_t[:, sl],
                start=True, stop=True), inc=sem["pe"])
            p.op("vector", lambda E, sl=sl, src_t=src_t, ctab=ctab:
                 E.tensor_tensor(t1r[:, :], src_t[:, sl], ctab[:, sl],
                                 ALU.mult),
                 reads=(nm(src_t), nm(ctab)), writes=(nm(t1r),))
            p.wait("vector", sem["pe"], c_pm)
            p.op("vector", lambda E, sl=sl, stab=stab:
                 E.scalar_tensor_tensor(
                     t2r[:, :], ps_rp[:, :], 0.0, stab[:, sl],
                     ALU.add, ALU.mult),
                 reads=(nm(ps_rp), nm(stab)), writes=(nm(t2r),))
            c_rt = p.op("vector", lambda E, sl=sl, dst_t=dst_t:
                        E.tensor_tensor(dst_t[:, sl], t1r[:, :], t2r[:, :],
                                        ALU.add),
                        reads=(nm(t1r), nm(t2r)), writes=(nm(dst_t),),
                        inc=sem["dv"])
            p.wait("tensor", sem["dv"], c_rt)
        return sem["dv"].n

    with stC:
        for name, srcd in (("cq", io["cqd"]), ("sq", io["sqd"]),
                           ("ck", io["ckd"]), ("sk", io["skd"])):
            p.dma("sync", tabs[name][:, :], srcd.ap(), dsem["blks"])
        p.wait("vector", dsem["blks"], dsem["blks"].n)
        exp_last = {0: 0, 1: 0}
        c_nr_prev = 0
        c_pv_prev = 0
        pv_gate = [0, 0]
        gp_end = (0, 0)
        strip_end = {}
        strip_scores_end = 0
        aosp_prev = 0
        for gp in range(4):
            p.wait("sync", sem["pe"], gp_end[0])
            p.wait("sync", sem["dv"], gp_end[1])
            c_k = p.dma("sync", kst[:, :],
                        bass.AP(io["k_s"], gp * 128 * S, [[S, 128], [1, S]]),
                        dsem["kst"])
            in_ap = bass.AP(io["v_s"], gp * NTC * 128 * 128,
                            [[128, 128], [128 * 128, NTC], [1, 128]])
            c_v = p.dma("sync",
                        vst.ap().rearrange("p (t c) -> p t c", c=128),
                        in_ap, dsem["vst"])
            p.wait("vector", dsem["kst"], c_k)
            p.wait("tensor", dsem["kst"], c_k)
            p.wait("vector", sem["pe"], gp_end[0])
            rope(kst, krot, tabs["ck"], tabs["sk"])
            p.wait("tensor", dsem["vst"], c_v)
            for i in range(4):
                sidx = 4 * gp + i
                qb = qst[i % 2]
                sq = dsem["qs%d" % (i % 2)]
                g2 = strip_end.get(sidx - 2)
                if g2:
                    p.wait("sync", sem["pe"], g2[0])
                    p.wait("sync", sem["dv"], g2[1])
                c_q = p.dma("sync", qb[:, :],
                            bass.AP(io["q_s"], sidx * 128 * S,
                                    [[S, 128], [1, S]]), sq)
                p.wait("vector", sq, c_q)
                p.wait("tensor", sq, c_q)
                if sidx > 0:
                    p.wait("vector", sem["pe"], strip_scores_end)
                c_qrot = rope(qb, qrot, tabs["cq"], tabs["sq"])
                strip_end[sidx] = (sem["pe"].n, c_qrot)
                p.wait("tensor", sem["dv"], c_qrot)
                def emit_tpv(n, qtc, hh, eb_src, c_nr):
                    p.wait("tensor", sem["dv"], c_nr)
                    ng = (qtc + 1 + 3) // 4
                    pvst = {}

                    def emit_grp(g, qtc=qtc, eb_src=eb_src):
                        t = ps_tp2[g % 2]
                        nt = min(4, qtc + 1 - g * 4)
                        c_tp = 0
                        for u in range(nt):
                            j = g * 4 + u
                            c_tp = p.op("tensor", lambda E, j=j, t=t,
                                        u=u, eb_src=eb_src: E.transpose(
                                            t[:, u * 128:(u + 1) * 128],
                                            eb_src[:, j * 128:(j + 1) * 128],
                                            ident_b[:, :]),
                                        inc=sem["pe"])
                        p.wait("scalar", sem["pe"], c_tp)
                        eb = etb[g % 2]
                        c_ev = p.op("scalar", lambda E, eb=eb, t=t,
                                    nt=nt: E.copy(eb[:, 0:nt * 128],
                                                  t[:, 0:nt * 128]),
                                    reads=(nm(t),), writes=(nm(eb),),
                                    inc=sem["ac"])
                        pvst[g] = (c_ev, eb, nt)

                    emit_grp(0)
                    for g in range(ng):
                        if g + 1 < ng:
                            emit_grp(g + 1)
                        c_ev, eb, nt = pvst.pop(g)
                        p.wait("tensor", sem["ac"], c_ev)
                        for u in range(nt):
                            j = g * 4 + u
                            p.op("tensor", lambda E, j=j, eb=eb, u=u,
                                 qtc=qtc: E.matmul(
                                     ps_pv[:, 0:128],
                                     vst[:, j * 128:(j + 1) * 128],
                                     eb[:, u * 128:(u + 1) * 128],
                                     start=(j == 0), stop=(j == qtc)))
                    c_pv = p.op("tensor", lambda E: E.drain(),
                                inc=sem["pe"])
                    p.wait("scalar", sem["pe"], c_pv)
                    c_ao = p.op("scalar", lambda E, hh=hh, qtc=qtc:
                                E.copy(ao[hh * 64:(hh + 1) * 64,
                                          qtc * 128:(qtc + 1) * 128],
                                       ps_pv[hh * 64:(hh + 1) * 64,
                                             0:128]),
                                reads=(nm(ps_pv),), writes=(nm(ao),),
                                inc=sem["ac"])
                    p.wait("tensor", sem["ac"], c_ao)
                    return c_pv

                pend = None
                nit = 0
                for hh in range(2):
                    rows = slice(hh * 64, (hh + 1) * 64)
                    for qtc in range(NTC):
                        L = (qtc + 1) * 128
                        n = nit
                        nit += 1
                        eb_cur = esb[n % 2]
                        if qtc <= 7:
                            base = 1024 * (n % 2)
                            gates = (exp_last[n % 2],)
                        else:
                            base = 0
                            gates = (exp_last[0], exp_last[1])
                        for cg in gates:
                            if cg:
                                p.wait("tensor", sem["ac"], cg)
                        slabs = []
                        off = 0
                        while off < L:
                            w = min(512, L - off)
                            slabs.append((off, w))
                            off += w
                        ns = len(slabs)
                        for si, (off, w) in enumerate(slabs):
                            p.op("tensor", lambda E, off=off, w=w, rows=rows,
                                 qtc=qtc, base=base: E.matmul(
                                     ps_sc[:, base + off:base + off + w],
                                     qrot[rows, qtc * 128:(qtc + 1) * 128],
                                     krot[rows, off:off + w],
                                     start=True, stop=True))
                            c_sl = p.op("tensor", lambda E: E.drain(),
                                        inc=sem["pe"])
                            p.wait("vector", sem["pe"], c_sl)
                            if si == ns - 1:
                                p.op("vector", lambda E, qtc=qtc, base=base:
                                     E.tensor_tensor(
                                         ps_sc[:, base + qtc * 128:
                                               base + (qtc + 1) * 128],
                                         ps_sc[:, base + qtc * 128:
                                               base + (qtc + 1) * 128],
                                         mask_b[:, :], ALU.add),
                                     reads=(nm(ps_sc), "mask_b"),
                                     writes=(nm(ps_sc),))
                            p.op("vector", lambda E, si=si, off=off, w=w,
                                 base=base:
                                 E.tensor_reduce(
                                     stat6[:, si:si + 1],
                                     ps_sc[:, base + off:base + off + w],
                                     AX.X, ALU.max),
                                 reads=(nm(ps_sc),), writes=("stat6",))
                        c_mx = p.op("vector", lambda E, ns=ns, n=n:
                                    E.tensor_reduce(
                                        negmax[:, n % 2:n % 2 + 1],
                                        stat6[:, 0:ns], AX.X,
                                        ALU.max, negate=True),
                                    reads=("stat6",), writes=("negmax",),
                                    inc=sem["dv"])
                        if pend is not None:
                            c_pv_prev = emit_tpv(*pend)
                            pv_gate[(n - 1) % 2] = c_pv_prev
                        p.wait("scalar", sem["dv"], c_mx)
                        if c_nr_prev:
                            p.wait("scalar", sem["dv"], c_nr_prev)
                        if pv_gate[n % 2]:
                            p.wait("scalar", sem["pe"], pv_gate[n % 2])
                        c_ex = p.op("scalar", lambda E, L=L, n=n,
                                    eb_cur=eb_cur, base=base: E.activation(
                                        eb_cur[:, 0:L],
                                        ps_sc[:, base:base + L],
                                        ACTF.Exp,
                                        bias=negmax[:, n % 2:n % 2 + 1],
                                        scale=1.0, accum_out=zsum[:, :]),
                                    reads=(nm(ps_sc), "negmax"),
                                    writes=(nm(eb_cur), "zsum"),
                                    inc=sem["ac"])
                        if qtc <= 7:
                            exp_last[n % 2] = c_ex
                        else:
                            exp_last[0] = exp_last[1] = c_ex
                        p.wait("vector", sem["ac"], c_ex)
                        p.op("vector", lambda E: E.reciprocal(
                            rz[:, :], zsum[:, :]), reads=("zsum",),
                            writes=("rz",))
                        c_nr = p.op("vector", lambda E, L=L, eb_cur=eb_cur:
                                    E.tensor_scalar(
                                        eb_cur[:, 0:L], eb_cur[:, 0:L],
                                        rz[:, 0:1], None, ALU.mult),
                                    reads=(nm(eb_cur), "rz"),
                                    writes=(nm(eb_cur),), inc=sem["dv"])
                        c_nr_prev = c_nr
                        pend = (n, qtc, hh, eb_cur, c_nr)
                if pend is not None:
                    c_pv_prev = emit_tpv(*pend)
                    pv_gate[(nit - 1) % 2] = c_pv_prev
                    pend = None
                strip_scores_end = sem["pe"].n
                p.wait("gpsimd", sem["ac"], sem["ac"].n)
                if aosp_prev:
                    p.wait("scalar", dsem["aosp"], aosp_prev)
                aosp_prev = p.dma("gpsimd",
                                  bass.AP(io["ao_s"], sidx * 128 * S,
                                          [[S, 128], [1, S]]), ao[:, :],
                                  dsem["aosp"])
            gp_end = (sem["pe"].n, sem["dv"].n)
        p.wait("sync", dsem["aosp"], dsem["aosp"].n)

    stD = contextlib.ExitStack()
    wob = [sbt(stD, "op_w%d" % s, [128, DM], BF16) for s in range(16)]
    aoc = [sbt(stD, "op_a%d" % i, [128, 16 * 128], BF16) for i in range(2)]
    xb2 = [sbt(stD, "op_x%d" % i, [128, DM], F32) for i in range(2)]
    ps_o = pst(stD, "op_ps", [128, DM], F32)
    with stD:
        for s in range(16):
            p.dma("gpsimd", wob[s][:, :],
                  bass.AP(io["wo"], s * 128 * DM, [[DM, 128], [1, DM]]),
                  dsem["blkp"])
        c_wo = dsem["blkp"].n
        p.wait("tensor", dsem["blkp"], c_wo)
        res_done = {}
        for tc in range(NTC):
            ab = aoc[tc % 2]
            xb = xb2[tc % 2]
            sao = dsem["ao%d" % (tc % 2)]
            sxb = dsem["xb%d" % (tc % 2)]
            gate = res_done.get(tc - 2)
            if gate:
                p.wait("sync", sxb, gate)
            in_ap = bass.AP(io["ao_s"], tc * 128,
                            [[S, 128], [128 * S, 16], [1, 128]])
            c_ao = p.dma("sync",
                         ab.ap().rearrange("p (s c) -> p s c", c=128),
                         in_ap, sao)
            c_x = p.dma("sync", xb[:, :],
                        xscr.ap()[tc * 128:(tc + 1) * 128, :], sxb)
            p.wait("tensor", sao, c_ao)
            for s in range(16):
                for ds in range(4):
                    p.op("tensor", lambda E, s=s, ds=ds, ab=ab: E.matmul(
                        ps_o[:, ds * 512:(ds + 1) * 512],
                        ab[:, s * 128:(s + 1) * 128],
                        wob[s][:, ds * 512:(ds + 1) * 512],
                        start=(s == 0), stop=(s == 15)))
            c_om = p.op("tensor", lambda E: E.drain(), inc=sem["pe"])
            p.wait("vector", sem["pe"], c_om)
            p.wait("vector", sxb, c_x)
            xnm = nm(xb)
            c_xr = p.op("vector", lambda E, xb=xb: E.tensor_tensor(
                xb[:, :], xb[:, :], ps_o[:, :], ALU.add),
                reads=(xnm, nm(ps_o)), writes=(xnm,), inc=sem["dv"])
            p.wait("tensor", sem["dv"], c_xr)
            p.wait("sync", sem["dv"], c_xr)
            c_st = p.dma("sync", xscr.ap()[tc * 128:(tc + 1) * 128, :],
                         xb[:, :], sxb)
            res_done[tc] = c_st
        for k in ("xb0", "xb1"):
            p.wait("sync", dsem[k], dsem[k].n)
        with nc.Block() as blk:
            p.flush_into_block(blk)



def kernel(L0_attn_x, L0_attn_Wq, L0_attn_Wk, L0_attn_Wv, L0_attn_Wo,
           L0_ffn_ln_w, L0_ffn_ln_b, L0_ffn_W1, L0_ffn_b1, L0_ffn_W2,
           L0_ffn_b2):
    x = np.ascontiguousarray(np.asarray(L0_attn_x, dtype=np.float32))
    consts = _host_consts()
    perm = _head_perm()
    wq_p = np.ascontiguousarray(np.asarray(L0_attn_Wq, np.float32)[:, perm])
    wo_p = np.ascontiguousarray(np.asarray(L0_attn_Wo, np.float32)[perm, :])
    wk_ = np.ascontiguousarray(np.asarray(L0_attn_Wk, np.float32))
    wv_ = np.ascontiguousarray(np.asarray(L0_attn_Wv, np.float32))
    w1_ = np.ascontiguousarray(np.asarray(L0_ffn_W1, np.float32))
    w2_ = np.ascontiguousarray(np.asarray(L0_ffn_W2, np.float32))
    lnw = np.ascontiguousarray(np.broadcast_to(
        np.asarray(L0_ffn_ln_w, np.float32)[None, :], (128, DM)))
    lnb = np.ascontiguousarray(np.broadcast_to(
        np.asarray(L0_ffn_ln_b, np.float32)[None, :], (128, DM)))
    b1v = np.ascontiguousarray(np.asarray(L0_ffn_b1, np.float32)[None, :])
    b2v = np.ascontiguousarray(np.asarray(L0_ffn_b2, np.float32)[None, :])

    nc = build_nc()
    base = {"wq": wq_p, "wk": wk_, "wv": wv_, "wo": wo_p, "w1": w1_,
            "w2": w2_, "lnww": lnw, "lnbb": lnb, "b1r": b1v, "b2r": b2v,
            "cqd": consts["cq"], "sqd": consts["sq"], "ckd": consts["ck"],
            "skd": consts["sk"], "permd": consts["perm"],
            "identd": consts["ident"], "maskd": consts["mask"]}
    in_maps = [dict(base, xin=x[c]) for c in range(8)]
    res = run_bass_kernel_spmd(nc, in_maps, core_ids=list(range(8)))
    return np.stack([np.asarray(r["yout"], dtype=np.float32)
                     for r in res.results], axis=0)
